# version 07
# baseline (speedup 1.0000x reference)
import math
import numpy as np
import ml_dtypes
from contextlib import ExitStack
import concourse.bass as bass
import concourse.mybir as mybir
from concourse.bass_utils import run_bass_kernel_spmd

F32 = mybir.dt.float32
BF16 = mybir.dt.bfloat16
I32 = mybir.dt.int32
AF = mybir.ActivationFunctionType
ALU = mybir.AluOpType
AX = mybir.AxisListType.X
BFN = ml_dtypes.bfloat16

P = 128
T = 2048
TC = 16
D = 1024
DC = 8
FP = 1152
FC = 9
NE = 8
CAP = 512
NSLOT = NE * CAP
FFD = 4096
LN_EPS = 1e-5
LAMBDA_INIT = 0.8 - 0.6 * math.exp(-0.3 * (2 - 1))

N_BLOCKS = 4


def _host_consts():
    c = {}
    c["ident"] = np.eye(P, dtype=BFN)
    c["ones1"] = np.ones((1, P), dtype=BFN)
    c["ones1f"] = np.ones((1, P), dtype=np.float32)
    c["onescol"] = np.ones((P, 1), dtype=BFN)
    c["ones128"] = np.ones((P, P), dtype=BFN)
    c["triu"] = np.triu(np.ones((P, P), dtype=np.float32)).astype(BFN)
    s = np.arange(T)[:, None].astype(np.float64)
    f = np.arange(FP)[None, :].astype(np.float64)
    ang = 2.0 * np.pi * (s * f) / T
    live = (f <= 1024).astype(np.float64)
    c["CF"] = (np.cos(ang) * live).astype(BFN)
    c["SF"] = (-np.sin(ang) * live).astype(BFN)
    w = np.where((f == 0) | (f == 1024), 1.0, 2.0) * live / T
    c["CI"] = np.ascontiguousarray((w * np.cos(ang)).T).astype(BFN)
    c["SI"] = np.ascontiguousarray((-w * np.sin(ang)).T).astype(BFN)
    c["ecap"] = (np.arange(NE, dtype=np.float32) * CAP).reshape(1, NE)
    sel = []
    for lv in range(8):
        S_lv = np.zeros((512, 256), dtype=np.float32)
        for m in range(256):
            node = (2 ** lv - 1) + (m >> (8 - lv))
            branch = (m >> (8 - lv - 1)) & 1
            row = node if branch else 256 + node
            S_lv[row, m] = 1.0
        for ch in ([0, 2] if lv < 7 else [0, 1, 2, 3]):
            sel.append(S_lv[ch * 128:(ch + 1) * 128, :])
    c["selp"] = np.stack(sel).astype(BFN)
    return c

SEL_MAP = [(lv, ch) for lv in range(8) for ch in ([0, 2] if lv < 7 else [0, 1, 2, 3])]
SEL_BY_LEVEL = {}
for _j, (_lv, _ch) in enumerate(SEL_MAP):
    SEL_BY_LEVEL.setdefault(_lv, []).append((_j, _ch))


def build_nc(n_blocks=4):
    nc = bass.Bass("TRN2", target_bir_lowering=False, detect_race_conditions=False)

    def din(name, shape, dt=F32):
        return nc.dram_tensor(name, list(shape), dt, kind="ExternalInput").ap()

    x_in = din("x", (T, D))
    wqT = din("wqT", (D, D)); wkT = din("wkT", (D, D)); wvT = din("wvT", (D, D)); woT = din("woT", (D, D))
    bqkvo = din("bqkvo", (4, D))
    ffin = din("ff", (1, D))
    lng0 = din("lng0", (1, D)); lnb0 = din("lnb0", (1, D))
    centT = din("centT", (D, NE))
    lng8 = din("lng8", (NE, D)); lnb8 = din("lnb8", (NE, D))
    b1d = din("b1d", (NE, FFD)); b2d = din("b2d", (NE, D))
    W1d = din("W1d", (NE, D, FFD)); W2d = din("W2d", (NE, FFD, D))
    q1T = din("q1T", (D, D)); q2T = din("q2T", (D, D)); k1T = din("k1T", (D, D)); k2T = din("k2T", (D, D))
    vTd = din("vT", (D, 2 * D)); cTd = din("cT", (2 * D, D))
    sublnd = din("subln", (P, 1))
    lpackd = din("lpack", (1, 256))
    bngd = din("bng", (P, DC)); bnbd = din("bnb", (P, DC))
    WqT2 = din("WqT2", (D, D)); routersT = din("routersT", (D, 256))
    leafVd = din("leafV", (256, D)); WoutT = din("WoutT", (D, D)); woutbd = din("woutb", (1, D))
    ident = din("ident", (P, P), BF16)
    ones1 = din("ones1", (1, P), BF16)
    ones1f = din("ones1f", (1, P), F32)
    onescol = din("onescol", (P, 1), BF16)
    ones128 = din("ones128", (P, P), BF16)
    triu = din("triu", (P, P), BF16)
    CFd = din("CF", (T, FP), BF16); SFd = din("SF", (T, FP), BF16)
    CId = din("CI", (FP, T), BF16); SId = din("SI", (FP, T), BF16)
    ecapd = din("ecap", (1, NE))
    selpd = din("selp", (18, P, 256), BF16)

    out_d = nc.dram_tensor("out", [P, TC, D], F32, kind="ExternalOutput").ap()
    xs_d = nc.dram_tensor("xs", [P, TC, D], F32).ap()
    zslot_d = nc.dram_tensor("zslot", [NSLOT, D], BF16).ap()
    dslot_d = nc.dram_tensor("dslot", [NSLOT, D], BF16).ap()
    bnin_d = nc.dram_tensor("bnin", [P, 16], F32).ap()
    bnout_d = nc.dram_tensor("bnout", [P, 16], F32).ap()

    CFr = CFd.rearrange("(a b) f -> b a f", b=P)
    SFr = SFd.rearrange("(a b) f -> b a f", b=P)
    CIr = CId.rearrange("(a b) s -> b a s", b=P)
    SIr = SId.rearrange("(a b) s -> b a s", b=P)
    zslot_r = zslot_d.rearrange("(e a b) d -> e b a d", e=NE, b=P)
    dslot_r = dslot_d.rearrange("(e a b) d -> e b a d", e=NE, b=P)

    stack = ExitStack()
    sem_names = ["qd", "gd", "v", "s", "t", "g", "cc", "wa", "wb", "dfa", "q0", "q1", "q2", "q3"]
    sems = {n: stack.enter_context(nc.semaphore(n)) for n in sem_names}
    C = {n: 0 for n in sem_names}

    DMA_SEMS = {"qd", "gd", "wa", "wb", "dfa", "q0", "q1", "q2", "q3"}

    def inc(inst, n, k=1):
        if n in DMA_SEMS:
            k = 16
        inst.then_inc(sems[n], k)
        C[n] += k
        return C[n]

    def wait(eng, n, val=None):
        eng.wait_ge(sems[n], C[n] if val is None else val)

    def wait_all(eng):
        for n in ["qd", "gd", "v", "s", "t", "g", "wa", "wb", "dfa", "cc",
                  "q0", "q1", "q2", "q3"]:
            if C[n]:
                eng.wait_ge(sems[n], C[n])

    ps = [stack.enter_context(nc.psum_tensor(f"ps{i}", [P, 512], F32)) for i in range(6)]
    pt = [stack.enter_context(nc.psum_tensor(f"pt{i}", [P, P], BF16)) for i in range(2)]

    block = stack.enter_context(nc.Block())

    def sec(eng, fn):
        getattr(block, eng)(fn)

    _name_ctr = [0]

    def sb(name, shape, dt):
        _name_ctr[0] += 1
        return nc.sbuf_tensor(f"{name}_{_name_ctr[0]}", list(shape), dt)

    g_ident = stack.enter_context(sb("g_ident", (P, P), BF16))
    g_ones1 = stack.enter_context(sb("g_ones1", (1, P), BF16))
    g_ones1f = stack.enter_context(sb("g_ones1f", (1, P), F32))
    g_onescol = stack.enter_context(sb("g_onescol", (P, 1), BF16))
    g_ones128 = stack.enter_context(sb("g_ones128", (P, P), BF16))
    g_triu = stack.enter_context(sb("g_triu", (P, P), BF16))
    xstr = stack.enter_context(sb("xstr", (P, 2, D), F32))
    xou = stack.enter_context(sb("xou", (P, 2, D), F32))
    st1 = stack.enter_context(sb("st1", (P, TC), F32))
    st2 = stack.enter_context(sb("st2", (P, TC), F32))
    st3 = stack.enter_context(sb("st3", (P, TC), F32))
    st4 = stack.enter_context(sb("st4", (P, TC), F32))
    rowA = stack.enter_context(sb("rowA", (1, D), F32))
    rowB = stack.enter_context(sb("rowB", (1, D), F32))
    escr = stack.enter_context(sb("escr", (P, 4, D), F32))

    def ld_consts(sy):
        for dst, src in [(g_ident, ident), (g_ones1, ones1), (g_ones1f, ones1f), (g_onescol, onescol),
                         (g_ones128, ones128), (g_triu, triu)]:
            inst = sy.dma_start(out=dst[:, :], in_=src[:, :])
            inc(inst, "qd")
    sec("sync", ld_consts)


    def bcast_row(src_row_ap, dst_tile_ap, ncols):
        nmm = (ncols + 511) // 512
        t0 = C["t"]; v0 = C["v"]
        def tf(t):
            wait_all(t)
            for j in range(nmm):
                if j >= 2:
                    t.wait_ge(sems["v"], v0 + j - 1)
                cw = min(512, ncols - j * 512)
                inst = t.matmul(ps[4 + (j % 2)][:, 0:cw], g_ones1f[:, :],
                                src_row_ap[0:1, j * 512:j * 512 + cw], start=True, stop=True)
                inc(inst, "t")
        sec("tensor", tf)
        def vf(v):
            for j in range(nmm):
                cw = min(512, ncols - j * 512)
                v.wait_ge(sems["t"], t0 + j + 1)
                inst = v.tensor_copy(dst_tile_ap[:, j * 512:j * 512 + cw], ps[4 + (j % 2)][:, 0:cw])
                inc(inst, "v")
        sec("vector", vf)

    def load_row_bcast(src_dram_row, dst_tile_ap, ncols):
        def ld(sy):
            wait_all(sy)
            inst = sy.dma_start(out=rowB[0:1, 0:ncols], in_=src_dram_row)
            inc(inst, "qd")
        sec("sync", ld)
        bcast_row(rowB, dst_tile_ap, ncols)

    def transpose_tm_fm(src_tm, dst_fm, nt=TC, nd=DC):
        t0 = C["t"]; v0 = C["v"]
        def tf(t):
            wait_all(t)
            j = 0
            for tc in range(nt):
                for dc in range(nd):
                    if j >= 2:
                        t.wait_ge(sems["v"], v0 + j - 1)
                    inst = t.transpose(pt[j % 2][:, :], src_tm[:, tc, dc * P:(dc + 1) * P], g_ident[:, :])
                    inc(inst, "t")
                    j += 1
        sec("tensor", tf)
        def vf(v):
            j = 0
            for tc in range(nt):
                for dc in range(nd):
                    v.wait_ge(sems["t"], t0 + j + 1)
                    inst = v.tensor_copy(dst_fm[:, dc, tc * P:(tc + 1) * P], pt[j % 2][:, :])
                    inc(inst, "v")
                    j += 1
        sec("vector", vf)

    def ln_u_from_dram(xsrc, u_tm):
        qp0 = [C["q0"], C["q1"]]; v0 = C["v"]
        def dsec(sy):
            wait_all(sy)
            for tc in range(TC):
                if tc >= 2:
                    sy.wait_ge(sems["v"], v0 + (tc - 1))
                inst = sy.dma_start(out=xstr[:, tc % 2, :], in_=xsrc[:, tc, :])
                inc(inst, f"q{tc % 2}")
        sec("sync", dsec)
        def vsec(v):
            v.wait_ge(sems["q2"], C["q2"])
            v.wait_ge(sems["q3"], C["q3"])
            for tc in range(TC):
                v.wait_ge(sems[f"q{tc % 2}"], qp0[tc % 2] + 16 * (tc // 2 + 1))
                xc = xstr[:, tc % 2, :]
                v.reduce_sum(st1[:, tc:tc + 1], xc, axis=AX)
                inst = v.scalar_tensor_tensor(xou[:, tc % 2, :], xc, 1.0, xc, ALU.mult, ALU.mult,
                                              accum_out=st2[:, tc:tc + 1])
                inc(inst, "v")
        sec("vector", vsec)
        def vf(v):
            wait(v, "v")
            v.tensor_scalar(st1[:, :], st1[:, :], 1.0 / D, None, ALU.mult)
            inst = v.tensor_scalar(st2[:, :], st2[:, :], 1.0 / D, None, ALU.mult)
            inc(inst, "v")
            wait(v, "v")
            inst = v.scalar_tensor_tensor(st3[:, :], st1[:, :], 1.0, st1[:, :], ALU.mult, ALU.mult)
            inc(inst, "v")
            wait(v, "v")
            inst = v.tensor_tensor(st2[:, :], st2[:, :], st3[:, :], ALU.subtract)
            inc(inst, "v")
            wait(v, "v")
            inst = v.tensor_scalar(st2[:, :], st2[:, :], LN_EPS, None, ALU.add)
            inc(inst, "v")
        sec("vector", vf)
        def sf(sc):
            wait(sc, "v")
            inst = sc.activation(st2[:, :], st2[:, :], AF.Ln, scale=1.0)
            inc(inst, "s")
            wait(sc, "s")
            inst = sc.activation(st2[:, :], st2[:, :], AF.Exp, scale=-0.5)
            inc(inst, "s")
        sec("scalar", sf)
        qp1 = [C["q0"], C["q1"]]; v1 = C["v"]
        def dsec2(sy):
            wait(sy, "s")
            for tc in range(TC):
                if tc >= 2:
                    sy.wait_ge(sems["v"], v1 + 2 * (tc - 1))
                inst = sy.dma_start(out=xstr[:, tc % 2, :], in_=xsrc[:, tc, :])
                inc(inst, f"q{tc % 2}")
        sec("sync", dsec2)
        def vsec2(v):
            for tc in range(TC):
                v.wait_ge(sems[f"q{tc % 2}"], qp1[tc % 2] + 16 * (tc // 2 + 1))
                inst = v.tensor_scalar(st3[:, tc:tc + 1], st1[:, tc:tc + 1], -1.0,
                                       st2[:, tc:tc + 1], ALU.mult, ALU.mult)
                inc(inst, "v")
                v.wait_ge(sems["v"], C["v"])
                inst = v.tensor_scalar(u_tm[:, tc, :], xstr[:, tc % 2, :], st2[:, tc:tc + 1],
                                       st3[:, tc:tc + 1], ALU.mult, ALU.add)
                inc(inst, "v")
        sec("vector", vsec2)

    def load_w(wbuf, wT_dram, ncols, col_off=0, total_cols=None):
        def g(gp):
            wait_all(gp)
            src = wT_dram.rearrange("(a b) n -> b a n", b=P)
            src = src[:, :, col_off:col_off + ncols]
            inst = gp.dma_start(out=wbuf[:, :DC, 0:ncols], in_=src)
            inc(inst, "gd")
        sec("gpsimd", g)

    def proj_tm(u_fm, wbuf, dst_tm, brep=None, ncols=D, kc_n=DC, out_off=0):
        nh = (ncols + 511) // 512
        t0 = C["t"]; v0 = C["v"]
        def tf(t):
            wait_all(t)
            j = 0
            for tc in range(TC):
                for h in range(nh):
                    if j >= 4:
                        t.wait_ge(sems["v"], v0 + j - 3)
                    cw = min(512, ncols - h * 512)
                    for kc in range(kc_n):
                        inst = t.matmul(ps[j % 4][:, 0:cw], u_fm[:, kc, tc * P:(tc + 1) * P],
                                        wbuf[:, kc, h * 512:h * 512 + cw],
                                        start=(kc == 0), stop=(kc == kc_n - 1))
                    inc(inst, "t")
                    j += 1
        sec("tensor", tf)
        def vf(v):
            j = 0
            for tc in range(TC):
                for h in range(nh):
                    v.wait_ge(sems["t"], t0 + j + 1)
                    cw = min(512, ncols - h * 512)
                    o0 = out_off + h * 512
                    dstap = dst_tm[:, tc, o0:o0 + cw]
                    if brep is not None:
                        inst = v.tensor_tensor(dstap, ps[j % 4][:, 0:cw],
                                               brep[:, o0:o0 + cw], ALU.add)
                    else:
                        inst = v.tensor_copy(dstap, ps[j % 4][:, 0:cw])
                    inc(inst, "v")
                    j += 1
        sec("vector", vf)

    def proj_fm(x_fm, wbuf, dst_fm, mc_n, kc_n=DC, wcol_of_mc=lambda mc: None, act=None, bias_col=None):
        t0 = C["t"]
        ekey = "v" if act is None else "s"
        e0 = C[ekey]
        def tf(t):
            wait_all(t)
            j = 0
            for mc in range(mc_n):
                for tcol in range(4):
                    if j >= 4:
                        t.wait_ge(sems[ekey], e0 + j - 3)
                    for kc in range(kc_n):
                        inst = t.matmul(ps[j % 4][:, :], wbuf[:, kc, mc * P:(mc + 1) * P],
                                        x_fm[:, kc, tcol * 512:(tcol + 1) * 512],
                                        start=(kc == 0), stop=(kc == kc_n - 1))
                    inc(inst, "t")
                    j += 1
        sec("tensor", tf)
        def ef(e):
            j = 0
            for mc in range(mc_n):
                for tcol in range(4):
                    e.wait_ge(sems["t"], t0 + j + 1)
                    dstap = dst_fm[:, mc, tcol * 512:(tcol + 1) * 512]
                    if act is None:
                        inst = e.tensor_copy(dstap, ps[j % 4][:, :])
                    else:
                        if bias_col is not None:
                            inst = e.activation(dstap, ps[j % 4][:, :], act, bias=bias_col[:, mc:mc + 1])
                        else:
                            inst = e.activation(dstap, ps[j % 4][:, :], act)
                    inc(inst, ekey)
                    j += 1
        sec("vector" if act is None else "scalar", ef)

    def plain_ln_epilogue(psrc_of_tc, src, dst, extra_brep=None, entry_snap=None):
        v0 = C["v"]; s0 = C["s"]
        qp0 = [C["q0"], C["q1"]]; sp0 = [C["q2"], C["q3"]]
        vmark = []
        KV = 9
        def dsec(sy):
            snapw = entry_snap or dict(C)
            for n, val in snapw.items():
                if n in sems and val:
                    sy.wait_ge(sems[n], val)
            for tc in range(TC):
                if tc >= 2:
                    sy.wait_ge(sems["v"], v0 + KV * (tc - 1))
                inst = sy.dma_start(out=xstr[:, tc % 2, :], in_=src[:, tc, :])
                inc(inst, f"q{tc % 2}")
        sec("sync", dsec)
        def vf(v):
            for tc in range(TC):
                plo, phi = psrc_of_tc(tc)
                v.wait_ge(sems[f"q{tc % 2}"], qp0[tc % 2] + 16 * (tc // 2 + 1))
                v.wait_ge(sems["t"], tmark_ref[0][tc])
                if tc >= 2:
                    v.wait_ge(sems[f"q{2 + tc % 2}"], sp0[tc % 2] + 16 * ((tc - 2) // 2 + 1))
                else:
                    v.wait_ge(sems[f"q{2 + tc % 2}"], sp0[tc % 2])
                if extra_brep is not None:
                    v.tensor_tensor(escr[:, 1, 0:512], plo, extra_brep[:, 0:512], ALU.add)
                    inst = v.tensor_tensor(escr[:, 1, 512:1024], phi, extra_brep[:, 512:1024], ALU.add)
                else:
                    v.tensor_copy(escr[:, 1, 0:512], plo)
                    inst = v.tensor_copy(escr[:, 1, 512:1024], phi)
                inc(inst, "v")
                v.wait_ge(sems["v"], C["v"])
                v.reduce_sum(st3[:, tc:tc + 1], escr[:, 1, :], axis=AX)
                inst = v.scalar_tensor_tensor(escr[:, 2, :], escr[:, 1, :], 1.0, escr[:, 1, :],
                                              ALU.mult, ALU.mult, accum_out=st4[:, tc:tc + 1])
                inc(inst, "v")
                v.wait_ge(sems["v"], C["v"])
                v.tensor_scalar(st3[:, tc:tc + 1], st3[:, tc:tc + 1], 1.0 / D, None, ALU.mult)
                inst = v.tensor_scalar(st4[:, tc:tc + 1], st4[:, tc:tc + 1], 1.0 / D, None, ALU.mult)
                inc(inst, "v")
                v.wait_ge(sems["v"], C["v"])
                inst = v.scalar_tensor_tensor(escr[:, 3, 0:1], st3[:, tc:tc + 1], 1.0,
                                              st3[:, tc:tc + 1], ALU.mult, ALU.mult)
                inc(inst, "v")
                v.wait_ge(sems["v"], C["v"])
                inst = v.tensor_scalar(st4[:, tc:tc + 1], escr[:, 3, 0:1], -1.0, st4[:, tc:tc + 1],
                                       ALU.mult, ALU.add)
                inc(inst, "v")
                v.wait_ge(sems["v"], C["v"])
                inst = v.tensor_scalar(st4[:, tc:tc + 1], st4[:, tc:tc + 1], LN_EPS, None, ALU.add)
                inc(inst, "v")
                v.wait_ge(sems["s"], s0 + 2 * (tc + 1))
                inst = v.tensor_scalar(escr[:, 3, 0:1], st3[:, tc:tc + 1], -1.0, st4[:, tc:tc + 1],
                                       ALU.mult, ALU.mult)
                inc(inst, "v")
                v.wait_ge(sems["v"], C["v"])
                inst = v.tensor_scalar(escr[:, 0, :], escr[:, 1, :], st4[:, tc:tc + 1],
                                       escr[:, 3, 0:1], ALU.mult, ALU.add)
                inc(inst, "v")
                v.wait_ge(sems["v"], C["v"])
                inst = v.tensor_tensor(xou[:, tc % 2, :], xstr[:, tc % 2, :], escr[:, 0, :], ALU.add)
                inc(inst, "v")
                vmark.append(v0 + KV * tc + 1)
        sec("vector", vf)
        def sf(sc):
            for tc in range(TC):
                sc.wait_ge(sems["v"], v0 + KV * tc + 6)
                inst = sc.activation(st4[:, tc:tc + 1], st4[:, tc:tc + 1], AF.Ln, scale=1.0)
                inc(inst, "s")
                sc.wait_ge(sems["s"], C["s"])
                inst = sc.activation(st4[:, tc:tc + 1], st4[:, tc:tc + 1], AF.Exp, scale=-0.5)
                inc(inst, "s")
        sec("scalar", sf)
        def osec(gp):
            for tc in range(TC):
                gp.wait_ge(sems["v"], v0 + KV * (tc + 1))
                inst = gp.dma_start(out=dst[:, tc, :], in_=xou[:, tc % 2, :])
                inc(inst, f"q{2 + tc % 2}")
        sec("gpsimd", osec)
        return v0, KV

    tmark_ref = [{}]
    qd_out_ref = [0]

    def proj_tm_epilogue(in_fm, wbuf, src, dst, extra_brep, kc_n=DC):
        entry_snap = dict(C)
        t0 = C["t"]
        tmark = {}
        v0_expected = C["v"]
        KV = 9
        def tf(t):
            wait_all(t)
            j = 0
            for tc in range(TC):
                for h in range(2):
                    if j >= 4:
                        t.wait_ge(sems["v"], v0_expected + KV * (tc - 2) + 1)
                    for kc in range(kc_n):
                        inst = t.matmul(ps[j % 4][:, :], in_fm[:, kc, tc * P:(tc + 1) * P],
                                        wbuf[:, kc, h * 512:(h + 1) * 512],
                                        start=(kc == 0), stop=(kc == kc_n - 1))
                    inc(inst, "t")
                    j += 1
                tmark[tc] = C["t"]
        sec("tensor", tf)
        tmark_ref[0] = tmark
        def psrc(tc):
            return ps[(2 * tc) % 4][:, :], ps[(2 * tc + 1) % 4][:, :]
        plain_ln_epilogue(psrc, src, dst, extra_brep=extra_brep, entry_snap=entry_snap)

    def block_l0_attn(src, dst):
        with (
            sb("QM", (P, FC, D), BF16) as QM,
            sb("Vc", (P, FC, D), BF16) as Vc,
            sb("Vs", (P, FC, D), BF16) as Vs,
            sb("scrA", (P, 2 * D), F32) as scrA,
            sb("scrB", (P, 2 * D), F32) as scrB,
            sb("repF", (P, D), F32) as repF,
        ):
            scrS = escr[:, 0:2, :]
            def fwd_dft(src_tm, kind, out_c, out_s=None):
                with sb("dfc", (P, TC, P), BF16) as dfc, sb("dfs", (P, TC, P), BF16) as dfs:
                    df0 = C["dfa"]; t0 = C["t"]; s0 = C["s"]; v0 = C["v"]
                    def dload(sy):
                        wait_all(sy)
                        for fc in range(FC):
                            if fc >= 1:
                                sy.wait_ge(sems["t"], t0 + fc)
                            i1 = sy.dma_start(out=dfc[:, :, :], in_=CFr[:, :, fc * P:(fc + 1) * P])
                            inc(i1, "dfa")
                            i2 = sy.dma_start(out=dfs[:, :, :], in_=SFr[:, :, fc * P:(fc + 1) * P])
                            inc(i2, "dfa")
                    sec("sync", dload)
                    def tf(t):
                        wait(t, "v"); wait(t, "s")
                        for fc in range(FC):
                            t.wait_ge(sems["dfa"], df0 + 32 * (fc + 1))
                            if fc >= 1:
                                if kind == "mag":
                                    t.wait_ge(sems["s"], s0 + 2 * fc - 1)
                                else:
                                    t.wait_ge(sems["v"], v0 + fc)
                            for h in range(2):
                                for i in range(TC):
                                    t.matmul(ps[2 * h][:, :], dfc[:, i, :],
                                             src_tm[:, i, h * 512:(h + 1) * 512],
                                             start=(i == 0), stop=(i == TC - 1))
                                for i in range(TC):
                                    inst = t.matmul(ps[2 * h + 1][:, :], dfs[:, i, :],
                                                    src_tm[:, i, h * 512:(h + 1) * 512],
                                                    start=(i == 0), stop=(i == TC - 1))
                            inc(inst, "t")
                    sec("tensor", tf)
                    if kind == "copy":
                        def vf(v):
                            for fc in range(FC):
                                v.wait_ge(sems["t"], t0 + fc + 1)
                                v.tensor_copy(out_c[:, fc, 0:512], ps[0][:, :])
                                v.tensor_copy(out_s[:, fc, 0:512], ps[1][:, :])
                                v.tensor_copy(out_c[:, fc, 512:1024], ps[2][:, :])
                                inst = v.tensor_copy(out_s[:, fc, 512:1024], ps[3][:, :])
                                inc(inst, "v")
                        sec("vector", vf)
                    else:
                        def sf(sc):
                            for fc in range(FC):
                                p_ = fc % 2
                                sc.wait_ge(sems["t"], t0 + fc + 1)
                                if fc >= 2:
                                    sc.wait_ge(sems["v"], v0 + fc - 1)
                                sc.activation(scrA[:, p_ * D:p_ * D + 512], ps[0][:, :], AF.Square)
                                sc.activation(scrB[:, p_ * D:p_ * D + 512], ps[1][:, :], AF.Square)
                                sc.activation(scrA[:, p_ * D + 512:(p_ + 1) * D], ps[2][:, :], AF.Square)
                                inst = sc.activation(scrB[:, p_ * D + 512:(p_ + 1) * D], ps[3][:, :], AF.Square)
                                inc(inst, "s")
                                sc.wait_ge(sems["v"], v0 + fc + 1)
                                inst = sc.activation(out_c[:, fc, :], escr[:, p_, :], AF.Ln, scale=1.0)
                                inc(inst, "s")
                        sec("scalar", sf)
                        def vf(v):
                            for fc in range(FC):
                                p_ = fc % 2
                                v.wait_ge(sems["s"], s0 + 2 * fc + 1)
                                inst = v.scalar_tensor_tensor(escr[:, p_, :], scrA[:, p_ * D:(p_ + 1) * D],
                                                              1e-30, scrB[:, p_ * D:(p_ + 1) * D],
                                                              ALU.add, ALU.add)
                                inc(inst, "v")
                        sec("vector", vf)

            with sb("u_fm", (P, DC, T), BF16) as u_fm:
                with sb("u_tm", (P, TC, D), BF16) as u_tm:
                    ln_u_from_dram(src, u_tm)
                    transpose_tm_fm(u_tm, u_fm)
                def ldff(sy):
                    wait_all(sy)
                    inst = sy.dma_start(out=rowA[0:1, 0:D], in_=ffin[0:1, :])
                    inc(inst, "qd")
                sec("sync", ldff)
                def sqff(v):
                    wait(v, "qd")
                    inst = v.tensor_tensor(rowA[0:1, 0:D], rowA[0:1, 0:D], rowA[0:1, 0:D], ALU.mult)
                    inc(inst, "v")
                sec("vector", sqff)
                bcast_row(rowA, repF, D)

                with (
                    sb("KM", (P, FC, D), BF16) as KM,
                    sb("tmb", (P, TC, D), BF16) as tmb,
                ):
                    brep = xou[:, 0, :]
                    for wT, brow, oc in [(wqT, 0, QM), (wkT, 1, KM)]:
                        load_row_bcast(bqkvo[brow:brow + 1, :], brep, D)
                        with sb(f"wbuf{brow}", (P, DC, 512), BF16) as wbuf:
                            for hh in range(2):
                                load_w(wbuf, wT, 512, col_off=hh * 512)
                                proj_tm(u_fm, wbuf, tmb, brep=brep, ncols=512, out_off=hh * 512)
                        fwd_dft(tmb, "mag", oc)
                    for fc in range(FC):
                        p_ = fc % 2
                        def v1(v, fc=fc, p_=p_):
                            if fc == 0:
                                wait_all(v)
                            wait(v, "s")
                            inst = v.tensor_tensor(escr[:, p_, :], QM[:, fc, :], KM[:, fc, :], ALU.add)
                            inc(inst, "v")
                        sec("vector", v1)
                        def s1(sc, fc=fc, p_=p_):
                            wait(sc, "v")
                            inst = sc.activation(scrA[:, p_ * D:(p_ + 1) * D], escr[:, p_, :], AF.Exp, scale=0.5)
                            inc(inst, "s")
                        sec("scalar", s1)
                        def v2(v, fc=fc, p_=p_):
                            wait(v, "s")
                            inst = v.tensor_tensor(QM[:, fc, :], scrA[:, p_ * D:(p_ + 1) * D], repF[:, :], ALU.mult)
                            inc(inst, "v")
                        sec("vector", v2)
                with sb("tmbv", (P, TC, D), BF16) as tmbv:
                    load_row_bcast(bqkvo[2:3, :], repF, D)
                    with sb("wbufv", (P, DC, 512), BF16) as wbufv:
                        for hh in range(2):
                            load_w(wbufv, wvT, 512, col_off=hh * 512)
                            proj_tm(u_fm, wbufv, tmbv, brep=repF, ncols=512, out_off=hh * 512)
                    fwd_dft(tmbv, "copy", Vc, Vs)

            with (
                sb("attn", (P, TC, D), BF16) as attn,
            ):
                Oc, Os = Vc, Vs
                with sb("ci1", (P, FC, P), BF16) as ci1:
                    df0 = C["dfa"]
                    t0 = C["t"]; v0 = C["v"]; s0 = C["s"]
                    tmk = {scn: (t0 + 1 if scn == 0 else t0 + 3 + scn) for scn in range(TC)}
                    vfree = {scn: v0 + 2 + scn + 1 for scn in range(TC)}
                    def dload(sy):
                        wait_all(sy)
                        for scn in range(TC):
                            if scn >= 1:
                                sy.wait_ge(sems["t"], tmk[scn - 1])
                            inst = sy.dma_start(out=ci1[:, :, :],
                                                in_=CIr[:, :, scn * P:(scn + 1) * P])
                            inc(inst, "dfa")
                    sec("sync", dload)
                    def tf(t):
                        wait(t, "v"); wait(t, "s")
                        for scn in range(TC):
                            t.wait_ge(sems["dfa"], df0 + 16 * (scn + 1))
                            if scn == 1:
                                t.wait_ge(sems["v"], v0 + 1)
                            if scn >= 2:
                                t.wait_ge(sems["v"], vfree[scn - 2])
                            pb = 2 * (scn % 2)
                            for h in range(2):
                                for fc in range(FC):
                                    inst = t.matmul(ps[pb + h][:, :], ci1[:, fc, :],
                                                    QM[:, fc, h * 512:(h + 1) * 512],
                                                    start=(fc == 0), stop=(fc == FC - 1))
                            inc(inst, "t")
                            assert C["t"] == tmk[scn], (scn, C["t"], tmk[scn])
                            if scn == 0:
                                t.wait_ge(sems["v"], v0 + 1)
                                for j in range(2):
                                    inst = t.matmul(ps[4 + j][:, :], g_ones1f[:, :],
                                                    rowA[0:1, j * 512:(j + 1) * 512],
                                                    start=True, stop=True)
                                    inc(inst, "t")
                    sec("tensor", tf)
                    def vf(v):
                        v.wait_ge(sems["t"], t0 + 1)
                        v.tensor_copy(rowA[0:1, 0:512], ps[0][0:1, :])
                        inst = v.tensor_copy(rowA[0:1, 512:1024], ps[1][0:1, :])
                        inc(inst, "v")
                        v.wait_ge(sems["t"], t0 + 3)
                        v.tensor_copy(repF[:, 0:512], ps[4][:, :])
                        inst = v.tensor_copy(repF[:, 512:1024], ps[5][:, :])
                        inc(inst, "v")
                        v.wait_ge(sems["v"], v0 + 2)
                        for scn in range(TC):
                            v.wait_ge(sems["t"], tmk[scn])
                            if scn >= 2:
                                v.wait_ge(sems["s"], s0 + (scn - 1))
                            pb = 2 * (scn % 2)
                            p_ = scn % 2
                            v.tensor_tensor(scrA[:, p_ * D:p_ * D + 512], ps[pb][:, :],
                                            repF[:, 0:512], ALU.subtract)
                            inst = v.tensor_tensor(scrA[:, p_ * D + 512:(p_ + 1) * D], ps[pb + 1][:, :],
                                                   repF[:, 512:1024], ALU.subtract)
                            inc(inst, "v")
                    sec("vector", vf)
                    def sf(sc):
                        for scn in range(TC):
                            sc.wait_ge(sems["v"], v0 + 2 + scn + 1)
                            p_ = scn % 2
                            sc.activation(attn[:, scn, 0:512], scrA[:, p_ * D:p_ * D + 512], AF.Exp)
                            inst = sc.activation(attn[:, scn, 512:1024],
                                                 scrA[:, p_ * D + 512:(p_ + 1) * D], AF.Exp)
                            inc(inst, "s")
                    sec("scalar", sf)

                t0 = C["t"]
                def ztf(t):
                    wait_all(t)
                    for h in range(2):
                        for i in range(TC):
                            inst = t.matmul(ps[4 + h][0:1, :], g_onescol[:, :],
                                            attn[:, i, h * 512:(h + 1) * 512],
                                            start=(i == 0), stop=(i == TC - 1))
                        inc(inst, "t")
                sec("tensor", ztf)
                def zvf(v):
                    v.wait_ge(sems["t"], t0 + 2)
                    v.tensor_copy(rowA[0:1, 0:512], ps[4][0:1, :])
                    inst = v.tensor_copy(rowA[0:1, 512:1024], ps[5][0:1, :])
                    inc(inst, "v")
                    wait(v, "v")
                    inst = v.reciprocal(rowA[0:1, 0:D], rowA[0:1, 0:D])
                    inc(inst, "v")
                sec("vector", zvf)
                bcast_row(rowA, repF, D)
                def vapply(v):
                    wait(v, "v")
                    for fc in range(FC):
                        v.tensor_tensor(Vc[:, fc, :], Vc[:, fc, :], repF[:, :], ALU.mult)
                        inst = v.tensor_tensor(Vs[:, fc, :], Vs[:, fc, :], repF[:, :], ALU.mult)
                        inc(inst, "v")
                        v.wait_ge(sems["v"], C["v"])
                sec("vector", vapply)

                with sb("dfc", (P, TC, P), BF16) as dfc, sb("dfs", (P, TC, P), BF16) as dfs:
                    df0 = C["dfa"]; t0 = C["t"]; v0 = C["v"]
                    def dload(sy):
                        wait_all(sy)
                        for fc in range(FC):
                            if fc >= 1:
                                sy.wait_ge(sems["t"], t0 + fc)
                            i1 = sy.dma_start(out=dfc[:, :, :], in_=CFr[:, :, fc * P:(fc + 1) * P])
                            inc(i1, "dfa")
                            i2 = sy.dma_start(out=dfs[:, :, :], in_=SFr[:, :, fc * P:(fc + 1) * P])
                            inc(i2, "dfa")
                    sec("sync", dload)
                    def tf(t):
                        wait(t, "v"); wait(t, "s")
                        for fc in range(FC):
                            t.wait_ge(sems["dfa"], df0 + 32 * (fc + 1))
                            if fc >= 1:
                                t.wait_ge(sems["v"], v0 + 8 * fc)
                            for h in range(2):
                                for i in range(TC):
                                    t.matmul(ps[2 * h][:, :], dfc[:, i, :],
                                             attn[:, i, h * 512:(h + 1) * 512],
                                             start=(i == 0), stop=(i == TC - 1))
                                for i in range(TC):
                                    inst = t.matmul(ps[2 * h + 1][:, :], dfs[:, i, :],
                                                    attn[:, i, h * 512:(h + 1) * 512],
                                                    start=(i == 0), stop=(i == TC - 1))
                            inc(inst, "t")
                    sec("tensor", tf)
                    def vf(v):
                        for fc in range(FC):
                            v.wait_ge(sems["t"], t0 + fc + 1)
                            for h in range(2):
                                sl = slice(h * 512, (h + 1) * 512)
                                pc, psn = ps[2 * h], ps[2 * h + 1]
                                v.scalar_tensor_tensor(scrA[:, 0:512], pc[:, :], 1.0, Vc[:, fc, sl], ALU.mult, ALU.mult)
                                v.scalar_tensor_tensor(scrA[:, 512:1024], psn[:, :], 1.0, Vs[:, fc, sl], ALU.mult, ALU.mult)
                                v.scalar_tensor_tensor(scrB[:, 0:512], pc[:, :], 1.0, Vs[:, fc, sl], ALU.mult, ALU.mult)
                                inst = v.scalar_tensor_tensor(scrB[:, 512:1024], psn[:, :], 1.0, Vc[:, fc, sl], ALU.mult, ALU.mult)
                                inc(inst, "v")
                                v.wait_ge(sems["v"], C["v"])
                                v.tensor_tensor(Oc[:, fc, sl], scrA[:, 0:512], scrA[:, 512:1024], ALU.subtract)
                                inst = v.tensor_tensor(Os[:, fc, sl], scrB[:, 0:512], scrB[:, 512:1024], ALU.add)
                                inc(inst, "v", 3)
                                v.wait_ge(sems["v"], C["v"])
                    sec("vector", vf)

                with (
                    sb("out_fm", (P, DC, T), BF16) as out_fm,
                    sb("ci2", (P, FC, 512), BF16) as ci2,
                    sb("si2", (P, FC, 512), BF16) as si2,
                ):
                    df0 = C["dfa"]; t0 = C["t"]; v0 = C["v"]
                    def dload(sy):
                        wait_all(sy)
                        for scol in range(4):
                            if scol >= 1:
                                sy.wait_ge(sems["t"], t0 + 8 * scol)
                            i1 = sy.dma_start(out=ci2[:, :, :],
                                              in_=CIr[:, :, scol * 512:(scol + 1) * 512])
                            inc(i1, "dfa")
                            i2 = sy.dma_start(out=si2[:, :, :],
                                              in_=SIr[:, :, scol * 512:(scol + 1) * 512])
                            inc(i2, "dfa")
                    sec("sync", dload)
                    def tf(t):
                        wait(t, "v"); wait(t, "s")
                        j = 0
                        for scol in range(4):
                            t.wait_ge(sems["dfa"], df0 + 32 * (scol + 1))
                            for cc in range(DC):
                                if j >= 4:
                                    t.wait_ge(sems["v"], v0 + j - 3)
                                for fc in range(FC):
                                    t.matmul(ps[j % 4][:, :], Oc[:, fc, cc * P:(cc + 1) * P],
                                             ci2[:, fc, :], start=(fc == 0), stop=False)
                                for fc in range(FC):
                                    inst = t.matmul(ps[j % 4][:, :], Os[:, fc, cc * P:(cc + 1) * P],
                                                    si2[:, fc, :], start=False, stop=(fc == FC - 1))
                                inc(inst, "t")
                                j += 1
                    sec("tensor", tf)
                    def vf(v):
                        j = 0
                        for scol in range(4):
                            for cc in range(DC):
                                v.wait_ge(sems["t"], t0 + j + 1)
                                inst = v.tensor_copy(out_fm[:, cc, scol * 512:(scol + 1) * 512],
                                                     ps[j % 4][:, :])
                                inc(inst, "v")
                                j += 1
                    sec("vector", vf)

                    if True:
                        wbuf2 = attn[:, 0:DC, :]
                        borep = scrB[:, 0:D]
                        grep = scrA[:, 0:D]
                        brep2 = scrA[:, D:2 * D]
                        load_row_bcast(bqkvo[3:4, :], borep, D)
                        load_row_bcast(lng0[0:1, :], grep, D)
                        load_row_bcast(lnb0[0:1, :], brep2, D)
                        load_w(wbuf2, woT, D)
                        affine_ln_epilogue(out_fm, wbuf2, src, dst, borep, grep, brep2)
        return

    def affine_ln_epilogue(in_fm, wbuf, src, dst, borep, grep, brep2):
        entry_snap = dict(C)
        t0 = C["t"]; tmark = {}
        v0 = C["v"]; s0 = C["s"]
        qp0 = [C["q0"], C["q1"]]; sp0 = [C["q2"], C["q3"]]
        KV = 21
        def tf(t):
            wait_all(t)
            j = 0
            for tc in range(TC):
                for h in range(2):
                    if tc >= 2 and h == 0:
                        t.wait_ge(sems["v"], v0 + KV * (tc - 2) + 4)
                    for kc in range(DC):
                        inst = t.matmul(ps[j % 4][:, :], in_fm[:, kc, tc * P:(tc + 1) * P],
                                        wbuf[:, kc, h * 512:(h + 1) * 512],
                                        start=(kc == 0), stop=(kc == DC - 1))
                    inc(inst, "t")
                    j += 1
                tmark[tc] = C["t"]
        sec("tensor", tf)
        def dsec(sy):
            for n, val in entry_snap.items():
                if n in sems and val:
                    sy.wait_ge(sems[n], val)
            for tc in range(TC):
                if tc >= 2:
                    sy.wait_ge(sems["v"], v0 + KV * (tc - 1))
                inst = sy.dma_start(out=xstr[:, tc % 2, :], in_=src[:, tc, :])
                inc(inst, f"q{tc % 2}")
        sec("sync", dsec)
        def vf(v):
            for tc in range(TC):
                v.wait_ge(sems[f"q{tc % 2}"], qp0[tc % 2] + 16 * (tc // 2 + 1))
                v.wait_ge(sems["t"], tmark[tc])
                if tc >= 2:
                    v.wait_ge(sems[f"q{2 + tc % 2}"], sp0[tc % 2] + 16 * ((tc - 2) // 2 + 1))
                else:
                    v.wait_ge(sems[f"q{2 + tc % 2}"], sp0[tc % 2])
                xc = xstr[:, tc % 2, :]
                inst = v.tensor_scalar(st3[:, tc:tc + 1], st1[:, tc:tc + 1], -1.0,
                                       st2[:, tc:tc + 1], ALU.mult, ALU.mult)
                inc(inst, "v")
                v.wait_ge(sems["v"], C["v"])
                inst = v.tensor_scalar(escr[:, 0, :], xc, st2[:, tc:tc + 1],
                                       st3[:, tc:tc + 1], ALU.mult, ALU.add)
                inc(inst, "v")
                v.wait_ge(sems["v"], C["v"])
                v.tensor_tensor(escr[:, 1, 0:512], ps[(2 * tc) % 4][:, :], borep[:, 0:512], ALU.add)
                inst = v.tensor_tensor(escr[:, 1, 512:1024], ps[(2 * tc + 1) % 4][:, :],
                                       borep[:, 512:1024], ALU.add)
                inc(inst, "v")
                v.wait_ge(sems["v"], C["v"])
                inst = v.tensor_tensor(escr[:, 1, :], escr[:, 1, :], escr[:, 0, :], ALU.add)
                inc(inst, "v")
                v.wait_ge(sems["v"], C["v"])
                v.reduce_sum(st3[:, tc:tc + 1], escr[:, 1, :], axis=AX)
                inst = v.scalar_tensor_tensor(escr[:, 2, :], escr[:, 1, :], 1.0, escr[:, 1, :],
                                              ALU.mult, ALU.mult, accum_out=st4[:, tc:tc + 1])
                inc(inst, "v")
                v.wait_ge(sems["v"], C["v"])
                v.tensor_scalar(st3[:, tc:tc + 1], st3[:, tc:tc + 1], 1.0 / D, None, ALU.mult)
                inst = v.tensor_scalar(st4[:, tc:tc + 1], st4[:, tc:tc + 1], 1.0 / D, None, ALU.mult)
                inc(inst, "v")
                v.wait_ge(sems["v"], C["v"])
                inst = v.scalar_tensor_tensor(escr[:, 3, 0:1], st3[:, tc:tc + 1], 1.0,
                                              st3[:, tc:tc + 1], ALU.mult, ALU.mult)
                inc(inst, "v")
                v.wait_ge(sems["v"], C["v"])
                inst = v.tensor_scalar(st4[:, tc:tc + 1], escr[:, 3, 0:1], -1.0, st4[:, tc:tc + 1],
                                       ALU.mult, ALU.add)
                inc(inst, "v")
                v.wait_ge(sems["v"], C["v"])
                inst = v.tensor_scalar(st4[:, tc:tc + 1], st4[:, tc:tc + 1], LN_EPS, None, ALU.add)
                inc(inst, "v")
                v.wait_ge(sems["s"], s0 + 4 * tc + 2)
                inst = v.tensor_scalar(escr[:, 3, 0:1], st3[:, tc:tc + 1], -1.0, st4[:, tc:tc + 1],
                                       ALU.mult, ALU.mult)
                inc(inst, "v")
                v.wait_ge(sems["v"], C["v"])
                inst = v.tensor_scalar(escr[:, 2, :], escr[:, 1, :], st4[:, tc:tc + 1],
                                       escr[:, 3, 0:1], ALU.mult, ALU.add)
                inc(inst, "v")
                v.wait_ge(sems["v"], C["v"])
                inst = v.tensor_tensor(escr[:, 2, :], escr[:, 2, :], grep[:, :], ALU.mult)
                inc(inst, "v")
                v.wait_ge(sems["v"], C["v"])
                inst = v.tensor_tensor(escr[:, 2, :], escr[:, 2, :], brep2[:, :], ALU.add)
                inc(inst, "v")
                v.wait_ge(sems["v"], C["v"])
                v.reduce_sum(st3[:, tc:tc + 1], escr[:, 2, :], axis=AX)
                inst = v.scalar_tensor_tensor(escr[:, 1, :], escr[:, 2, :], 1.0, escr[:, 2, :],
                                              ALU.mult, ALU.mult, accum_out=st4[:, tc:tc + 1])
                inc(inst, "v")
                v.wait_ge(sems["v"], C["v"])
                v.tensor_scalar(st3[:, tc:tc + 1], st3[:, tc:tc + 1], 1.0 / D, None, ALU.mult)
                inst = v.tensor_scalar(st4[:, tc:tc + 1], st4[:, tc:tc + 1], 1.0 / D, None, ALU.mult)
                inc(inst, "v")
                v.wait_ge(sems["v"], C["v"])
                inst = v.scalar_tensor_tensor(escr[:, 3, 0:1], st3[:, tc:tc + 1], 1.0,
                                              st3[:, tc:tc + 1], ALU.mult, ALU.mult)
                inc(inst, "v")
                v.wait_ge(sems["v"], C["v"])
                inst = v.tensor_scalar(st4[:, tc:tc + 1], escr[:, 3, 0:1], -1.0, st4[:, tc:tc + 1],
                                       ALU.mult, ALU.add)
                inc(inst, "v")
                v.wait_ge(sems["v"], C["v"])
                inst = v.tensor_scalar(st4[:, tc:tc + 1], st4[:, tc:tc + 1], LN_EPS, None, ALU.add)
                inc(inst, "v")
                v.wait_ge(sems["s"], s0 + 4 * tc + 4)
                inst = v.tensor_scalar(escr[:, 3, 0:1], st3[:, tc:tc + 1], -1.0, st4[:, tc:tc + 1],
                                       ALU.mult, ALU.mult)
                inc(inst, "v")
                v.wait_ge(sems["v"], C["v"])
                inst = v.tensor_scalar(escr[:, 0, :], escr[:, 2, :], st4[:, tc:tc + 1],
                                       escr[:, 3, 0:1], ALU.mult, ALU.add)
                inc(inst, "v")
                v.wait_ge(sems["v"], C["v"])
                inst = v.tensor_tensor(xou[:, tc % 2, :], xstr[:, tc % 2, :], escr[:, 0, :], ALU.add)
                inc(inst, "v")
        sec("vector", vf)
        def sf(sc):
            for tc in range(TC):
                sc.wait_ge(sems["v"], v0 + KV * tc + 9)
                inst = sc.activation(st4[:, tc:tc + 1], st4[:, tc:tc + 1], AF.Ln, scale=1.0)
                inc(inst, "s")
                sc.wait_ge(sems["s"], C["s"])
                inst = sc.activation(st4[:, tc:tc + 1], st4[:, tc:tc + 1], AF.Exp, scale=-0.5)
                inc(inst, "s")
                sc.wait_ge(sems["v"], v0 + KV * tc + 18)
                inst = sc.activation(st4[:, tc:tc + 1], st4[:, tc:tc + 1], AF.Ln, scale=1.0)
                inc(inst, "s")
                sc.wait_ge(sems["s"], C["s"])
                inst = sc.activation(st4[:, tc:tc + 1], st4[:, tc:tc + 1], AF.Exp, scale=-0.5)
                inc(inst, "s")
        sec("scalar", sf)
        def osec(gp):
            for tc in range(TC):
                gp.wait_ge(sems["v"], v0 + KV * (tc + 1))
                inst = gp.dma_start(out=dst[:, tc, :], in_=xou[:, tc % 2, :])
                inc(inst, f"q{2 + tc % 2}")
        sec("gpsimd", osec)

    def block_moe(src, dst):
        with (
            sb("z_tm", (P, TC, D), BF16) as z_tm,
            sb("SL32", (P, TC), I32) as SL32,
            sb("CUM", (P, TC, NE), F32) as CUM,
            sb("MKb", (P, TC, NE), BF16) as MKb,
            sb("LG", (P, TC, NE), F32) as LG,
            sb("BREP", (P, TC * NE), F32) as BREP,
            sb("ECREP", (P, NE), F32) as ECREP,
            sb("mscr", (P, 2 * D), F32) as mscr,
        ):
            ln_u_from_dram(src, z_tm)

            with sb("u_fm", (P, DC, T), BF16) as u_fm, sb("cb", (P, DC, NE), BF16) as cb:
                transpose_tm_fm(z_tm, u_fm)
                def ldc(gp):
                    wait_all(gp)
                    inst = gp.dma_start(out=cb[:, :, :], in_=centT.rearrange("(a b) n -> b a n", b=P))
                    inc(inst, "gd")
                sec("gpsimd", ldc)
                t0 = C["t"]; v0 = C["v"]
                def tf(t):
                    wait_all(t)
                    for tc in range(TC):
                        if tc >= 2:
                            t.wait_ge(sems["v"], v0 + tc - 1)
                        for kc in range(DC):
                            inst = t.matmul(ps[tc % 2][:, 0:NE], u_fm[:, kc, tc * P:(tc + 1) * P],
                                            cb[:, kc, :], start=(kc == 0), stop=(kc == DC - 1))
                        inc(inst, "t")
                sec("tensor", tf)
                def vf(v):
                    for tc in range(TC):
                        v.wait_ge(sems["t"], t0 + tc + 1)
                        inst = v.tensor_copy(LG[:, tc, :], ps[tc % 2][:, 0:NE])
                        inc(inst, "v")
                sec("vector", vf)

            def vm(v):
                wait(v, "v")
                for tc in range(TC):
                    inst = v.reduce_max(st3[:, tc:tc + 1], LG[:, tc, :], axis=AX)
                    inc(inst, "v")
                    v.wait_ge(sems["v"], C["v"])
                    inst = v.tensor_scalar(MKb[:, tc, :], LG[:, tc, :], st3[:, tc:tc + 1], None, ALU.is_ge)
                    inc(inst, "v")
                    v.wait_ge(sems["v"], C["v"])
            sec("vector", vm)

            t0 = C["t"]; v0 = C["v"]
            def tcum(t):
                wait_all(t)
                for tc in range(TC):
                    if tc >= 2:
                        t.wait_ge(sems["v"], v0 + 2 * (tc - 1))
                    t.matmul(ps[2 + tc % 2][:, 0:NE], g_triu[:, :], MKb[:, tc, :],
                             start=True, stop=True)
                    inst = t.matmul(ps[4 + tc % 2][0:1, 0:NE], g_onescol[:, :], MKb[:, tc, :],
                                    start=True, stop=True)
                    inc(inst, "t")
            sec("tensor", tcum)
            def vcum(v):
                for tc in range(TC):
                    v.wait_ge(sems["t"], t0 + tc + 1)
                    inst = v.tensor_copy(CUM[:, tc, :], ps[2 + tc % 2][:, 0:NE])
                    inc(inst, "v")
                    inst = v.tensor_copy(rowA[0:1, tc * NE:(tc + 1) * NE], ps[4 + tc % 2][0:1, 0:NE])
                    inc(inst, "v")
            sec("vector", vcum)
            def ldec(sy):
                wait_all(sy)
                inst = sy.dma_start(out=rowB[0:1, 128 + 0:128 + NE], in_=ecapd[0:1, :])
                inc(inst, "qd")
            sec("sync", ldec)
            def vpre(v):
                wait(v, "qd"); wait(v, "v")
                inst = v.memset(rowB[0:1, 0:NE], 0.0)
                inc(inst, "v")
                for tc in range(1, TC):
                    v.wait_ge(sems["v"], C["v"])
                    inst = v.tensor_tensor(rowB[0:1, tc * NE:(tc + 1) * NE],
                                           rowB[0:1, (tc - 1) * NE:tc * NE],
                                           rowA[0:1, (tc - 1) * NE:tc * NE], ALU.add)
                    inc(inst, "v")
                v.wait_ge(sems["v"], C["v"])
                inst = v.tensor_scalar(rowB[0:1, 0:TC * NE], rowB[0:1, 0:TC * NE], -1.0, None, ALU.add)
                inc(inst, "v")
            sec("vector", vpre)
            bcast_row(rowB, BREP, TC * NE)
            def vpre2(v):
                wait(v, "v")
                inst = v.tensor_copy(rowA[0:1, 0:NE], rowB[0:1, 128:128 + NE])
                inc(inst, "v")
            sec("vector", vpre2)
            bcast_row(rowA, ECREP, NE)

            def vslot(v):
                wait(v, "v")
                for tc in range(TC):
                    inst = v.tensor_tensor(mscr[:, 0:NE], CUM[:, tc, :],
                                           BREP[:, tc * NE:(tc + 1) * NE], ALU.add)
                    inc(inst, "v")
                    v.wait_ge(sems["v"], C["v"])
                    inst = v.tensor_scalar(mscr[:, 0:NE], mscr[:, 0:NE], float(CAP - 1), None, ALU.min)
                    inc(inst, "v")
                    v.wait_ge(sems["v"], C["v"])
                    inst = v.tensor_tensor(mscr[:, 0:NE], mscr[:, 0:NE], ECREP[:, :], ALU.add)
                    inc(inst, "v")
                    v.wait_ge(sems["v"], C["v"])
                    inst = v.tensor_tensor(mscr[:, NE:2 * NE], mscr[:, 0:NE], MKb[:, tc, :], ALU.mult)
                    inc(inst, "v")
                    v.wait_ge(sems["v"], C["v"])
                    inst = v.reduce_sum(mscr[:, 2 * NE:2 * NE + 1], mscr[:, NE:2 * NE], axis=AX)
                    inc(inst, "v")
                    v.wait_ge(sems["v"], C["v"])
                    inst = v.tensor_scalar(mscr[:, 2 * NE:2 * NE + 1], mscr[:, 2 * NE:2 * NE + 1],
                                           float(NSLOT - 1), None, ALU.min)
                    inc(inst, "v")
                    v.wait_ge(sems["v"], C["v"])
                    inst = v.tensor_copy(SL32[:, tc:tc + 1], mscr[:, 2 * NE:2 * NE + 1])
                    inc(inst, "v")
                    v.wait_ge(sems["v"], C["v"])
            sec("vector", vslot)

            with sb("zzero", (P, 4, D), BF16) as zzero:
                def vz(v):
                    wait(v, "v")
                    inst = v.memset(zzero[:, :, :], 0.0)
                    inc(inst, "v")
                sec("vector", vz)
                def zfill(sy):
                    wait(sy, "v")
                    for e in range(NE):
                        inst = sy.dma_start(out=zslot_r[e], in_=zzero[:, :, :])
                        inc(inst, "qd")
                sec("sync", zfill)
                def gsc(gp):
                    wait_all(gp)
                    for tc in range(TC):
                        inst = gp.indirect_dma_start(
                            out=zslot_d,
                            out_offset=bass.IndirectOffsetOnAxis(ap=SL32[:, tc:tc + 1], axis=0),
                            in_=z_tm[:, tc, :], in_offset=None)
                        inc(inst, "gd")
                    wait(gp, "gd")
                sec("gpsimd", gsc)

            with (
                sb("W1b", (P, 2, DC, D), BF16) as W1b,
                sb("W2b", (P, 2, DC, D), BF16) as W2b,
                sb("xn", (P, DC, CAP), BF16) as xn,
                sb("hq", (P, DC, CAP), BF16) as hq,
                sb("Y", (P, 4, D), F32) as Y,
                sb("zr", (P, 4, D), BF16) as zr,
                sb("db", (P, 4, D), BF16) as db,
                sb("lgc", (P, DC), F32) as lgc,
                sb("lbc", (P, DC), F32) as lbc,
                sb("b1c", (P, 32), F32) as b1c,
                sb("B2R", (P, D), F32) as B2R,
            ):
                wsem = ["wa", "wb"]
                def pre(gp):
                    wait_all(gp)
                    i1 = gp.dma_start(out=W1b[:, 0, :, :],
                                      in_=W1d[0].rearrange("(a b) f -> b a f", b=P)[:, :, 0:D])
                    inc(i1, "wa")
                    i2 = gp.dma_start(out=W2b[:, 0, :, :],
                                      in_=W2d[0][0:D, :].rearrange("(a b) f -> b a f", b=P))
                    inc(i2, "wa")
                sec("gpsimd", pre)

                for e in range(NE):
                    def lde(sy, e=e):
                        wait_all(sy)
                        i1 = sy.dma_start(out=lgc[:, :], in_=lng8[e].rearrange("(a b) -> b a", b=P),
                                          allow_slow_non_contiguous=True)
                        inc(i1, "qd")
                        i2 = sy.dma_start(out=lbc[:, :], in_=lnb8[e].rearrange("(a b) -> b a", b=P),
                                          allow_slow_non_contiguous=True)
                        inc(i2, "qd")
                        i3 = sy.dma_start(out=b1c[:, :], in_=b1d[e].rearrange("(a b) -> b a", b=P),
                                          allow_slow_non_contiguous=True)
                        inc(i3, "qd")
                        i4 = sy.dma_start(out=rowB[0:1, 0:D], in_=b2d[e:e + 1, :])
                        inc(i4, "qd")
                    sec("sync", lde)
                    bcast_row(rowB, B2R, D)
                    def gxn(sy, e=e):
                        wait_all(sy)
                        for dc in range(DC):
                            inst = sy.dma_start(out=xn[:, dc, :],
                                                in_=zslot_d[e * CAP:(e + 1) * CAP, dc * P:(dc + 1) * P],
                                                transpose=True)
                            inc(inst, "qd")
                    sec("sync", gxn)
                    def vxn(v, e=e):
                        wait(v, "qd")
                        for dc in range(DC):
                            inst = v.tensor_scalar(xn[:, dc, :], xn[:, dc, :], lgc[:, dc:dc + 1],
                                                   lbc[:, dc:dc + 1], ALU.mult, ALU.add)
                            inc(inst, "v")
                            v.wait_ge(sems["v"], C["v"])
                    sec("vector", vxn)

                    for qt in range(4):
                        buf = (e * 4 + qt) % 2
                        wcur = wsem[buf]
                        wn = (e * 4 + qt)
                        if wn + 1 < 32:
                            ne_, nq = divmod(wn + 1, 4)
                            def prefetch(gp, ne_=ne_, nq=nq, buf=buf):
                                wait(gp, "t")
                                i1 = gp.dma_start(out=W1b[:, 1 - buf, :, :],
                                                  in_=W1d[ne_].rearrange("(a b) f -> b a f", b=P)[:, :, nq * D:(nq + 1) * D])
                                inc(i1, wsem[1 - buf])
                                i2 = gp.dma_start(out=W2b[:, 1 - buf, :, :],
                                                  in_=W2d[ne_][nq * D:(nq + 1) * D, :].rearrange("(a b) f -> b a f", b=P))
                                inc(i2, wsem[1 - buf])
                            sec("gpsimd", prefetch)
                        t0 = C["t"]; s0 = C["s"]
                        def th(t, buf=buf, wcur=wcur):
                            wait(t, wcur)
                            wait(t, "v"); wait(t, "s")
                            for fc in range(DC):
                                if fc >= 2:
                                    t.wait_ge(sems["s"], s0 + fc - 1)
                                for kc in range(DC):
                                    inst = t.matmul(ps[fc % 2][:, :], W1b[:, buf, kc, fc * P:(fc + 1) * P],
                                                    xn[:, kc, :], start=(kc == 0), stop=(kc == DC - 1))
                                inc(inst, "t")
                        sec("tensor", th)
                        def sh(sc, qt=qt):
                            for fc in range(DC):
                                sc.wait_ge(sems["t"], t0 + fc + 1)
                                inst = sc.activation(hq[:, fc, :], ps[fc % 2][:, :], AF.Relu,
                                                     bias=b1c[:, qt * 8 + fc:qt * 8 + fc + 1])
                                inc(inst, "s")
                        sec("scalar", sh)
                        t1 = C["t"]; v1 = C["v"]
                        def ty(t, buf=buf):
                            wait(t, "s")
                            j = 0
                            for sc_ in range(4):
                                for h in range(2):
                                    if j >= 2:
                                        t.wait_ge(sems["v"], v1 + j - 1)
                                    for fc in range(DC):
                                        inst = t.matmul(ps[2 + j % 2][:, :], hq[:, fc, sc_ * P:(sc_ + 1) * P],
                                                        W2b[:, buf, fc, h * 512:(h + 1) * 512],
                                                        start=(fc == 0), stop=(fc == DC - 1))
                                    inc(inst, "t")
                                    j += 1
                        sec("tensor", ty)
                        def vy(v, qt=qt):
                            j = 0
                            for sc_ in range(4):
                                for h in range(2):
                                    v.wait_ge(sems["t"], t1 + j + 1)
                                    dstap = Y[:, sc_, h * 512:(h + 1) * 512]
                                    if qt == 0:
                                        inst = v.tensor_tensor(dstap, ps[2 + j % 2][:, :],
                                                               B2R[:, h * 512:(h + 1) * 512], ALU.add)
                                    else:
                                        inst = v.tensor_tensor(dstap, dstap, ps[2 + j % 2][:, :], ALU.add)
                                    inc(inst, "v")
                                    j += 1
                        sec("vector", vy)

                    def ldz(sy, e=e):
                        wait(sy, "v")
                        inst = sy.dma_start(out=zr[:, :, :], in_=zslot_r[e])
                        inc(inst, "qd")
                    sec("sync", ldz)
                    v0 = C["v"]; s0 = C["s"]
                    def vw(v, e=e):
                        wait(v, "qd")
                        for sc_ in range(4):
                            inst = v.tensor_tensor(escr[:, 0, :], Y[:, sc_, :], zr[:, sc_, :], ALU.add)
                            inc(inst, "v")
                            v.wait_ge(sems["v"], C["v"])
                            v.reduce_sum(st3[:, sc_:sc_ + 1], escr[:, 0, :], axis=AX)
                            inst = v.scalar_tensor_tensor(escr[:, 1, :], escr[:, 0, :], 1.0, escr[:, 0, :],
                                                          ALU.mult, ALU.mult, accum_out=st4[:, sc_:sc_ + 1])
                            inc(inst, "v")
                            v.wait_ge(sems["v"], C["v"])
                            v.tensor_scalar(st3[:, sc_:sc_ + 1], st3[:, sc_:sc_ + 1], 1.0 / D, None, ALU.mult)
                            inst = v.tensor_scalar(st4[:, sc_:sc_ + 1], st4[:, sc_:sc_ + 1], 1.0 / D, None, ALU.mult)
                            inc(inst, "v")
                            v.wait_ge(sems["v"], C["v"])
                            inst = v.scalar_tensor_tensor(escr[:, 3, 0:1], st3[:, sc_:sc_ + 1], 1.0,
                                                          st3[:, sc_:sc_ + 1], ALU.mult, ALU.mult)
                            inc(inst, "v")
                            v.wait_ge(sems["v"], C["v"])
                            inst = v.tensor_scalar(st4[:, sc_:sc_ + 1], escr[:, 3, 0:1], -1.0,
                                                   st4[:, sc_:sc_ + 1], ALU.mult, ALU.add)
                            inc(inst, "v")
                            v.wait_ge(sems["v"], C["v"])
                            inst = v.tensor_scalar(st4[:, sc_:sc_ + 1], st4[:, sc_:sc_ + 1], LN_EPS, None, ALU.add)
                            inc(inst, "v")
                            v.wait_ge(sems["s"], s0 + 2 * (sc_ + 1))
                            inst = v.tensor_scalar(escr[:, 3, 0:1], st3[:, sc_:sc_ + 1], -1.0,
                                                   st4[:, sc_:sc_ + 1], ALU.mult, ALU.mult)
                            inc(inst, "v")
                            v.wait_ge(sems["v"], C["v"])
                            inst = v.tensor_scalar(db[:, sc_, :], escr[:, 0, :], st4[:, sc_:sc_ + 1],
                                                   escr[:, 3, 0:1], ALU.mult, ALU.add)
                            inc(inst, "v")
                            v.wait_ge(sems["v"], C["v"])
                    sec("vector", vw)
                    def sw(sc, e=e):
                        for sc_ in range(4):
                            sc.wait_ge(sems["v"], v0 + 8 * sc_ + 6)
                            inst = sc.activation(st4[:, sc_:sc_ + 1], st4[:, sc_:sc_ + 1], AF.Ln, scale=1.0)
                            inc(inst, "s")
                            sc.wait_ge(sems["s"], C["s"])
                            inst = sc.activation(st4[:, sc_:sc_ + 1], st4[:, sc_:sc_ + 1], AF.Exp, scale=-0.5)
                            inc(inst, "s")
                    sec("scalar", sw)
                    def std(sy, e=e):
                        wait(sy, "v")
                        inst = sy.dma_start(out=dslot_r[e], in_=db[:, :, :])
                        inc(inst, "qd")
                    sec("sync", std)

            with sb("DT", (P, TC, D), BF16) as DT:
                def gug(gp):
                    wait_all(gp)
                    for tc in range(TC):
                        inst = gp.indirect_dma_start(
                            out=DT[:, tc, :], out_offset=None, in_=dslot_d,
                            in_offset=bass.IndirectOffsetOnAxis(ap=SL32[:, tc:tc + 1], axis=0))
                        inc(inst, "gd")
                    wait(gp, "gd")
                sec("gpsimd", gug)
                v0 = C["v"]
                qp0 = [C["q0"], C["q1"]]; sp0 = [C["q2"], C["q3"]]
                def dsec(sy):
                    wait_all(sy)
                    for tc in range(TC):
                        if tc >= 2:
                            sy.wait_ge(sems["v"], v0 + (tc - 1))
                        inst = sy.dma_start(out=xstr[:, tc % 2, :], in_=src[:, tc, :])
                        inc(inst, f"q{tc % 2}")
                sec("sync", dsec)
                def vf(v):
                    for tc in range(TC):
                        v.wait_ge(sems[f"q{tc % 2}"], qp0[tc % 2] + 16 * (tc // 2 + 1))
                        if tc >= 2:
                            v.wait_ge(sems[f"q{2 + tc % 2}"], sp0[tc % 2] + 16 * ((tc - 2) // 2 + 1))
                        else:
                            v.wait_ge(sems[f"q{2 + tc % 2}"], sp0[tc % 2])
                        inst = v.tensor_tensor(xou[:, tc % 2, :], xstr[:, tc % 2, :], DT[:, tc, :], ALU.add)
                        inc(inst, "v")
                sec("vector", vf)
                def osec(gp):
                    for tc in range(TC):
                        gp.wait_ge(sems["v"], v0 + tc + 1)
                        inst = gp.dma_start(out=dst[:, tc, :], in_=xou[:, tc % 2, :])
                        inc(inst, f"q{2 + tc % 2}")
                sec("gpsimd", osec)
        return

    def block_l1_attn(src, dst):
        with (
            sb("xn_fm", (P, DC, T), BF16) as xn_fm,
            sb("BN", (P, 16), F32) as BN,
            sb("A8", (P, DC), F32) as A8,
            sb("B8", (P, DC), F32) as B8,
            sb("LAM", (P, 1), F32) as LAM,
            sb("SUB2", (P, 1), F32) as SUB2,
            sb("YN", (P, TC, T), BF16) as YN,
            sb("ascr", (P, 4, 512), F32) as ascr,
            sb("sqb2", (P, 2, 512), BF16) as sqb2,
            
        ):
            with sb("u_tm", (P, TC, D), BF16) as u_tm:
                ln_u_from_dram(src, u_tm)
                transpose_tm_fm(u_tm, xn_fm)

            def vbn(v):
                wait(v, "v")
                for dc in range(DC):
                    v.reduce_sum(BN[:, dc:dc + 1], xn_fm[:, dc, :], axis=AX)
                    inst = v.scalar_tensor_tensor(escr[:, 0, :512], xn_fm[:, dc, 0:512], 1.0,
                                                  xn_fm[:, dc, 0:512], ALU.mult, ALU.mult)
                    inc(inst, "v")
                    v.wait_ge(sems["v"], C["v"])
                for dc in range(DC):
                    inst = v.scalar_tensor_tensor(xou[:, 0, :], xn_fm[:, dc, 0:D], 1.0,
                                                  xn_fm[:, dc, 0:D], ALU.mult, ALU.mult,
                                                  accum_out=st3[:, 0:1])
                    inc(inst, "v")
                    v.wait_ge(sems["v"], C["v"])
                    inst = v.scalar_tensor_tensor(xou[:, 0, :], xn_fm[:, dc, D:2 * D], 1.0,
                                                  xn_fm[:, dc, D:2 * D], ALU.mult, ALU.mult,
                                                  accum_out=st3[:, 1:2])
                    inc(inst, "v")
                    v.wait_ge(sems["v"], C["v"])
                    inst = v.tensor_tensor(BN[:, 8 + dc:9 + dc], st3[:, 0:1], st3[:, 1:2], ALU.add)
                    inc(inst, "v")
                    v.wait_ge(sems["v"], C["v"])
            sec("vector", vbn)
            def bnout(sy):
                wait(sy, "v")
                inst = sy.dma_start(out=bnin_d[:, :], in_=BN[:, :])
                inc(inst, "qd")
            sec("sync", bnout)
            def gcc(gp):
                wait(gp, "qd")
                inst = gp.collective_compute("AllReduce", ALU.add,
                                             replica_groups=[list(range(8))],
                                             ins=[bnin_d.opt() if hasattr(bnin_d, "opt") else bnin_d],
                                             outs=[bnout_d.opt() if hasattr(bnout_d, "opt") else bnout_d])
                inc(inst, "cc")
                wait(gp, "cc")
            sec("gpsimd", gcc)
            def bnin(sy):
                wait(sy, "cc")
                inst = sy.dma_start(out=BN[:, :], in_=bnout_d[:, :])
                inc(inst, "qd")
            sec("sync", bnin)
            with sb("gb", (P, 2 * DC), F32) as gb:
                def ldgb(sy):
                    wait(sy, "qd")
                    i1 = sy.dma_start(out=gb[:, 0:DC], in_=bngd[:, :])
                    inc(i1, "qd")
                    i2 = sy.dma_start(out=gb[:, DC:2 * DC], in_=bnbd[:, :])
                    inc(i2, "qd")
                sec("sync", ldgb)
                def vbn2(v):
                    wait(v, "qd")
                    inst = v.tensor_scalar(BN[:, :], BN[:, :], 1.0 / (8 * T), None, ALU.mult)
                    inc(inst, "v")
                    wait(v, "v")
                    inst = v.scalar_tensor_tensor(st3[:, 0:DC], BN[:, 0:DC], 1.0, BN[:, 0:DC],
                                                  ALU.mult, ALU.mult)
                    inc(inst, "v")
                    wait(v, "v")
                    inst = v.tensor_tensor(BN[:, 8:16], BN[:, 8:16], st3[:, 0:DC], ALU.subtract)
                    inc(inst, "v")
                    wait(v, "v")
                    inst = v.tensor_scalar(BN[:, 8:16], BN[:, 8:16], LN_EPS, None, ALU.add)
                    inc(inst, "v")
                sec("vector", vbn2)
                def sbn(sc):
                    wait(sc, "v")
                    inst = sc.activation(BN[:, 8:16], BN[:, 8:16], AF.Ln, scale=1.0)
                    inc(inst, "s")
                    wait(sc, "s")
                    inst = sc.activation(BN[:, 8:16], BN[:, 8:16], AF.Exp, scale=-0.5)
                    inc(inst, "s")
                sec("scalar", sbn)
                def vbn3(v):
                    wait(v, "s")
                    inst = v.tensor_tensor(A8[:, :], gb[:, 0:DC], BN[:, 8:16], ALU.mult)
                    inc(inst, "v")
                    wait(v, "v")
                    inst = v.tensor_tensor(st3[:, 0:DC], BN[:, 0:DC], A8[:, :], ALU.mult)
                    inc(inst, "v")
                    wait(v, "v")
                    inst = v.tensor_tensor(B8[:, :], gb[:, DC:2 * DC], st3[:, 0:DC], ALU.subtract)
                    inc(inst, "v")
                    wait(v, "v")
                    for dc in range(DC):
                        for h in range(2):
                            inst = v.tensor_scalar(xn_fm[:, dc, h * D:(h + 1) * D],
                                                   xn_fm[:, dc, h * D:(h + 1) * D],
                                                   A8[:, dc:dc + 1], B8[:, dc:dc + 1],
                                                   ALU.mult, ALU.add)
                            inc(inst, "v")
                            v.wait_ge(sems["v"], C["v"])
                sec("vector", vbn3)

            def ldl(sy):
                wait_all(sy)
                i1 = sy.dma_start(out=rowA[0:1, 0:256], in_=lpackd[0:1, :])
                inc(i1, "qd")
                i2 = sy.dma_start(out=st4[:, 15:16], in_=sublnd[:, :])
                inc(i2, "qd")
            sec("sync", ldl)
            def vl(v):
                wait(v, "qd")
                inst = v.tensor_tensor(rowA[0:1, 0:64], rowA[0:1, 0:64], rowA[0:1, 64:128], ALU.mult)
                inc(inst, "v")
                wait(v, "v")
                inst = v.reduce_sum(rowB[0:1, 0:1], rowA[0:1, 0:64], axis=AX)
                inc(inst, "v")
                wait(v, "v")
                inst = v.tensor_tensor(rowA[0:1, 128:192], rowA[0:1, 128:192], rowA[0:1, 192:256], ALU.mult)
                inc(inst, "v")
                wait(v, "v")
                inst = v.reduce_sum(rowB[0:1, 1:2], rowA[0:1, 128:192], axis=AX)
                inc(inst, "v")
            sec("vector", vl)
            def sl(sc):
                wait(sc, "v")
                inst = sc.activation(rowB[0:1, 0:2], rowB[0:1, 0:2], AF.Exp)
                inc(inst, "s")
            sec("scalar", sl)
            def vl2(v):
                wait(v, "s")
                inst = v.tensor_tensor(rowB[0:1, 2:3], rowB[0:1, 0:1], rowB[0:1, 1:2], ALU.subtract)
                inc(inst, "v")
                wait(v, "v")
                inst = v.tensor_scalar(rowB[0:1, 2:3], rowB[0:1, 2:3], LAMBDA_INIT, None, ALU.add)
                inc(inst, "v")
                wait(v, "v"); wait(v, "qd")
                inst = v.tensor_scalar(SUB2[:, :], st4[:, 15:16], 1.0 - LAMBDA_INIT, None, ALU.mult)
                inc(inst, "v")
            sec("vector", vl2)
            bcast_row(rowB[0:1, 2:3].rearrange("a b -> a b"), LAM, 1)

            with (
                sb("VG", (P, TC, 512), BF16) as VG,
                sb("QK", (P, 4, T), BF16) as QK,
                sb("E", (P, TC, 512), BF16) as E,
                sb("wq4", (P, DC, 512), BF16) as wq4,
            ):
                state = {"vz": 0, "tr2": 0, "qk": 0, "first": True, "pend": None, "sexend": {}, "tzmark": {}, "pvprog": 0, "t0": {}}

                def emit_proj_v(g4):
                    load_w(wq4, vTd, 512, col_off=g4 * 512)
                    t0 = C["t"]; v0 = C["v"]
                    def tv(t):
                        wait_all(t)
                        j = 0
                        for tc in range(TC):
                            if j >= 4:
                                t.wait_ge(sems["v"], v0 + j - 3)
                            for kc in range(DC):
                                inst = t.matmul(ps[j % 4][:, :], xn_fm[:, kc, tc * P:(tc + 1) * P],
                                                wq4[:, kc, :], start=(kc == 0), stop=(kc == DC - 1))
                            inc(inst, "t")
                            j += 1
                    sec("tensor", tv)
                    def vv(v):
                        for tc in range(TC):
                            v.wait_ge(sems["t"], t0 + tc + 1)
                            inst = v.tensor_copy(VG[:, tc, :], ps[tc % 4][:, :])
                            inc(inst, "v")
                    sec("vector", vv)
                    state["qk"] = C["v"]

                def emit_proj_qk(g2):
                    for i, wT in enumerate([q1T, q2T, k1T, k2T]):
                        def ldq(gp, wT=wT, i=i, g2=g2):
                            wait_all(gp)
                            inst = gp.dma_start(out=wq4[:, :, i * P:(i + 1) * P],
                                                in_=wT.rearrange("(a b) n -> b a n", b=P)[:, :, g2 * P:(g2 + 1) * P])
                            inc(inst, "gd")
                        sec("gpsimd", ldq)
                    t0 = C["t"]; v0 = C["v"]
                    def tq(t):
                        wait_all(t)
                        j = 0
                        for i in range(4):
                            for tcol in range(4):
                                if j >= 4:
                                    t.wait_ge(sems["v"], v0 + j - 3)
                                for kc in range(DC):
                                    inst = t.matmul(ps[j % 4][:, :], wq4[:, kc, i * P:(i + 1) * P],
                                                    xn_fm[:, kc, tcol * 512:(tcol + 1) * 512],
                                                    start=(kc == 0), stop=(kc == DC - 1))
                                inc(inst, "t")
                                j += 1
                    sec("tensor", tq)
                    def vq(v):
                        j = 0
                        for i in range(4):
                            for tcol in range(4):
                                v.wait_ge(sems["t"], t0 + j + 1)
                                inst = v.tensor_copy(QK[:, i, tcol * 512:(tcol + 1) * 512], ps[j % 4][:, :])
                                inc(inst, "v")
                                j += 1
                    sec("vector", vq)
                    state["qk"] = C["v"]

                def emit_tsc(u):
                    h, qq, st_ = u
                    hb = (h % 2) * 64
                    ql = slice(qq * 512, (qq + 1) * 512)
                    s0 = C["s"]
                    first = state["first"]; state["first"] = False
                    qkm = state["qk"]
                    t0 = C["t"]
                    state["t0"][u] = t0
                    def tsc(t):
                        if first:
                            wait_all(t)
                        else:
                            t.wait_ge(sems["v"], qkm)
                        for scn in range(TC):
                            if (not first) or scn >= 2:
                                t.wait_ge(sems["s"], s0 + scn - 1)
                            inst = t.matmul(ps[scn % 2][:, :],
                                            QK[hb:hb + 64, 2 + st_, scn * P:(scn + 1) * P],
                                            QK[hb:hb + 64, st_, ql],
                                            start=True, stop=True)
                            inc(inst, "t")
                    sec("tensor", tsc)

                def emit_sex(u):
                    t0 = state["t0"][u]
                    pvprog = state["pvprog"]
                    def sex(sc):
                        for scn in range(TC):
                            sc.wait_ge(sems["t"], t0 + scn + 1)
                            if pvprog:
                                sc.wait_ge(sems["t"], pvprog + 16 + scn + 1)
                            inst = sc.activation(E[:, scn, :], ps[scn % 2][:, :], AF.Exp, scale=0.125)
                            inc(inst, "s")
                    sec("scalar", sex)
                    state["sexend"][u] = C["s"]

                def emit_zpv(u):
                    h, qq, st_ = u
                    hv = (h % 4) * P
                    smark = state["sexend"][u]
                    pv = state["vz"]
                    t1 = C["t"]
                    state["pvprog"] = t1
                    def tz(t):
                        t.wait_ge(sems["s"], smark)
                        if pv:
                            t.wait_ge(sems["v"], pv)
                        for scn in range(TC):
                            inst = t.matmul(ps[2][0:1, :], g_onescol[:, :], E[:, scn, :],
                                            start=(scn == 0), stop=(scn == TC - 1))
                            inc(inst, "t")
                        for scn in range(TC):
                            inst = t.matmul(ps[3][:, :], VG[:, scn, hv:hv + P], E[:, scn, :],
                                            start=(scn == 0), stop=(scn == TC - 1))
                            inc(inst, "t")
                    sec("tensor", tz)
                    state["tzmark"][u] = C["t"]
                    par = qq % 2
                    rowX = rowA if par == 0 else rowB
                    def vz(v):
                        wait(v, "v")
                        v.wait_ge(sems["t"], t1 + 16)
                        inst = v.tensor_copy(rowX[0:1, st_ * 512:(st_ + 1) * 512], ps[2][0:1, :])
                        inc(inst, "v")
                        v.wait_ge(sems["t"], t1 + 32)
                        inst = v.tensor_copy(ascr[:, 2 * par + st_, :], ps[3][:, :])
                        inc(inst, "v")
                    sec("vector", vz)
                    state["vz"] = C["v"]

                def emit_tail(h, qq):
                    par = qq % 2
                    rowX = rowA if par == 0 else rowB
                    a0, a1 = 2 * par, 2 * par + 1
                    A2 = escr[:, 0, 0:512]; A3 = escr[:, 0, 512:1024]
                    def vr(v):
                        wait(v, "v")
                        inst = v.reciprocal(rowX[0:1, 0:1024], rowX[0:1, 0:1024])
                        inc(inst, "v")
                    sec("vector", vr)
                    t2 = C["t"]
                    def tb(t):
                        wait(t, "v")
                        inst = t.matmul(ps[4][:, :], g_ones1f[:, :], rowX[0:1, 0:512], start=True, stop=True)
                        inc(inst, "t")
                        inst = t.matmul(ps[5][:, :], g_ones1f[:, :], rowX[0:1, 512:1024], start=True, stop=True)
                        inc(inst, "t")
                    sec("tensor", tb)
                    def vc(v):
                        v.wait_ge(sems["t"], t2 + 2)
                        v.tensor_copy(A2, ps[4][:, :])
                        inst = v.tensor_copy(A3, ps[5][:, :])
                        inc(inst, "v")
                        wait(v, "v")
                        inst = v.tensor_tensor(ascr[:, a0, :], ascr[:, a0, :], A2, ALU.mult)
                        inc(inst, "v")
                        wait(v, "v")
                        inst = v.scalar_tensor_tensor(ascr[:, a1, :], ascr[:, a1, :], LAM[:, 0:1],
                                                      A3, ALU.mult, ALU.mult)
                        inc(inst, "v")
                        wait(v, "v")
                        inst = v.tensor_tensor(ascr[:, a0, :], ascr[:, a0, :], ascr[:, a1, :], ALU.subtract)
                        inc(inst, "v")
                    sec("vector", vc)
                    trp = state["tr2"]
                    def sq1(sc):
                        wait(sc, "v")
                        if trp:
                            sc.wait_ge(sems["t"], trp)
                        inst = sc.activation(sqb2[:, par, :], ascr[:, a0, :], AF.Square)
                        inc(inst, "s")
                    sec("scalar", sq1)
                    t3 = C["t"]
                    def tr2(t):
                        wait(t, "s")
                        inst = t.matmul(ps[4][:, :], g_ones128[:, :], sqb2[:, par, :], start=True, stop=True)
                        inc(inst, "t")
                    sec("tensor", tr2)
                    state["tr2"] = C["t"]
                    def vr2(v):
                        v.wait_ge(sems["t"], t3 + 1)
                        inst = v.tensor_scalar(A2, ps[4][:, :], 1.0 / P, 1e-5, ALU.mult, ALU.add)
                        inc(inst, "v")
                    sec("vector", vr2)
                    def sr2(sc):
                        wait(sc, "v")
                        inst = sc.activation(A2, A2, AF.Ln, scale=1.0)
                        inc(inst, "s")
                        wait(sc, "s")
                        inst = sc.activation(A2, A2, AF.Exp, scale=-0.5)
                        inc(inst, "s")
                    sec("scalar", sr2)
                    def vfin(v, h=h, qq=qq):
                        wait(v, "s")
                        inst = v.tensor_tensor(ascr[:, a0, :], ascr[:, a0, :], A2, ALU.mult)
                        inc(inst, "v")
                        wait(v, "v")
                        inst = v.tensor_scalar(YN[:, h, qq * 512:(qq + 1) * 512], ascr[:, a0, :],
                                               SUB2[:, 0:1], None, ALU.mult)
                        inc(inst, "v")
                    sec("vector", vfin)

                for h in range(TC):
                    if h % 4 == 0:
                        emit_proj_v(h // 4)
                    if h % 2 == 0:
                        emit_proj_qk(h // 2)
                    pend = None
                    tailq = []
                    for qq in range(4):
                        for st_ in range(2):
                            u = (h, qq, st_)
                            if pend is not None:
                                emit_zpv(pend)
                            emit_tsc(u)
                            emit_sex(u)
                            if pend is not None and pend[2] == 1:
                                tailq.append((pend[0], pend[1]))
                                if len(tailq) > 1:
                                    emit_tail(*tailq.pop(0))
                            pend = u
                    emit_zpv(pend)
                    tailq.append((pend[0], pend[1]))
                    for tt_ in tailq:
                        emit_tail(*tt_)

            with sb("cb", (P, TC, D), BF16) as cbuf:
                def ldc(gp):
                    wait_all(gp)
                    inst = gp.dma_start(out=cbuf[:, :, :], in_=cTd.rearrange("(a b) n -> b a n", b=P))
                    inc(inst, "gd")
                sec("gpsimd", ldc)
                proj_tm_epilogue(YN, cbuf, src, dst, extra_brep=None, kc_n=TC)
        return

    def block_l1_ffn(src, dst):
        with (
            sb("u_fm", (P, DC, T), BF16) as u_fm,
            sb("Q", (P, DC, T), BF16) as Q,
            sb("G2", (P, 4, T), BF16) as G2,
            sb("LP", (P, 2, T), F32) as LP,
            sb("LPb", (P, 2, T), BF16) as LPb,
            sb("selb", (P, 18, 256), BF16) as selb,
            sb("wb", (P, DC, D), BF16) as wb,
            sb("wbrep", (P, D), F32) as wbrep,
        ):
            with sb("u_tm", (P, TC, D), BF16) as u_tm:
                ln_u_from_dram(src, u_tm)
                transpose_tm_fm(u_tm, u_fm)
            load_w(wb, WqT2, D)
            proj_fm(u_fm, wb, Q, DC)
            with sb("wr", (P, DC, 256), BF16) as wr:
                def ldr(gp):
                    wait_all(gp)
                    inst = gp.dma_start(out=wr[:, :, :], in_=routersT.rearrange("(a b) n -> b a n", b=P))
                    inc(inst, "gd")
                sec("gpsimd", ldr)
                t0 = C["t"]; s0 = C["s"]
                def tg(t):
                    wait_all(t)
                    j = 0
                    for rc in range(2):
                        for tcol in range(4):
                            if j >= 4:
                                t.wait_ge(sems["s"], s0 + 2 * (j - 4) + 2)
                            for kc in range(DC):
                                inst = t.matmul(ps[j % 4][:, :], wr[:, kc, rc * P:(rc + 1) * P],
                                                Q[:, kc, tcol * 512:(tcol + 1) * 512],
                                                start=(kc == 0), stop=(kc == DC - 1))
                            inc(inst, "t")
                            j += 1
                sec("tensor", tg)
                def sg(sc):
                    j = 0
                    for rc in range(2):
                        for tcol in range(4):
                            sc.wait_ge(sems["t"], t0 + j + 1)
                            sl = slice(tcol * 512, (tcol + 1) * 512)
                            inst = sc.activation(G2[:, rc, sl], ps[j % 4][:, :], AF.Sigmoid)
                            inc(inst, "s")
                            inst = sc.activation(G2[:, rc + 2, sl], ps[j % 4][:, :], AF.Sigmoid, scale=-1.0)
                            inc(inst, "s")
                            j += 1
                sec("scalar", sg)
            def lds(sy):
                wait_all(sy)
                inst = sy.dma_start(out=selb[:, :, :], in_=selpd.rearrange("j b n -> b j n"))
                inc(inst, "qd")
            sec("sync", lds)
            for lv in range(8):
                pieces = SEL_BY_LEVEL[lv]
                t0 = C["t"]; v0 = C["v"]
                def tl(t, pieces=pieces, lv=lv):
                    if lv == 0:
                        wait_all(t)
                    wait(t, "v")
                    j = 0
                    for mc in range(2):
                        for tcol in range(4):
                            if j >= 4:
                                t.wait_ge(sems["v"], v0 + j - 3)
                            for kk, (jj, ch) in enumerate(pieces):
                                inst = t.matmul(ps[j % 4][:, :], selb[:, jj, mc * P:(mc + 1) * P],
                                                G2[:, ch, tcol * 512:(tcol + 1) * 512],
                                                start=(kk == 0), stop=(kk == len(pieces) - 1))
                            inc(inst, "t")
                            j += 1
                sec("tensor", tl)
                def vl_(v, lv=lv):
                    j = 0
                    for mc in range(2):
                        for tcol in range(4):
                            v.wait_ge(sems["t"], t0 + j + 1)
                            dstap = LP[:, mc, tcol * 512:(tcol + 1) * 512]
                            if lv == 0:
                                inst = v.tensor_copy(dstap, ps[j % 4][:, :])
                            else:
                                inst = v.tensor_tensor(dstap, dstap, ps[j % 4][:, :], ALU.mult)
                            inc(inst, "v")
                            j += 1
                sec("vector", vl_)
            def vcast(v):
                wait(v, "v")
                inst = v.tensor_copy(LPb[:, :, :], LP[:, :, :])
                inc(inst, "v")
            sec("vector", vcast)
            with sb("lvb", (P, 2, D), BF16) as lvb:
                def ldlv(gp):
                    wait_all(gp)
                    inst = gp.dma_start(out=lvb[:, :, :], in_=leafVd.rearrange("(a b) n -> b a n", b=P))
                    inc(inst, "gd")
                sec("gpsimd", ldlv)
                proj_fm(LPb, lvb, Q, DC, kc_n=2)
            load_w(wb, WoutT, D)
            load_row_bcast(woutbd[0:1, :], wbrep, D)
            proj_tm_epilogue(Q, wb, src, dst, extra_brep=wbrep, kc_n=DC)
        return

    srcs = [x_in.rearrange("(a b) d -> b a d", b=P), xs_d, xs_d, xs_d]
    dsts = [xs_d, xs_d, xs_d, out_d]
    dsts[n_blocks - 1] = out_d
    blocks = [block_l0_attn, block_moe, block_l1_attn, block_l1_ffn]
    for bi in range(n_blocks):
        blocks[bi](srcs[bi], dsts[bi])

    def fin(gp):
        wait_all(gp)
    sec("gpsimd", fin)

    stack.close()
    return nc


def _prep_inputs(inputs):
    c = _host_consts()
    f32 = np.float32
    def t(a):
        return np.ascontiguousarray(np.asarray(a, f32).T)
    shared = {
        "wqT": t(inputs["L0_attn_wq"]), "wkT": t(inputs["L0_attn_wk"]),
        "wvT": t(inputs["L0_attn_wv"]), "woT": t(inputs["L0_attn_wo"]),
        "bqkvo": np.stack([np.asarray(inputs["L0_attn_b" + k], f32) for k in "qkvo"]),
        "ff": np.asarray(inputs["L0_attn_freq_filter"], f32).reshape(1, D),
        "lng0": np.asarray(inputs["L0_attn_ln_gamma"], f32).reshape(1, D),
        "lnb0": np.asarray(inputs["L0_attn_ln_beta"], f32).reshape(1, D),
        "centT": t(inputs["L0_ffn_centroids"]),
        "lng8": np.asarray(inputs["L0_ffn_ln_g"], f32),
        "lnb8": np.asarray(inputs["L0_ffn_ln_b"], f32),
        "b1d": np.asarray(inputs["L0_ffn_b1"], f32),
        "b2d": np.asarray(inputs["L0_ffn_b2"], f32),
        "W1d": np.asarray(inputs["L0_ffn_W1"], f32),
        "W2d": np.asarray(inputs["L0_ffn_W2"], f32),
        "q1T": t(inputs["L1_attn_q1_w"]), "q2T": t(inputs["L1_attn_q2_w"]),
        "k1T": t(inputs["L1_attn_k1_w"]), "k2T": t(inputs["L1_attn_k2_w"]),
        "vT": t(inputs["L1_attn_v_w"]), "cT": t(inputs["L1_attn_c_w"]),
        "subln": np.asarray(inputs["L1_attn_subln_w"], f32).reshape(P, 1),
        "lpack": np.concatenate([np.asarray(inputs["L1_attn_" + k], f32).ravel()
                                 for k in ("lq1", "lk1", "lq2", "lk2")]).reshape(1, 256),
        "bng": np.ascontiguousarray(np.asarray(inputs["L1_attn_bn_gamma"], f32).reshape(DC, P).T),
        "bnb": np.ascontiguousarray(np.asarray(inputs["L1_attn_bn_beta"], f32).reshape(DC, P).T),
        "WqT2": t(inputs["L1_ffn_Wq"]),
        "routersT": np.ascontiguousarray(np.pad(t(inputs["L1_ffn_routers"]), ((0, 0), (0, 1)))),
        "leafV": np.asarray(inputs["L1_ffn_leaf_values"], f32),
        "WoutT": t(inputs["L1_ffn_Wout_w"]),
        "woutb": np.asarray(inputs["L1_ffn_Wout_b"], f32).reshape(1, D),
        "ident": c["ident"], "ones1": c["ones1"], "ones1f": c["ones1f"], "onescol": c["onescol"],
        "ones128": c["ones128"], "triu": c["triu"],
        "CF": c["CF"], "SF": c["SF"], "CI": c["CI"], "SI": c["SI"],
        "ecap": c["ecap"], "selp": c["selp"],
    }
    x = np.asarray(inputs["L0_attn_x"], f32)
    return shared, x


def kernel(**inputs):
    shared, x = _prep_inputs(inputs)
    nc = build_nc(N_BLOCKS)
    in_maps = [dict(shared, x=np.ascontiguousarray(x[i])) for i in range(8)]
    res = run_bass_kernel_spmd(nc, in_maps, core_ids=list(range(8)))
    outs = []
    for i in range(8):
        o = np.asarray(res.results[i]["out"])
        outs.append(np.ascontiguousarray(o.transpose(1, 0, 2).reshape(T, D)))
    return np.stack(outs, axis=0)


# version 08
# speedup vs baseline: 1.0298x; 1.0298x over previous
import math
import numpy as np
import ml_dtypes
from contextlib import ExitStack
import concourse.bass as bass
import concourse.mybir as mybir
from concourse.bass_utils import run_bass_kernel_spmd

F32 = mybir.dt.float32
BF16 = mybir.dt.bfloat16
I32 = mybir.dt.int32
AF = mybir.ActivationFunctionType
ALU = mybir.AluOpType
AX = mybir.AxisListType.X
BFN = ml_dtypes.bfloat16

P = 128
T = 2048
TC = 16
D = 1024
DC = 8
FP = 1152
FC = 9
NE = 8
CAP = 512
NSLOT = NE * CAP
FFD = 4096
LN_EPS = 1e-5
LAMBDA_INIT = 0.8 - 0.6 * math.exp(-0.3 * (2 - 1))

N_BLOCKS = 4


def _host_consts():
    c = {}
    c["ident"] = np.eye(P, dtype=BFN)
    c["ones1"] = np.ones((1, P), dtype=BFN)
    c["ones1f"] = np.ones((1, P), dtype=np.float32)
    c["onescol"] = np.ones((P, 1), dtype=BFN)
    c["ones128"] = np.ones((P, P), dtype=BFN)
    c["triu"] = np.triu(np.ones((P, P), dtype=np.float32)).astype(BFN)
    s = np.arange(T)[:, None].astype(np.float64)
    f = np.arange(FP)[None, :].astype(np.float64)
    ang = 2.0 * np.pi * (s * f) / T
    live = (f <= 1024).astype(np.float64)
    c["CF"] = (np.cos(ang) * live).astype(BFN)
    c["SF"] = (-np.sin(ang) * live).astype(BFN)
    w = np.where((f == 0) | (f == 1024), 1.0, 2.0) * live / T
    c["CI"] = np.ascontiguousarray((w * np.cos(ang)).T).astype(BFN)
    c["SI"] = np.ascontiguousarray((-w * np.sin(ang)).T).astype(BFN)
    c["ecap"] = (np.arange(NE, dtype=np.float32) * CAP).reshape(1, NE)
    sel = []
    for lv in range(8):
        S_lv = np.zeros((512, 256), dtype=np.float32)
        for m in range(256):
            node = (2 ** lv - 1) + (m >> (8 - lv))
            branch = (m >> (8 - lv - 1)) & 1
            row = node if branch else 256 + node
            S_lv[row, m] = 1.0
        for ch in ([0, 2] if lv < 7 else [0, 1, 2, 3]):
            sel.append(S_lv[ch * 128:(ch + 1) * 128, :])
    c["selp"] = np.stack(sel).astype(BFN)
    return c

SEL_MAP = [(lv, ch) for lv in range(8) for ch in ([0, 2] if lv < 7 else [0, 1, 2, 3])]
SEL_BY_LEVEL = {}
for _j, (_lv, _ch) in enumerate(SEL_MAP):
    SEL_BY_LEVEL.setdefault(_lv, []).append((_j, _ch))


def build_nc(n_blocks=4):
    nc = bass.Bass("TRN2", target_bir_lowering=False, detect_race_conditions=False)

    def din(name, shape, dt=F32):
        return nc.dram_tensor(name, list(shape), dt, kind="ExternalInput").ap()

    x_in = din("x", (T, D))
    wqT = din("wqT", (D, D)); wkT = din("wkT", (D, D)); wvT = din("wvT", (D, D)); woT = din("woT", (D, D))
    bqkvo = din("bqkvo", (4, D))
    ffin = din("ff", (1, D))
    lng0 = din("lng0", (1, D)); lnb0 = din("lnb0", (1, D))
    centT = din("centT", (D, NE))
    lng8 = din("lng8", (NE, D)); lnb8 = din("lnb8", (NE, D))
    b1d = din("b1d", (NE, FFD)); b2d = din("b2d", (NE, D))
    W1d = din("W1d", (NE, D, FFD)); W2d = din("W2d", (NE, FFD, D))
    q1T = din("q1T", (D, D)); q2T = din("q2T", (D, D)); k1T = din("k1T", (D, D)); k2T = din("k2T", (D, D))
    vTd = din("vT", (D, 2 * D)); cTd = din("cT", (2 * D, D))
    sublnd = din("subln", (P, 1))
    lpackd = din("lpack", (1, 256))
    bngd = din("bng", (P, DC)); bnbd = din("bnb", (P, DC))
    WqT2 = din("WqT2", (D, D)); routersT = din("routersT", (D, 256))
    leafVd = din("leafV", (256, D)); WoutT = din("WoutT", (D, D)); woutbd = din("woutb", (1, D))
    ident = din("ident", (P, P), BF16)
    ones1 = din("ones1", (1, P), BF16)
    ones1f = din("ones1f", (1, P), F32)
    onescol = din("onescol", (P, 1), BF16)
    ones128 = din("ones128", (P, P), BF16)
    triu = din("triu", (P, P), BF16)
    CFd = din("CF", (T, FP), BF16); SFd = din("SF", (T, FP), BF16)
    CId = din("CI", (FP, T), BF16); SId = din("SI", (FP, T), BF16)
    ecapd = din("ecap", (1, NE))
    selpd = din("selp", (18, P, 256), BF16)

    out_d = nc.dram_tensor("out", [P, TC, D], F32, kind="ExternalOutput").ap()
    xs_d = nc.dram_tensor("xs", [P, TC, D], F32).ap()
    zslot_d = nc.dram_tensor("zslot", [NSLOT, D], BF16).ap()
    dslot_d = nc.dram_tensor("dslot", [NSLOT, D], BF16).ap()
    bnin_d = nc.dram_tensor("bnin", [P, 16], F32).ap()
    bnout_d = nc.dram_tensor("bnout", [P, 16], F32).ap()

    CFr = CFd.rearrange("(a b) f -> b a f", b=P)
    SFr = SFd.rearrange("(a b) f -> b a f", b=P)
    CIr = CId.rearrange("(a b) s -> b a s", b=P)
    SIr = SId.rearrange("(a b) s -> b a s", b=P)
    zslot_r = zslot_d.rearrange("(e a b) d -> e b a d", e=NE, b=P)
    dslot_r = dslot_d.rearrange("(e a b) d -> e b a d", e=NE, b=P)

    stack = ExitStack()
    sem_names = ["qd", "gd", "v", "s", "t", "g", "cc", "wa", "wb", "dfa", "q0", "q1", "q2", "q3"]
    sems = {n: stack.enter_context(nc.semaphore(n)) for n in sem_names}
    C = {n: 0 for n in sem_names}

    DMA_SEMS = {"qd", "gd", "wa", "wb", "dfa", "q0", "q1", "q2", "q3"}

    def inc(inst, n, k=1):
        if n in DMA_SEMS:
            k = 16
        inst.then_inc(sems[n], k)
        C[n] += k
        return C[n]

    def wait(eng, n, val=None):
        eng.wait_ge(sems[n], C[n] if val is None else val)

    def wait_all(eng):
        for n in ["qd", "gd", "v", "s", "t", "g", "wa", "wb", "dfa", "cc",
                  "q0", "q1", "q2", "q3"]:
            if C[n]:
                eng.wait_ge(sems[n], C[n])

    ps = [stack.enter_context(nc.psum_tensor(f"ps{i}", [P, 512], F32)) for i in range(6)]
    pt = [stack.enter_context(nc.psum_tensor(f"pt{i}", [P, P], BF16)) for i in range(2)]

    block = stack.enter_context(nc.Block())

    def sec(eng, fn):
        getattr(block, eng)(fn)

    _name_ctr = [0]

    def sb(name, shape, dt):
        _name_ctr[0] += 1
        return nc.sbuf_tensor(f"{name}_{_name_ctr[0]}", list(shape), dt)

    g_ident = stack.enter_context(sb("g_ident", (P, P), BF16))
    g_ones1 = stack.enter_context(sb("g_ones1", (1, P), BF16))
    g_ones1f = stack.enter_context(sb("g_ones1f", (1, P), F32))
    g_onescol = stack.enter_context(sb("g_onescol", (P, 1), BF16))
    g_ones128 = stack.enter_context(sb("g_ones128", (P, P), BF16))
    g_triu = stack.enter_context(sb("g_triu", (P, P), BF16))
    xstr = stack.enter_context(sb("xstr", (P, 2, D), F32))
    xou = stack.enter_context(sb("xou", (P, 2, D), F32))
    st1 = stack.enter_context(sb("st1", (P, TC), F32))
    st2 = stack.enter_context(sb("st2", (P, TC), F32))
    st3 = stack.enter_context(sb("st3", (P, TC), F32))
    st4 = stack.enter_context(sb("st4", (P, TC), F32))
    rowA = stack.enter_context(sb("rowA", (1, D), F32))
    rowB = stack.enter_context(sb("rowB", (1, D), F32))
    escr = stack.enter_context(sb("escr", (P, 4, D), F32))

    def ld_consts(sy):
        for dst, src in [(g_ident, ident), (g_ones1, ones1), (g_ones1f, ones1f), (g_onescol, onescol),
                         (g_ones128, ones128), (g_triu, triu)]:
            inst = sy.dma_start(out=dst[:, :], in_=src[:, :])
            inc(inst, "qd")
    sec("sync", ld_consts)


    def bcast_row(src_row_ap, dst_tile_ap, ncols):
        nmm = (ncols + 511) // 512
        t0 = C["t"]; v0 = C["v"]
        def tf(t):
            wait_all(t)
            for j in range(nmm):
                if j >= 2:
                    t.wait_ge(sems["v"], v0 + j - 1)
                cw = min(512, ncols - j * 512)
                inst = t.matmul(ps[4 + (j % 2)][:, 0:cw], g_ones1f[:, :],
                                src_row_ap[0:1, j * 512:j * 512 + cw], start=True, stop=True)
                inc(inst, "t")
        sec("tensor", tf)
        def vf(v):
            for j in range(nmm):
                cw = min(512, ncols - j * 512)
                v.wait_ge(sems["t"], t0 + j + 1)
                inst = v.tensor_copy(dst_tile_ap[:, j * 512:j * 512 + cw], ps[4 + (j % 2)][:, 0:cw])
                inc(inst, "v")
        sec("vector", vf)

    def load_row_bcast(src_dram_row, dst_tile_ap, ncols):
        def ld(sy):
            wait_all(sy)
            inst = sy.dma_start(out=rowB[0:1, 0:ncols], in_=src_dram_row)
            inc(inst, "qd")
        sec("sync", ld)
        bcast_row(rowB, dst_tile_ap, ncols)

    def transpose_tm_fm(src_tm, dst_fm, nt=TC, nd=DC):
        t0 = C["t"]; v0 = C["v"]
        def tf(t):
            wait_all(t)
            j = 0
            for tc in range(nt):
                for dc in range(nd):
                    if j >= 2:
                        t.wait_ge(sems["v"], v0 + j - 1)
                    inst = t.transpose(pt[j % 2][:, :], src_tm[:, tc, dc * P:(dc + 1) * P], g_ident[:, :])
                    inc(inst, "t")
                    j += 1
        sec("tensor", tf)
        def vf(v):
            j = 0
            for tc in range(nt):
                for dc in range(nd):
                    v.wait_ge(sems["t"], t0 + j + 1)
                    inst = v.tensor_copy(dst_fm[:, dc, tc * P:(tc + 1) * P], pt[j % 2][:, :])
                    inc(inst, "v")
                    j += 1
        sec("vector", vf)

    def ln_u_from_dram(xsrc, u_tm):
        qp0 = [C["q0"], C["q1"]]; v0 = C["v"]
        def dsec(sy):
            wait_all(sy)
            for tc in range(TC):
                if tc >= 2:
                    sy.wait_ge(sems["v"], v0 + (tc - 1))
                inst = sy.dma_start(out=xstr[:, tc % 2, :], in_=xsrc[:, tc, :])
                inc(inst, f"q{tc % 2}")
        sec("sync", dsec)
        def vsec(v):
            v.wait_ge(sems["q2"], C["q2"])
            v.wait_ge(sems["q3"], C["q3"])
            for tc in range(TC):
                v.wait_ge(sems[f"q{tc % 2}"], qp0[tc % 2] + 16 * (tc // 2 + 1))
                xc = xstr[:, tc % 2, :]
                v.reduce_sum(st1[:, tc:tc + 1], xc, axis=AX)
                inst = v.scalar_tensor_tensor(xou[:, tc % 2, :], xc, 1.0, xc, ALU.mult, ALU.mult,
                                              accum_out=st2[:, tc:tc + 1])
                inc(inst, "v")
        sec("vector", vsec)
        def vf(v):
            wait(v, "v")
            v.tensor_scalar(st1[:, :], st1[:, :], 1.0 / D, None, ALU.mult)
            inst = v.tensor_scalar(st2[:, :], st2[:, :], 1.0 / D, None, ALU.mult)
            inc(inst, "v")
            wait(v, "v")
            inst = v.scalar_tensor_tensor(st3[:, :], st1[:, :], 1.0, st1[:, :], ALU.mult, ALU.mult)
            inc(inst, "v")
            wait(v, "v")
            inst = v.tensor_tensor(st2[:, :], st2[:, :], st3[:, :], ALU.subtract)
            inc(inst, "v")
            wait(v, "v")
            inst = v.tensor_scalar(st2[:, :], st2[:, :], LN_EPS, None, ALU.add)
            inc(inst, "v")
        sec("vector", vf)
        def sf(sc):
            wait(sc, "v")
            inst = sc.activation(st2[:, :], st2[:, :], AF.Ln, scale=1.0)
            inc(inst, "s")
            wait(sc, "s")
            inst = sc.activation(st2[:, :], st2[:, :], AF.Exp, scale=-0.5)
            inc(inst, "s")
        sec("scalar", sf)
        qp1 = [C["q0"], C["q1"]]; v1 = C["v"]
        def dsec2(sy):
            wait(sy, "s")
            for tc in range(TC):
                if tc >= 2:
                    sy.wait_ge(sems["v"], v1 + 2 * (tc - 1))
                inst = sy.dma_start(out=xstr[:, tc % 2, :], in_=xsrc[:, tc, :])
                inc(inst, f"q{tc % 2}")
        sec("sync", dsec2)
        def vsec2(v):
            for tc in range(TC):
                v.wait_ge(sems[f"q{tc % 2}"], qp1[tc % 2] + 16 * (tc // 2 + 1))
                inst = v.tensor_scalar(st3[:, tc:tc + 1], st1[:, tc:tc + 1], -1.0,
                                       st2[:, tc:tc + 1], ALU.mult, ALU.mult)
                inc(inst, "v")
                v.wait_ge(sems["v"], C["v"])
                inst = v.tensor_scalar(u_tm[:, tc, :], xstr[:, tc % 2, :], st2[:, tc:tc + 1],
                                       st3[:, tc:tc + 1], ALU.mult, ALU.add)
                inc(inst, "v")
        sec("vector", vsec2)

    def load_w(wbuf, wT_dram, ncols, col_off=0, total_cols=None):
        def g(gp):
            wait_all(gp)
            src = wT_dram.rearrange("(a b) n -> b a n", b=P)
            src = src[:, :, col_off:col_off + ncols]
            inst = gp.dma_start(out=wbuf[:, :DC, 0:ncols], in_=src)
            inc(inst, "gd")
        sec("gpsimd", g)

    def proj_tm(u_fm, wbuf, dst_tm, brep=None, ncols=D, kc_n=DC, out_off=0):
        nh = (ncols + 511) // 512
        t0 = C["t"]; v0 = C["v"]
        def tf(t):
            wait_all(t)
            j = 0
            for tc in range(TC):
                for h in range(nh):
                    if j >= 4:
                        t.wait_ge(sems["v"], v0 + j - 3)
                    cw = min(512, ncols - h * 512)
                    for kc in range(kc_n):
                        inst = t.matmul(ps[j % 4][:, 0:cw], u_fm[:, kc, tc * P:(tc + 1) * P],
                                        wbuf[:, kc, h * 512:h * 512 + cw],
                                        start=(kc == 0), stop=(kc == kc_n - 1))
                    inc(inst, "t")
                    j += 1
        sec("tensor", tf)
        def vf(v):
            j = 0
            for tc in range(TC):
                for h in range(nh):
                    v.wait_ge(sems["t"], t0 + j + 1)
                    cw = min(512, ncols - h * 512)
                    o0 = out_off + h * 512
                    dstap = dst_tm[:, tc, o0:o0 + cw]
                    if brep is not None:
                        inst = v.tensor_tensor(dstap, ps[j % 4][:, 0:cw],
                                               brep[:, o0:o0 + cw], ALU.add)
                    else:
                        inst = v.tensor_copy(dstap, ps[j % 4][:, 0:cw])
                    inc(inst, "v")
                    j += 1
        sec("vector", vf)

    def proj_fm(x_fm, wbuf, dst_fm, mc_n, kc_n=DC, wcol_of_mc=lambda mc: None, act=None, bias_col=None):
        t0 = C["t"]
        ekey = "v" if act is None else "s"
        e0 = C[ekey]
        def tf(t):
            wait_all(t)
            j = 0
            for mc in range(mc_n):
                for tcol in range(4):
                    if j >= 4:
                        t.wait_ge(sems[ekey], e0 + j - 3)
                    for kc in range(kc_n):
                        inst = t.matmul(ps[j % 4][:, :], wbuf[:, kc, mc * P:(mc + 1) * P],
                                        x_fm[:, kc, tcol * 512:(tcol + 1) * 512],
                                        start=(kc == 0), stop=(kc == kc_n - 1))
                    inc(inst, "t")
                    j += 1
        sec("tensor", tf)
        def ef(e):
            j = 0
            for mc in range(mc_n):
                for tcol in range(4):
                    e.wait_ge(sems["t"], t0 + j + 1)
                    dstap = dst_fm[:, mc, tcol * 512:(tcol + 1) * 512]
                    if act is None:
                        inst = e.tensor_copy(dstap, ps[j % 4][:, :])
                    else:
                        if bias_col is not None:
                            inst = e.activation(dstap, ps[j % 4][:, :], act, bias=bias_col[:, mc:mc + 1])
                        else:
                            inst = e.activation(dstap, ps[j % 4][:, :], act)
                    inc(inst, ekey)
                    j += 1
        sec("vector" if act is None else "scalar", ef)

    def plain_ln_epilogue(psrc_of_tc, src, dst, extra_brep=None, entry_snap=None):
        v0 = C["v"]; s0 = C["s"]
        qp0 = [C["q0"], C["q1"]]; sp0 = [C["q2"], C["q3"]]
        vmark = []
        KV = 9
        def dsec(sy):
            snapw = entry_snap or dict(C)
            for n, val in snapw.items():
                if n in sems and val:
                    sy.wait_ge(sems[n], val)
            for tc in range(TC):
                if tc >= 2:
                    sy.wait_ge(sems["v"], v0 + KV * (tc - 1))
                inst = sy.dma_start(out=xstr[:, tc % 2, :], in_=src[:, tc, :])
                inc(inst, f"q{tc % 2}")
        sec("sync", dsec)
        def vf(v):
            for tc in range(TC):
                plo, phi = psrc_of_tc(tc)
                v.wait_ge(sems[f"q{tc % 2}"], qp0[tc % 2] + 16 * (tc // 2 + 1))
                v.wait_ge(sems["t"], tmark_ref[0][tc])
                if tc >= 2:
                    v.wait_ge(sems[f"q{2 + tc % 2}"], sp0[tc % 2] + 16 * ((tc - 2) // 2 + 1))
                else:
                    v.wait_ge(sems[f"q{2 + tc % 2}"], sp0[tc % 2])
                if extra_brep is not None:
                    v.tensor_tensor(escr[:, 1, 0:512], plo, extra_brep[:, 0:512], ALU.add)
                    inst = v.tensor_tensor(escr[:, 1, 512:1024], phi, extra_brep[:, 512:1024], ALU.add)
                else:
                    v.tensor_copy(escr[:, 1, 0:512], plo)
                    inst = v.tensor_copy(escr[:, 1, 512:1024], phi)
                inc(inst, "v")
                v.wait_ge(sems["v"], C["v"])
                v.reduce_sum(st3[:, tc:tc + 1], escr[:, 1, :], axis=AX)
                inst = v.scalar_tensor_tensor(escr[:, 2, :], escr[:, 1, :], 1.0, escr[:, 1, :],
                                              ALU.mult, ALU.mult, accum_out=st4[:, tc:tc + 1])
                inc(inst, "v")
                v.wait_ge(sems["v"], C["v"])
                v.tensor_scalar(st3[:, tc:tc + 1], st3[:, tc:tc + 1], 1.0 / D, None, ALU.mult)
                inst = v.tensor_scalar(st4[:, tc:tc + 1], st4[:, tc:tc + 1], 1.0 / D, None, ALU.mult)
                inc(inst, "v")
                v.wait_ge(sems["v"], C["v"])
                inst = v.scalar_tensor_tensor(escr[:, 3, 0:1], st3[:, tc:tc + 1], 1.0,
                                              st3[:, tc:tc + 1], ALU.mult, ALU.mult)
                inc(inst, "v")
                v.wait_ge(sems["v"], C["v"])
                inst = v.tensor_scalar(st4[:, tc:tc + 1], escr[:, 3, 0:1], -1.0, st4[:, tc:tc + 1],
                                       ALU.mult, ALU.add)
                inc(inst, "v")
                v.wait_ge(sems["v"], C["v"])
                inst = v.tensor_scalar(st4[:, tc:tc + 1], st4[:, tc:tc + 1], LN_EPS, None, ALU.add)
                inc(inst, "v")
                v.wait_ge(sems["s"], s0 + 2 * (tc + 1))
                inst = v.tensor_scalar(escr[:, 3, 0:1], st3[:, tc:tc + 1], -1.0, st4[:, tc:tc + 1],
                                       ALU.mult, ALU.mult)
                inc(inst, "v")
                v.wait_ge(sems["v"], C["v"])
                inst = v.tensor_scalar(escr[:, 0, :], escr[:, 1, :], st4[:, tc:tc + 1],
                                       escr[:, 3, 0:1], ALU.mult, ALU.add)
                inc(inst, "v")
                v.wait_ge(sems["v"], C["v"])
                inst = v.tensor_tensor(xou[:, tc % 2, :], xstr[:, tc % 2, :], escr[:, 0, :], ALU.add)
                inc(inst, "v")
                vmark.append(v0 + KV * tc + 1)
        sec("vector", vf)
        def sf(sc):
            for tc in range(TC):
                sc.wait_ge(sems["v"], v0 + KV * tc + 6)
                inst = sc.activation(st4[:, tc:tc + 1], st4[:, tc:tc + 1], AF.Ln, scale=1.0)
                inc(inst, "s")
                sc.wait_ge(sems["s"], C["s"])
                inst = sc.activation(st4[:, tc:tc + 1], st4[:, tc:tc + 1], AF.Exp, scale=-0.5)
                inc(inst, "s")
        sec("scalar", sf)
        def osec(gp):
            for tc in range(TC):
                gp.wait_ge(sems["v"], v0 + KV * (tc + 1))
                inst = gp.dma_start(out=dst[:, tc, :], in_=xou[:, tc % 2, :])
                inc(inst, f"q{2 + tc % 2}")
        sec("gpsimd", osec)
        return v0, KV

    tmark_ref = [{}]
    qd_out_ref = [0]

    def proj_tm_epilogue(in_fm, wbuf, src, dst, extra_brep, kc_n=DC):
        entry_snap = dict(C)
        t0 = C["t"]
        tmark = {}
        v0_expected = C["v"]
        KV = 9
        def tf(t):
            wait_all(t)
            j = 0
            for tc in range(TC):
                for h in range(2):
                    if j >= 4:
                        t.wait_ge(sems["v"], v0_expected + KV * (tc - 2) + 1)
                    for kc in range(kc_n):
                        inst = t.matmul(ps[j % 4][:, :], in_fm[:, kc, tc * P:(tc + 1) * P],
                                        wbuf[:, kc, h * 512:(h + 1) * 512],
                                        start=(kc == 0), stop=(kc == kc_n - 1))
                    inc(inst, "t")
                    j += 1
                tmark[tc] = C["t"]
        sec("tensor", tf)
        tmark_ref[0] = tmark
        def psrc(tc):
            return ps[(2 * tc) % 4][:, :], ps[(2 * tc + 1) % 4][:, :]
        plain_ln_epilogue(psrc, src, dst, extra_brep=extra_brep, entry_snap=entry_snap)

    def block_l0_attn(src, dst):
        with (
            sb("QM", (P, FC, D), BF16) as QM,
            sb("Vc", (P, FC, D), BF16) as Vc,
            sb("Vs", (P, FC, D), BF16) as Vs,
            sb("scrA", (P, 2 * D), F32) as scrA,
            sb("scrB", (P, 2 * D), F32) as scrB,
            sb("repF", (P, D), F32) as repF,
        ):
            scrS = escr[:, 0:2, :]
            def fwd_dft(src_tm, kind, out_c, out_s=None):
                with sb("dfc", (P, TC, P), BF16) as dfc, sb("dfs", (P, TC, P), BF16) as dfs:
                    df0 = C["dfa"]; t0 = C["t"]; s0 = C["s"]; v0 = C["v"]
                    def dload(sy):
                        wait_all(sy)
                        for fc in range(FC):
                            if fc >= 1:
                                sy.wait_ge(sems["t"], t0 + fc)
                            i1 = sy.dma_start(out=dfc[:, :, :], in_=CFr[:, :, fc * P:(fc + 1) * P])
                            inc(i1, "dfa")
                            i2 = sy.dma_start(out=dfs[:, :, :], in_=SFr[:, :, fc * P:(fc + 1) * P])
                            inc(i2, "dfa")
                    sec("sync", dload)
                    def tf(t):
                        wait(t, "v"); wait(t, "s")
                        for fc in range(FC):
                            t.wait_ge(sems["dfa"], df0 + 32 * (fc + 1))
                            if fc >= 1:
                                if kind == "mag":
                                    t.wait_ge(sems["s"], s0 + 2 * fc - 1)
                                else:
                                    t.wait_ge(sems["v"], v0 + fc)
                            for h in range(2):
                                for i in range(TC):
                                    t.matmul(ps[2 * h][:, :], dfc[:, i, :],
                                             src_tm[:, i, h * 512:(h + 1) * 512],
                                             start=(i == 0), stop=(i == TC - 1))
                                for i in range(TC):
                                    inst = t.matmul(ps[2 * h + 1][:, :], dfs[:, i, :],
                                                    src_tm[:, i, h * 512:(h + 1) * 512],
                                                    start=(i == 0), stop=(i == TC - 1))
                            inc(inst, "t")
                    sec("tensor", tf)
                    if kind == "copy":
                        def vf(v):
                            for fc in range(FC):
                                v.wait_ge(sems["t"], t0 + fc + 1)
                                v.tensor_copy(out_c[:, fc, 0:512], ps[0][:, :])
                                v.tensor_copy(out_s[:, fc, 0:512], ps[1][:, :])
                                v.tensor_copy(out_c[:, fc, 512:1024], ps[2][:, :])
                                inst = v.tensor_copy(out_s[:, fc, 512:1024], ps[3][:, :])
                                inc(inst, "v")
                        sec("vector", vf)
                    else:
                        def sf(sc):
                            for fc in range(FC):
                                p_ = fc % 2
                                sc.wait_ge(sems["t"], t0 + fc + 1)
                                if fc >= 2:
                                    sc.wait_ge(sems["v"], v0 + fc - 1)
                                sc.activation(scrA[:, p_ * D:p_ * D + 512], ps[0][:, :], AF.Square)
                                sc.activation(scrB[:, p_ * D:p_ * D + 512], ps[1][:, :], AF.Square)
                                sc.activation(scrA[:, p_ * D + 512:(p_ + 1) * D], ps[2][:, :], AF.Square)
                                inst = sc.activation(scrB[:, p_ * D + 512:(p_ + 1) * D], ps[3][:, :], AF.Square)
                                inc(inst, "s")
                                sc.wait_ge(sems["v"], v0 + fc + 1)
                                inst = sc.activation(out_c[:, fc, :], escr[:, p_, :], AF.Ln, scale=1.0)
                                inc(inst, "s")
                        sec("scalar", sf)
                        def vf(v):
                            for fc in range(FC):
                                p_ = fc % 2
                                v.wait_ge(sems["s"], s0 + 2 * fc + 1)
                                inst = v.scalar_tensor_tensor(escr[:, p_, :], scrA[:, p_ * D:(p_ + 1) * D],
                                                              1e-30, scrB[:, p_ * D:(p_ + 1) * D],
                                                              ALU.add, ALU.add)
                                inc(inst, "v")
                        sec("vector", vf)

            with sb("u_fm", (P, DC, T), BF16) as u_fm:
                with sb("u_tm", (P, TC, D), BF16) as u_tm:
                    ln_u_from_dram(src, u_tm)
                    transpose_tm_fm(u_tm, u_fm)
                def ldff(sy):
                    wait_all(sy)
                    inst = sy.dma_start(out=rowA[0:1, 0:D], in_=ffin[0:1, :])
                    inc(inst, "qd")
                sec("sync", ldff)
                def sqff(v):
                    wait(v, "qd")
                    inst = v.tensor_tensor(rowA[0:1, 0:D], rowA[0:1, 0:D], rowA[0:1, 0:D], ALU.mult)
                    inc(inst, "v")
                sec("vector", sqff)
                bcast_row(rowA, repF, D)

                with (
                    sb("KM", (P, FC, D), BF16) as KM,
                    sb("tmb", (P, TC, D), BF16) as tmb,
                ):
                    brep = xou[:, 0, :]
                    for wT, brow, oc in [(wqT, 0, QM), (wkT, 1, KM)]:
                        load_row_bcast(bqkvo[brow:brow + 1, :], brep, D)
                        with sb(f"wbuf{brow}", (P, DC, 512), BF16) as wbuf:
                            for hh in range(2):
                                load_w(wbuf, wT, 512, col_off=hh * 512)
                                proj_tm(u_fm, wbuf, tmb, brep=brep, ncols=512, out_off=hh * 512)
                        fwd_dft(tmb, "mag", oc)
                    for fc in range(FC):
                        p_ = fc % 2
                        def v1(v, fc=fc, p_=p_):
                            if fc == 0:
                                wait_all(v)
                            wait(v, "s")
                            inst = v.tensor_tensor(escr[:, p_, :], QM[:, fc, :], KM[:, fc, :], ALU.add)
                            inc(inst, "v")
                        sec("vector", v1)
                        def s1(sc, fc=fc, p_=p_):
                            wait(sc, "v")
                            inst = sc.activation(scrA[:, p_ * D:(p_ + 1) * D], escr[:, p_, :], AF.Exp, scale=0.5)
                            inc(inst, "s")
                        sec("scalar", s1)
                        def v2(v, fc=fc, p_=p_):
                            wait(v, "s")
                            inst = v.tensor_tensor(QM[:, fc, :], scrA[:, p_ * D:(p_ + 1) * D], repF[:, :], ALU.mult)
                            inc(inst, "v")
                        sec("vector", v2)
                with sb("tmbv", (P, TC, D), BF16) as tmbv:
                    load_row_bcast(bqkvo[2:3, :], repF, D)
                    with sb("wbufv", (P, DC, 512), BF16) as wbufv:
                        for hh in range(2):
                            load_w(wbufv, wvT, 512, col_off=hh * 512)
                            proj_tm(u_fm, wbufv, tmbv, brep=repF, ncols=512, out_off=hh * 512)
                    fwd_dft(tmbv, "copy", Vc, Vs)

            with (
                sb("attn", (P, TC, D), BF16) as attn,
            ):
                Oc, Os = Vc, Vs
                with sb("ci1", (P, FC, P), BF16) as ci1:
                    df0 = C["dfa"]
                    t0 = C["t"]; v0 = C["v"]; s0 = C["s"]
                    tmk = {scn: (t0 + 1 if scn == 0 else t0 + 3 + scn) for scn in range(TC)}
                    vfree = {scn: v0 + 2 + scn + 1 for scn in range(TC)}
                    def dload(sy):
                        wait_all(sy)
                        for scn in range(TC):
                            if scn >= 1:
                                sy.wait_ge(sems["t"], tmk[scn - 1])
                            inst = sy.dma_start(out=ci1[:, :, :],
                                                in_=CIr[:, :, scn * P:(scn + 1) * P])
                            inc(inst, "dfa")
                    sec("sync", dload)
                    def tf(t):
                        wait(t, "v"); wait(t, "s")
                        for scn in range(TC):
                            t.wait_ge(sems["dfa"], df0 + 16 * (scn + 1))
                            if scn == 1:
                                t.wait_ge(sems["v"], v0 + 1)
                            if scn >= 2:
                                t.wait_ge(sems["v"], vfree[scn - 2])
                            pb = 2 * (scn % 2)
                            for h in range(2):
                                for fc in range(FC):
                                    inst = t.matmul(ps[pb + h][:, :], ci1[:, fc, :],
                                                    QM[:, fc, h * 512:(h + 1) * 512],
                                                    start=(fc == 0), stop=(fc == FC - 1))
                            inc(inst, "t")
                            assert C["t"] == tmk[scn], (scn, C["t"], tmk[scn])
                            if scn == 0:
                                t.wait_ge(sems["v"], v0 + 1)
                                for j in range(2):
                                    inst = t.matmul(ps[4 + j][:, :], g_ones1f[:, :],
                                                    rowA[0:1, j * 512:(j + 1) * 512],
                                                    start=True, stop=True)
                                    inc(inst, "t")
                    sec("tensor", tf)
                    def vf(v):
                        v.wait_ge(sems["t"], t0 + 1)
                        v.tensor_copy(rowA[0:1, 0:512], ps[0][0:1, :])
                        inst = v.tensor_copy(rowA[0:1, 512:1024], ps[1][0:1, :])
                        inc(inst, "v")
                        v.wait_ge(sems["t"], t0 + 3)
                        v.tensor_copy(repF[:, 0:512], ps[4][:, :])
                        inst = v.tensor_copy(repF[:, 512:1024], ps[5][:, :])
                        inc(inst, "v")
                        v.wait_ge(sems["v"], v0 + 2)
                        for scn in range(TC):
                            v.wait_ge(sems["t"], tmk[scn])
                            if scn >= 2:
                                v.wait_ge(sems["s"], s0 + (scn - 1))
                            pb = 2 * (scn % 2)
                            p_ = scn % 2
                            v.tensor_tensor(scrA[:, p_ * D:p_ * D + 512], ps[pb][:, :],
                                            repF[:, 0:512], ALU.subtract)
                            inst = v.tensor_tensor(scrA[:, p_ * D + 512:(p_ + 1) * D], ps[pb + 1][:, :],
                                                   repF[:, 512:1024], ALU.subtract)
                            inc(inst, "v")
                    sec("vector", vf)
                    def sf(sc):
                        for scn in range(TC):
                            sc.wait_ge(sems["v"], v0 + 2 + scn + 1)
                            p_ = scn % 2
                            sc.activation(attn[:, scn, 0:512], scrA[:, p_ * D:p_ * D + 512], AF.Exp)
                            inst = sc.activation(attn[:, scn, 512:1024],
                                                 scrA[:, p_ * D + 512:(p_ + 1) * D], AF.Exp)
                            inc(inst, "s")
                    sec("scalar", sf)

                t0 = C["t"]
                def ztf(t):
                    wait_all(t)
                    for h in range(2):
                        for i in range(TC):
                            inst = t.matmul(ps[4 + h][0:1, :], g_onescol[:, :],
                                            attn[:, i, h * 512:(h + 1) * 512],
                                            start=(i == 0), stop=(i == TC - 1))
                        inc(inst, "t")
                sec("tensor", ztf)
                def zvf(v):
                    v.wait_ge(sems["t"], t0 + 2)
                    v.tensor_copy(rowA[0:1, 0:512], ps[4][0:1, :])
                    inst = v.tensor_copy(rowA[0:1, 512:1024], ps[5][0:1, :])
                    inc(inst, "v")
                    wait(v, "v")
                    inst = v.reciprocal(rowA[0:1, 0:D], rowA[0:1, 0:D])
                    inc(inst, "v")
                sec("vector", zvf)
                bcast_row(rowA, repF, D)
                def vapply(v):
                    wait(v, "v")
                    for fc in range(FC):
                        v.tensor_tensor(Vc[:, fc, :], Vc[:, fc, :], repF[:, :], ALU.mult)
                        inst = v.tensor_tensor(Vs[:, fc, :], Vs[:, fc, :], repF[:, :], ALU.mult)
                        inc(inst, "v")
                        v.wait_ge(sems["v"], C["v"])
                sec("vector", vapply)

                with sb("dfc", (P, TC, P), BF16) as dfc, sb("dfs", (P, TC, P), BF16) as dfs:
                    df0 = C["dfa"]; t0 = C["t"]; v0 = C["v"]
                    def dload(sy):
                        wait_all(sy)
                        for fc in range(FC):
                            if fc >= 1:
                                sy.wait_ge(sems["t"], t0 + fc)
                            i1 = sy.dma_start(out=dfc[:, :, :], in_=CFr[:, :, fc * P:(fc + 1) * P])
                            inc(i1, "dfa")
                            i2 = sy.dma_start(out=dfs[:, :, :], in_=SFr[:, :, fc * P:(fc + 1) * P])
                            inc(i2, "dfa")
                    sec("sync", dload)
                    def tf(t):
                        wait(t, "v"); wait(t, "s")
                        for fc in range(FC):
                            t.wait_ge(sems["dfa"], df0 + 32 * (fc + 1))
                            if fc >= 1:
                                t.wait_ge(sems["v"], v0 + 8 * fc)
                            for h in range(2):
                                for i in range(TC):
                                    t.matmul(ps[2 * h][:, :], dfc[:, i, :],
                                             attn[:, i, h * 512:(h + 1) * 512],
                                             start=(i == 0), stop=(i == TC - 1))
                                for i in range(TC):
                                    inst = t.matmul(ps[2 * h + 1][:, :], dfs[:, i, :],
                                                    attn[:, i, h * 512:(h + 1) * 512],
                                                    start=(i == 0), stop=(i == TC - 1))
                            inc(inst, "t")
                    sec("tensor", tf)
                    def vf(v):
                        for fc in range(FC):
                            v.wait_ge(sems["t"], t0 + fc + 1)
                            for h in range(2):
                                sl = slice(h * 512, (h + 1) * 512)
                                pc, psn = ps[2 * h], ps[2 * h + 1]
                                v.scalar_tensor_tensor(scrA[:, 0:512], pc[:, :], 1.0, Vc[:, fc, sl], ALU.mult, ALU.mult)
                                v.scalar_tensor_tensor(scrA[:, 512:1024], psn[:, :], 1.0, Vs[:, fc, sl], ALU.mult, ALU.mult)
                                v.scalar_tensor_tensor(scrB[:, 0:512], pc[:, :], 1.0, Vs[:, fc, sl], ALU.mult, ALU.mult)
                                inst = v.scalar_tensor_tensor(scrB[:, 512:1024], psn[:, :], 1.0, Vc[:, fc, sl], ALU.mult, ALU.mult)
                                inc(inst, "v")
                                v.wait_ge(sems["v"], C["v"])
                                v.tensor_tensor(Oc[:, fc, sl], scrA[:, 0:512], scrA[:, 512:1024], ALU.subtract)
                                inst = v.tensor_tensor(Os[:, fc, sl], scrB[:, 0:512], scrB[:, 512:1024], ALU.add)
                                inc(inst, "v", 3)
                                v.wait_ge(sems["v"], C["v"])
                    sec("vector", vf)

                with (
                    sb("out_fm", (P, DC, T), BF16) as out_fm,
                    sb("ci2", (P, FC, 512), BF16) as ci2,
                    sb("si2", (P, FC, 512), BF16) as si2,
                ):
                    df0 = C["dfa"]; t0 = C["t"]; v0 = C["v"]
                    def dload(sy):
                        wait_all(sy)
                        for scol in range(4):
                            if scol >= 1:
                                sy.wait_ge(sems["t"], t0 + 8 * scol)
                            i1 = sy.dma_start(out=ci2[:, :, :],
                                              in_=CIr[:, :, scol * 512:(scol + 1) * 512])
                            inc(i1, "dfa")
                            i2 = sy.dma_start(out=si2[:, :, :],
                                              in_=SIr[:, :, scol * 512:(scol + 1) * 512])
                            inc(i2, "dfa")
                    sec("sync", dload)
                    def tf(t):
                        wait(t, "v"); wait(t, "s")
                        j = 0
                        for scol in range(4):
                            t.wait_ge(sems["dfa"], df0 + 32 * (scol + 1))
                            for cc in range(DC):
                                if j >= 4:
                                    t.wait_ge(sems["v"], v0 + j - 3)
                                for fc in range(FC):
                                    t.matmul(ps[j % 4][:, :], Oc[:, fc, cc * P:(cc + 1) * P],
                                             ci2[:, fc, :], start=(fc == 0), stop=False)
                                for fc in range(FC):
                                    inst = t.matmul(ps[j % 4][:, :], Os[:, fc, cc * P:(cc + 1) * P],
                                                    si2[:, fc, :], start=False, stop=(fc == FC - 1))
                                inc(inst, "t")
                                j += 1
                    sec("tensor", tf)
                    def vf(v):
                        j = 0
                        for scol in range(4):
                            for cc in range(DC):
                                v.wait_ge(sems["t"], t0 + j + 1)
                                inst = v.tensor_copy(out_fm[:, cc, scol * 512:(scol + 1) * 512],
                                                     ps[j % 4][:, :])
                                inc(inst, "v")
                                j += 1
                    sec("vector", vf)

                    if True:
                        wbuf2 = attn[:, 0:DC, :]
                        borep = scrB[:, 0:D]
                        grep = scrA[:, 0:D]
                        brep2 = scrA[:, D:2 * D]
                        load_row_bcast(bqkvo[3:4, :], borep, D)
                        load_row_bcast(lng0[0:1, :], grep, D)
                        load_row_bcast(lnb0[0:1, :], brep2, D)
                        load_w(wbuf2, woT, D)
                        affine_ln_epilogue(out_fm, wbuf2, src, dst, borep, grep, brep2)
        return

    def affine_ln_epilogue(in_fm, wbuf, src, dst, borep, grep, brep2):
        entry_snap = dict(C)
        t0 = C["t"]; tmark = {}
        v0 = C["v"]; s0 = C["s"]
        qp0 = [C["q0"], C["q1"]]; sp0 = [C["q2"], C["q3"]]
        KV = 21
        def tf(t):
            wait_all(t)
            j = 0
            for tc in range(TC):
                for h in range(2):
                    if tc >= 2 and h == 0:
                        t.wait_ge(sems["v"], v0 + KV * (tc - 2) + 4)
                    for kc in range(DC):
                        inst = t.matmul(ps[j % 4][:, :], in_fm[:, kc, tc * P:(tc + 1) * P],
                                        wbuf[:, kc, h * 512:(h + 1) * 512],
                                        start=(kc == 0), stop=(kc == DC - 1))
                    inc(inst, "t")
                    j += 1
                tmark[tc] = C["t"]
        sec("tensor", tf)
        def dsec(sy):
            for n, val in entry_snap.items():
                if n in sems and val:
                    sy.wait_ge(sems[n], val)
            for tc in range(TC):
                if tc >= 2:
                    sy.wait_ge(sems["v"], v0 + KV * (tc - 1))
                inst = sy.dma_start(out=xstr[:, tc % 2, :], in_=src[:, tc, :])
                inc(inst, f"q{tc % 2}")
        sec("sync", dsec)
        def vf(v):
            for tc in range(TC):
                v.wait_ge(sems[f"q{tc % 2}"], qp0[tc % 2] + 16 * (tc // 2 + 1))
                v.wait_ge(sems["t"], tmark[tc])
                if tc >= 2:
                    v.wait_ge(sems[f"q{2 + tc % 2}"], sp0[tc % 2] + 16 * ((tc - 2) // 2 + 1))
                else:
                    v.wait_ge(sems[f"q{2 + tc % 2}"], sp0[tc % 2])
                xc = xstr[:, tc % 2, :]
                inst = v.tensor_scalar(st3[:, tc:tc + 1], st1[:, tc:tc + 1], -1.0,
                                       st2[:, tc:tc + 1], ALU.mult, ALU.mult)
                inc(inst, "v")
                v.wait_ge(sems["v"], C["v"])
                inst = v.tensor_scalar(escr[:, 0, :], xc, st2[:, tc:tc + 1],
                                       st3[:, tc:tc + 1], ALU.mult, ALU.add)
                inc(inst, "v")
                v.wait_ge(sems["v"], C["v"])
                v.tensor_tensor(escr[:, 1, 0:512], ps[(2 * tc) % 4][:, :], borep[:, 0:512], ALU.add)
                inst = v.tensor_tensor(escr[:, 1, 512:1024], ps[(2 * tc + 1) % 4][:, :],
                                       borep[:, 512:1024], ALU.add)
                inc(inst, "v")
                v.wait_ge(sems["v"], C["v"])
                inst = v.tensor_tensor(escr[:, 1, :], escr[:, 1, :], escr[:, 0, :], ALU.add)
                inc(inst, "v")
                v.wait_ge(sems["v"], C["v"])
                v.reduce_sum(st3[:, tc:tc + 1], escr[:, 1, :], axis=AX)
                inst = v.scalar_tensor_tensor(escr[:, 2, :], escr[:, 1, :], 1.0, escr[:, 1, :],
                                              ALU.mult, ALU.mult, accum_out=st4[:, tc:tc + 1])
                inc(inst, "v")
                v.wait_ge(sems["v"], C["v"])
                v.tensor_scalar(st3[:, tc:tc + 1], st3[:, tc:tc + 1], 1.0 / D, None, ALU.mult)
                inst = v.tensor_scalar(st4[:, tc:tc + 1], st4[:, tc:tc + 1], 1.0 / D, None, ALU.mult)
                inc(inst, "v")
                v.wait_ge(sems["v"], C["v"])
                inst = v.scalar_tensor_tensor(escr[:, 3, 0:1], st3[:, tc:tc + 1], 1.0,
                                              st3[:, tc:tc + 1], ALU.mult, ALU.mult)
                inc(inst, "v")
                v.wait_ge(sems["v"], C["v"])
                inst = v.tensor_scalar(st4[:, tc:tc + 1], escr[:, 3, 0:1], -1.0, st4[:, tc:tc + 1],
                                       ALU.mult, ALU.add)
                inc(inst, "v")
                v.wait_ge(sems["v"], C["v"])
                inst = v.tensor_scalar(st4[:, tc:tc + 1], st4[:, tc:tc + 1], LN_EPS, None, ALU.add)
                inc(inst, "v")
                v.wait_ge(sems["s"], s0 + 4 * tc + 2)
                inst = v.tensor_scalar(escr[:, 3, 0:1], st3[:, tc:tc + 1], -1.0, st4[:, tc:tc + 1],
                                       ALU.mult, ALU.mult)
                inc(inst, "v")
                v.wait_ge(sems["v"], C["v"])
                inst = v.tensor_scalar(escr[:, 2, :], escr[:, 1, :], st4[:, tc:tc + 1],
                                       escr[:, 3, 0:1], ALU.mult, ALU.add)
                inc(inst, "v")
                v.wait_ge(sems["v"], C["v"])
                inst = v.tensor_tensor(escr[:, 2, :], escr[:, 2, :], grep[:, :], ALU.mult)
                inc(inst, "v")
                v.wait_ge(sems["v"], C["v"])
                inst = v.tensor_tensor(escr[:, 2, :], escr[:, 2, :], brep2[:, :], ALU.add)
                inc(inst, "v")
                v.wait_ge(sems["v"], C["v"])
                v.reduce_sum(st3[:, tc:tc + 1], escr[:, 2, :], axis=AX)
                inst = v.scalar_tensor_tensor(escr[:, 1, :], escr[:, 2, :], 1.0, escr[:, 2, :],
                                              ALU.mult, ALU.mult, accum_out=st4[:, tc:tc + 1])
                inc(inst, "v")
                v.wait_ge(sems["v"], C["v"])
                v.tensor_scalar(st3[:, tc:tc + 1], st3[:, tc:tc + 1], 1.0 / D, None, ALU.mult)
                inst = v.tensor_scalar(st4[:, tc:tc + 1], st4[:, tc:tc + 1], 1.0 / D, None, ALU.mult)
                inc(inst, "v")
                v.wait_ge(sems["v"], C["v"])
                inst = v.scalar_tensor_tensor(escr[:, 3, 0:1], st3[:, tc:tc + 1], 1.0,
                                              st3[:, tc:tc + 1], ALU.mult, ALU.mult)
                inc(inst, "v")
                v.wait_ge(sems["v"], C["v"])
                inst = v.tensor_scalar(st4[:, tc:tc + 1], escr[:, 3, 0:1], -1.0, st4[:, tc:tc + 1],
                                       ALU.mult, ALU.add)
                inc(inst, "v")
                v.wait_ge(sems["v"], C["v"])
                inst = v.tensor_scalar(st4[:, tc:tc + 1], st4[:, tc:tc + 1], LN_EPS, None, ALU.add)
                inc(inst, "v")
                v.wait_ge(sems["s"], s0 + 4 * tc + 4)
                inst = v.tensor_scalar(escr[:, 3, 0:1], st3[:, tc:tc + 1], -1.0, st4[:, tc:tc + 1],
                                       ALU.mult, ALU.mult)
                inc(inst, "v")
                v.wait_ge(sems["v"], C["v"])
                inst = v.tensor_scalar(escr[:, 0, :], escr[:, 2, :], st4[:, tc:tc + 1],
                                       escr[:, 3, 0:1], ALU.mult, ALU.add)
                inc(inst, "v")
                v.wait_ge(sems["v"], C["v"])
                inst = v.tensor_tensor(xou[:, tc % 2, :], xstr[:, tc % 2, :], escr[:, 0, :], ALU.add)
                inc(inst, "v")
        sec("vector", vf)
        def sf(sc):
            for tc in range(TC):
                sc.wait_ge(sems["v"], v0 + KV * tc + 9)
                inst = sc.activation(st4[:, tc:tc + 1], st4[:, tc:tc + 1], AF.Ln, scale=1.0)
                inc(inst, "s")
                sc.wait_ge(sems["s"], C["s"])
                inst = sc.activation(st4[:, tc:tc + 1], st4[:, tc:tc + 1], AF.Exp, scale=-0.5)
                inc(inst, "s")
                sc.wait_ge(sems["v"], v0 + KV * tc + 18)
                inst = sc.activation(st4[:, tc:tc + 1], st4[:, tc:tc + 1], AF.Ln, scale=1.0)
                inc(inst, "s")
                sc.wait_ge(sems["s"], C["s"])
                inst = sc.activation(st4[:, tc:tc + 1], st4[:, tc:tc + 1], AF.Exp, scale=-0.5)
                inc(inst, "s")
        sec("scalar", sf)
        def osec(gp):
            for tc in range(TC):
                gp.wait_ge(sems["v"], v0 + KV * (tc + 1))
                inst = gp.dma_start(out=dst[:, tc, :], in_=xou[:, tc % 2, :])
                inc(inst, f"q{2 + tc % 2}")
        sec("gpsimd", osec)

    def block_moe(src, dst):
        with (
            sb("z_tm", (P, TC, D), BF16) as z_tm,
            sb("SL32", (P, TC), I32) as SL32,
            sb("CUM", (P, TC, NE), F32) as CUM,
            sb("MKb", (P, TC, NE), BF16) as MKb,
            sb("LG", (P, TC, NE), F32) as LG,
            sb("BREP", (P, TC * NE), F32) as BREP,
            sb("ECREP", (P, NE), F32) as ECREP,
            sb("mscr", (P, 2 * D), F32) as mscr,
        ):
            ln_u_from_dram(src, z_tm)

            with sb("u_fm", (P, DC, T), BF16) as u_fm, sb("cb", (P, DC, NE), BF16) as cb:
                transpose_tm_fm(z_tm, u_fm)
                def ldc(gp):
                    wait_all(gp)
                    inst = gp.dma_start(out=cb[:, :, :], in_=centT.rearrange("(a b) n -> b a n", b=P))
                    inc(inst, "gd")
                sec("gpsimd", ldc)
                t0 = C["t"]; v0 = C["v"]
                def tf(t):
                    wait_all(t)
                    for tc in range(TC):
                        if tc >= 2:
                            t.wait_ge(sems["v"], v0 + tc - 1)
                        for kc in range(DC):
                            inst = t.matmul(ps[tc % 2][:, 0:NE], u_fm[:, kc, tc * P:(tc + 1) * P],
                                            cb[:, kc, :], start=(kc == 0), stop=(kc == DC - 1))
                        inc(inst, "t")
                sec("tensor", tf)
                def vf(v):
                    for tc in range(TC):
                        v.wait_ge(sems["t"], t0 + tc + 1)
                        inst = v.tensor_copy(LG[:, tc, :], ps[tc % 2][:, 0:NE])
                        inc(inst, "v")
                sec("vector", vf)

            def vm(v):
                wait(v, "v")
                for tc in range(TC):
                    inst = v.reduce_max(st3[:, tc:tc + 1], LG[:, tc, :], axis=AX)
                    inc(inst, "v")
                    v.wait_ge(sems["v"], C["v"])
                    inst = v.tensor_scalar(MKb[:, tc, :], LG[:, tc, :], st3[:, tc:tc + 1], None, ALU.is_ge)
                    inc(inst, "v")
                    v.wait_ge(sems["v"], C["v"])
            sec("vector", vm)

            t0 = C["t"]; v0 = C["v"]
            def tcum(t):
                wait_all(t)
                for tc in range(TC):
                    if tc >= 2:
                        t.wait_ge(sems["v"], v0 + 2 * (tc - 1))
                    t.matmul(ps[2 + tc % 2][:, 0:NE], g_triu[:, :], MKb[:, tc, :],
                             start=True, stop=True)
                    inst = t.matmul(ps[4 + tc % 2][0:1, 0:NE], g_onescol[:, :], MKb[:, tc, :],
                                    start=True, stop=True)
                    inc(inst, "t")
            sec("tensor", tcum)
            def vcum(v):
                for tc in range(TC):
                    v.wait_ge(sems["t"], t0 + tc + 1)
                    inst = v.tensor_copy(CUM[:, tc, :], ps[2 + tc % 2][:, 0:NE])
                    inc(inst, "v")
                    inst = v.tensor_copy(rowA[0:1, tc * NE:(tc + 1) * NE], ps[4 + tc % 2][0:1, 0:NE])
                    inc(inst, "v")
            sec("vector", vcum)
            def ldec(sy):
                wait_all(sy)
                inst = sy.dma_start(out=rowB[0:1, 128 + 0:128 + NE], in_=ecapd[0:1, :])
                inc(inst, "qd")
            sec("sync", ldec)
            def vpre(v):
                wait(v, "qd"); wait(v, "v")
                inst = v.memset(rowB[0:1, 0:NE], 0.0)
                inc(inst, "v")
                for tc in range(1, TC):
                    v.wait_ge(sems["v"], C["v"])
                    inst = v.tensor_tensor(rowB[0:1, tc * NE:(tc + 1) * NE],
                                           rowB[0:1, (tc - 1) * NE:tc * NE],
                                           rowA[0:1, (tc - 1) * NE:tc * NE], ALU.add)
                    inc(inst, "v")
                v.wait_ge(sems["v"], C["v"])
                inst = v.tensor_scalar(rowB[0:1, 0:TC * NE], rowB[0:1, 0:TC * NE], -1.0, None, ALU.add)
                inc(inst, "v")
            sec("vector", vpre)
            bcast_row(rowB, BREP, TC * NE)
            def vpre2(v):
                wait(v, "v")
                inst = v.tensor_copy(rowA[0:1, 0:NE], rowB[0:1, 128:128 + NE])
                inc(inst, "v")
            sec("vector", vpre2)
            bcast_row(rowA, ECREP, NE)

            def vslot(v):
                wait(v, "v")
                for tc in range(TC):
                    inst = v.tensor_tensor(mscr[:, 0:NE], CUM[:, tc, :],
                                           BREP[:, tc * NE:(tc + 1) * NE], ALU.add)
                    inc(inst, "v")
                    v.wait_ge(sems["v"], C["v"])
                    inst = v.tensor_scalar(mscr[:, 0:NE], mscr[:, 0:NE], float(CAP - 1), None, ALU.min)
                    inc(inst, "v")
                    v.wait_ge(sems["v"], C["v"])
                    inst = v.tensor_tensor(mscr[:, 0:NE], mscr[:, 0:NE], ECREP[:, :], ALU.add)
                    inc(inst, "v")
                    v.wait_ge(sems["v"], C["v"])
                    inst = v.tensor_tensor(mscr[:, NE:2 * NE], mscr[:, 0:NE], MKb[:, tc, :], ALU.mult)
                    inc(inst, "v")
                    v.wait_ge(sems["v"], C["v"])
                    inst = v.reduce_sum(mscr[:, 2 * NE:2 * NE + 1], mscr[:, NE:2 * NE], axis=AX)
                    inc(inst, "v")
                    v.wait_ge(sems["v"], C["v"])
                    inst = v.tensor_scalar(mscr[:, 2 * NE:2 * NE + 1], mscr[:, 2 * NE:2 * NE + 1],
                                           float(NSLOT - 1), None, ALU.min)
                    inc(inst, "v")
                    v.wait_ge(sems["v"], C["v"])
                    inst = v.tensor_copy(SL32[:, tc:tc + 1], mscr[:, 2 * NE:2 * NE + 1])
                    inc(inst, "v")
                    v.wait_ge(sems["v"], C["v"])
            sec("vector", vslot)

            with sb("zzero", (P, 4, D), BF16) as zzero:
                def vz(v):
                    wait(v, "v")
                    inst = v.memset(zzero[:, :, :], 0.0)
                    inc(inst, "v")
                sec("vector", vz)
                def zfill(sy):
                    wait(sy, "v")
                    for e in range(NE):
                        inst = sy.dma_start(out=zslot_r[e], in_=zzero[:, :, :])
                        inc(inst, "qd")
                sec("sync", zfill)
                def gsc(gp):
                    wait_all(gp)
                    for tc in range(TC):
                        inst = gp.indirect_dma_start(
                            out=zslot_d,
                            out_offset=bass.IndirectOffsetOnAxis(ap=SL32[:, tc:tc + 1], axis=0),
                            in_=z_tm[:, tc, :], in_offset=None)
                        inc(inst, "gd")
                    wait(gp, "gd")
                sec("gpsimd", gsc)

            with (
                sb("W1b", (P, 2, DC, D), BF16) as W1b,
                sb("W2b", (P, 2, DC, D), BF16) as W2b,
                sb("xn", (P, DC, CAP), BF16) as xn,
                sb("hq", (P, DC, CAP), BF16) as hq,
                sb("Y", (P, 4, D), F32) as Y,
                sb("zr", (P, 4, D), BF16) as zr,
                sb("db", (P, 4, D), BF16) as db,
                sb("lgc", (P, DC), F32) as lgc,
                sb("lbc", (P, DC), F32) as lbc,
                sb("b1c", (P, 32), F32) as b1c,
                sb("B2R", (P, D), F32) as B2R,
            ):
                wsem = ["wa", "wb"]
                def pre(gp):
                    wait_all(gp)
                    i1 = gp.dma_start(out=W1b[:, 0, :, :],
                                      in_=W1d[0].rearrange("(a b) f -> b a f", b=P)[:, :, 0:D])
                    inc(i1, "wa")
                    i2 = gp.dma_start(out=W2b[:, 0, :, :],
                                      in_=W2d[0][0:D, :].rearrange("(a b) f -> b a f", b=P))
                    inc(i2, "wa")
                sec("gpsimd", pre)

                for e in range(NE):
                    def lde(sy, e=e):
                        wait_all(sy)
                        i1 = sy.dma_start(out=lgc[:, :], in_=lng8[e].rearrange("(a b) -> b a", b=P),
                                          allow_slow_non_contiguous=True)
                        inc(i1, "qd")
                        i2 = sy.dma_start(out=lbc[:, :], in_=lnb8[e].rearrange("(a b) -> b a", b=P),
                                          allow_slow_non_contiguous=True)
                        inc(i2, "qd")
                        i3 = sy.dma_start(out=b1c[:, :], in_=b1d[e].rearrange("(a b) -> b a", b=P),
                                          allow_slow_non_contiguous=True)
                        inc(i3, "qd")
                        i4 = sy.dma_start(out=rowB[0:1, 0:D], in_=b2d[e:e + 1, :])
                        inc(i4, "qd")
                    sec("sync", lde)
                    bcast_row(rowB, B2R, D)
                    def gxn(sy, e=e):
                        wait_all(sy)
                        for dc in range(DC):
                            inst = sy.dma_start(out=xn[:, dc, :],
                                                in_=zslot_d[e * CAP:(e + 1) * CAP, dc * P:(dc + 1) * P],
                                                transpose=True)
                            inc(inst, "qd")
                    sec("sync", gxn)
                    def vxn(v, e=e):
                        wait(v, "qd")
                        for dc in range(DC):
                            inst = v.tensor_scalar(xn[:, dc, :], xn[:, dc, :], lgc[:, dc:dc + 1],
                                                   lbc[:, dc:dc + 1], ALU.mult, ALU.add)
                            inc(inst, "v")
                            v.wait_ge(sems["v"], C["v"])
                    sec("vector", vxn)

                    for qt in range(4):
                        buf = (e * 4 + qt) % 2
                        wcur = wsem[buf]
                        wn = (e * 4 + qt)
                        if wn + 1 < 32:
                            ne_, nq = divmod(wn + 1, 4)
                            def prefetch(gp, ne_=ne_, nq=nq, buf=buf):
                                wait(gp, "t")
                                i1 = gp.dma_start(out=W1b[:, 1 - buf, :, :],
                                                  in_=W1d[ne_].rearrange("(a b) f -> b a f", b=P)[:, :, nq * D:(nq + 1) * D])
                                inc(i1, wsem[1 - buf])
                                i2 = gp.dma_start(out=W2b[:, 1 - buf, :, :],
                                                  in_=W2d[ne_][nq * D:(nq + 1) * D, :].rearrange("(a b) f -> b a f", b=P))
                                inc(i2, wsem[1 - buf])
                            sec("gpsimd", prefetch)
                        t0 = C["t"]; s0 = C["s"]
                        def th(t, buf=buf, wcur=wcur):
                            wait(t, wcur)
                            wait(t, "v"); wait(t, "s")
                            for fc in range(DC):
                                if fc >= 2:
                                    t.wait_ge(sems["s"], s0 + fc - 1)
                                for kc in range(DC):
                                    inst = t.matmul(ps[fc % 2][:, :], W1b[:, buf, kc, fc * P:(fc + 1) * P],
                                                    xn[:, kc, :], start=(kc == 0), stop=(kc == DC - 1))
                                inc(inst, "t")
                        sec("tensor", th)
                        def sh(sc, qt=qt):
                            for fc in range(DC):
                                sc.wait_ge(sems["t"], t0 + fc + 1)
                                inst = sc.activation(hq[:, fc, :], ps[fc % 2][:, :], AF.Relu,
                                                     bias=b1c[:, qt * 8 + fc:qt * 8 + fc + 1])
                                inc(inst, "s")
                        sec("scalar", sh)
                        t1 = C["t"]; v1 = C["v"]
                        def ty(t, buf=buf):
                            wait(t, "s")
                            j = 0
                            for sc_ in range(4):
                                for h in range(2):
                                    if j >= 2:
                                        t.wait_ge(sems["v"], v1 + j - 1)
                                    for fc in range(DC):
                                        inst = t.matmul(ps[2 + j % 2][:, :], hq[:, fc, sc_ * P:(sc_ + 1) * P],
                                                        W2b[:, buf, fc, h * 512:(h + 1) * 512],
                                                        start=(fc == 0), stop=(fc == DC - 1))
                                    inc(inst, "t")
                                    j += 1
                        sec("tensor", ty)
                        def vy(v, qt=qt):
                            j = 0
                            for sc_ in range(4):
                                for h in range(2):
                                    v.wait_ge(sems["t"], t1 + j + 1)
                                    dstap = Y[:, sc_, h * 512:(h + 1) * 512]
                                    if qt == 0:
                                        inst = v.tensor_tensor(dstap, ps[2 + j % 2][:, :],
                                                               B2R[:, h * 512:(h + 1) * 512], ALU.add)
                                    else:
                                        inst = v.tensor_tensor(dstap, dstap, ps[2 + j % 2][:, :], ALU.add)
                                    inc(inst, "v")
                                    j += 1
                        sec("vector", vy)

                    def ldz(sy, e=e):
                        wait(sy, "v")
                        inst = sy.dma_start(out=zr[:, :, :], in_=zslot_r[e])
                        inc(inst, "qd")
                    sec("sync", ldz)
                    v0 = C["v"]; s0 = C["s"]
                    def vw(v, e=e):
                        wait(v, "qd")
                        for sc_ in range(4):
                            inst = v.tensor_tensor(escr[:, 0, :], Y[:, sc_, :], zr[:, sc_, :], ALU.add)
                            inc(inst, "v")
                            v.wait_ge(sems["v"], C["v"])
                            v.reduce_sum(st3[:, sc_:sc_ + 1], escr[:, 0, :], axis=AX)
                            inst = v.scalar_tensor_tensor(escr[:, 1, :], escr[:, 0, :], 1.0, escr[:, 0, :],
                                                          ALU.mult, ALU.mult, accum_out=st4[:, sc_:sc_ + 1])
                            inc(inst, "v")
                            v.wait_ge(sems["v"], C["v"])
                            v.tensor_scalar(st3[:, sc_:sc_ + 1], st3[:, sc_:sc_ + 1], 1.0 / D, None, ALU.mult)
                            inst = v.tensor_scalar(st4[:, sc_:sc_ + 1], st4[:, sc_:sc_ + 1], 1.0 / D, None, ALU.mult)
                            inc(inst, "v")
                            v.wait_ge(sems["v"], C["v"])
                            inst = v.scalar_tensor_tensor(escr[:, 3, 0:1], st3[:, sc_:sc_ + 1], 1.0,
                                                          st3[:, sc_:sc_ + 1], ALU.mult, ALU.mult)
                            inc(inst, "v")
                            v.wait_ge(sems["v"], C["v"])
                            inst = v.tensor_scalar(st4[:, sc_:sc_ + 1], escr[:, 3, 0:1], -1.0,
                                                   st4[:, sc_:sc_ + 1], ALU.mult, ALU.add)
                            inc(inst, "v")
                            v.wait_ge(sems["v"], C["v"])
                            inst = v.tensor_scalar(st4[:, sc_:sc_ + 1], st4[:, sc_:sc_ + 1], LN_EPS, None, ALU.add)
                            inc(inst, "v")
                            v.wait_ge(sems["s"], s0 + 2 * (sc_ + 1))
                            inst = v.tensor_scalar(escr[:, 3, 0:1], st3[:, sc_:sc_ + 1], -1.0,
                                                   st4[:, sc_:sc_ + 1], ALU.mult, ALU.mult)
                            inc(inst, "v")
                            v.wait_ge(sems["v"], C["v"])
                            inst = v.tensor_scalar(db[:, sc_, :], escr[:, 0, :], st4[:, sc_:sc_ + 1],
                                                   escr[:, 3, 0:1], ALU.mult, ALU.add)
                            inc(inst, "v")
                            v.wait_ge(sems["v"], C["v"])
                    sec("vector", vw)
                    def sw(sc, e=e):
                        for sc_ in range(4):
                            sc.wait_ge(sems["v"], v0 + 8 * sc_ + 6)
                            inst = sc.activation(st4[:, sc_:sc_ + 1], st4[:, sc_:sc_ + 1], AF.Ln, scale=1.0)
                            inc(inst, "s")
                            sc.wait_ge(sems["s"], C["s"])
                            inst = sc.activation(st4[:, sc_:sc_ + 1], st4[:, sc_:sc_ + 1], AF.Exp, scale=-0.5)
                            inc(inst, "s")
                    sec("scalar", sw)
                    def std(sy, e=e):
                        wait(sy, "v")
                        inst = sy.dma_start(out=dslot_r[e], in_=db[:, :, :])
                        inc(inst, "qd")
                    sec("sync", std)

            with sb("DT", (P, TC, D), BF16) as DT:
                def gug(gp):
                    wait_all(gp)
                    for tc in range(TC):
                        inst = gp.indirect_dma_start(
                            out=DT[:, tc, :], out_offset=None, in_=dslot_d,
                            in_offset=bass.IndirectOffsetOnAxis(ap=SL32[:, tc:tc + 1], axis=0))
                        inc(inst, "gd")
                    wait(gp, "gd")
                sec("gpsimd", gug)
                v0 = C["v"]
                qp0 = [C["q0"], C["q1"]]; sp0 = [C["q2"], C["q3"]]
                def dsec(sy):
                    wait_all(sy)
                    for tc in range(TC):
                        if tc >= 2:
                            sy.wait_ge(sems["v"], v0 + (tc - 1))
                        inst = sy.dma_start(out=xstr[:, tc % 2, :], in_=src[:, tc, :])
                        inc(inst, f"q{tc % 2}")
                sec("sync", dsec)
                def vf(v):
                    for tc in range(TC):
                        v.wait_ge(sems[f"q{tc % 2}"], qp0[tc % 2] + 16 * (tc // 2 + 1))
                        if tc >= 2:
                            v.wait_ge(sems[f"q{2 + tc % 2}"], sp0[tc % 2] + 16 * ((tc - 2) // 2 + 1))
                        else:
                            v.wait_ge(sems[f"q{2 + tc % 2}"], sp0[tc % 2])
                        inst = v.tensor_tensor(xou[:, tc % 2, :], xstr[:, tc % 2, :], DT[:, tc, :], ALU.add)
                        inc(inst, "v")
                sec("vector", vf)
                def osec(gp):
                    for tc in range(TC):
                        gp.wait_ge(sems["v"], v0 + tc + 1)
                        inst = gp.dma_start(out=dst[:, tc, :], in_=xou[:, tc % 2, :])
                        inc(inst, f"q{2 + tc % 2}")
                sec("gpsimd", osec)
        return

    def block_l1_attn(src, dst):
        with (
            sb("xn_fm", (P, DC, T), BF16) as xn_fm,
            sb("BN", (P, 16), F32) as BN,
            sb("A8", (P, DC), F32) as A8,
            sb("B8", (P, DC), F32) as B8,
            sb("LAM", (P, 1), F32) as LAM,
            sb("SUB2", (P, 1), F32) as SUB2,
            sb("YN", (P, TC, T), BF16) as YN,
            sb("ascr", (P, 4, 512), F32) as ascr,
            sb("sqb2", (P, 2, 512), BF16) as sqb2,
            
        ):
            with sb("u_tm", (P, TC, D), BF16) as u_tm:
                ln_u_from_dram(src, u_tm)
                transpose_tm_fm(u_tm, xn_fm)

            def vbn(v):
                wait(v, "v")
                for dc in range(DC):
                    v.reduce_sum(BN[:, dc:dc + 1], xn_fm[:, dc, :], axis=AX)
                    inst = v.scalar_tensor_tensor(escr[:, 0, :512], xn_fm[:, dc, 0:512], 1.0,
                                                  xn_fm[:, dc, 0:512], ALU.mult, ALU.mult)
                    inc(inst, "v")
                    v.wait_ge(sems["v"], C["v"])
                for dc in range(DC):
                    inst = v.scalar_tensor_tensor(xou[:, 0, :], xn_fm[:, dc, 0:D], 1.0,
                                                  xn_fm[:, dc, 0:D], ALU.mult, ALU.mult,
                                                  accum_out=st3[:, 0:1])
                    inc(inst, "v")
                    v.wait_ge(sems["v"], C["v"])
                    inst = v.scalar_tensor_tensor(xou[:, 0, :], xn_fm[:, dc, D:2 * D], 1.0,
                                                  xn_fm[:, dc, D:2 * D], ALU.mult, ALU.mult,
                                                  accum_out=st3[:, 1:2])
                    inc(inst, "v")
                    v.wait_ge(sems["v"], C["v"])
                    inst = v.tensor_tensor(BN[:, 8 + dc:9 + dc], st3[:, 0:1], st3[:, 1:2], ALU.add)
                    inc(inst, "v")
                    v.wait_ge(sems["v"], C["v"])
            sec("vector", vbn)
            def bnout(sy):
                wait(sy, "v")
                inst = sy.dma_start(out=bnin_d[:, :], in_=BN[:, :])
                inc(inst, "qd")
            sec("sync", bnout)
            def gcc(gp):
                wait(gp, "qd")
                inst = gp.collective_compute("AllReduce", ALU.add,
                                             replica_groups=[list(range(8))],
                                             ins=[bnin_d.opt() if hasattr(bnin_d, "opt") else bnin_d],
                                             outs=[bnout_d.opt() if hasattr(bnout_d, "opt") else bnout_d])
                inc(inst, "cc")
                wait(gp, "cc")
            sec("gpsimd", gcc)
            def bnin(sy):
                wait(sy, "cc")
                inst = sy.dma_start(out=BN[:, :], in_=bnout_d[:, :])
                inc(inst, "qd")
            sec("sync", bnin)
            with sb("gb", (P, 2 * DC), F32) as gb:
                def ldgb(sy):
                    wait(sy, "qd")
                    i1 = sy.dma_start(out=gb[:, 0:DC], in_=bngd[:, :])
                    inc(i1, "qd")
                    i2 = sy.dma_start(out=gb[:, DC:2 * DC], in_=bnbd[:, :])
                    inc(i2, "qd")
                sec("sync", ldgb)
                def vbn2(v):
                    wait(v, "qd")
                    inst = v.tensor_scalar(BN[:, :], BN[:, :], 1.0 / (8 * T), None, ALU.mult)
                    inc(inst, "v")
                    wait(v, "v")
                    inst = v.scalar_tensor_tensor(st3[:, 0:DC], BN[:, 0:DC], 1.0, BN[:, 0:DC],
                                                  ALU.mult, ALU.mult)
                    inc(inst, "v")
                    wait(v, "v")
                    inst = v.tensor_tensor(BN[:, 8:16], BN[:, 8:16], st3[:, 0:DC], ALU.subtract)
                    inc(inst, "v")
                    wait(v, "v")
                    inst = v.tensor_scalar(BN[:, 8:16], BN[:, 8:16], LN_EPS, None, ALU.add)
                    inc(inst, "v")
                sec("vector", vbn2)
                def sbn(sc):
                    wait(sc, "v")
                    inst = sc.activation(BN[:, 8:16], BN[:, 8:16], AF.Ln, scale=1.0)
                    inc(inst, "s")
                    wait(sc, "s")
                    inst = sc.activation(BN[:, 8:16], BN[:, 8:16], AF.Exp, scale=-0.5)
                    inc(inst, "s")
                sec("scalar", sbn)
                def vbn3(v):
                    wait(v, "s")
                    inst = v.tensor_tensor(A8[:, :], gb[:, 0:DC], BN[:, 8:16], ALU.mult)
                    inc(inst, "v")
                    wait(v, "v")
                    inst = v.tensor_tensor(st3[:, 0:DC], BN[:, 0:DC], A8[:, :], ALU.mult)
                    inc(inst, "v")
                    wait(v, "v")
                    inst = v.tensor_tensor(B8[:, :], gb[:, DC:2 * DC], st3[:, 0:DC], ALU.subtract)
                    inc(inst, "v")
                    wait(v, "v")
                    for dc in range(DC):
                        for h in range(2):
                            inst = v.tensor_scalar(xn_fm[:, dc, h * D:(h + 1) * D],
                                                   xn_fm[:, dc, h * D:(h + 1) * D],
                                                   A8[:, dc:dc + 1], B8[:, dc:dc + 1],
                                                   ALU.mult, ALU.add)
                            inc(inst, "v")
                            v.wait_ge(sems["v"], C["v"])
                sec("vector", vbn3)

            def ldl(sy):
                wait_all(sy)
                i1 = sy.dma_start(out=rowA[0:1, 0:256], in_=lpackd[0:1, :])
                inc(i1, "qd")
                i2 = sy.dma_start(out=st4[:, 15:16], in_=sublnd[:, :])
                inc(i2, "qd")
            sec("sync", ldl)
            def vl(v):
                wait(v, "qd")
                inst = v.tensor_tensor(rowA[0:1, 0:64], rowA[0:1, 0:64], rowA[0:1, 64:128], ALU.mult)
                inc(inst, "v")
                wait(v, "v")
                inst = v.reduce_sum(rowB[0:1, 0:1], rowA[0:1, 0:64], axis=AX)
                inc(inst, "v")
                wait(v, "v")
                inst = v.tensor_tensor(rowA[0:1, 128:192], rowA[0:1, 128:192], rowA[0:1, 192:256], ALU.mult)
                inc(inst, "v")
                wait(v, "v")
                inst = v.reduce_sum(rowB[0:1, 1:2], rowA[0:1, 128:192], axis=AX)
                inc(inst, "v")
            sec("vector", vl)
            def sl(sc):
                wait(sc, "v")
                inst = sc.activation(rowB[0:1, 0:2], rowB[0:1, 0:2], AF.Exp)
                inc(inst, "s")
            sec("scalar", sl)
            def vl2(v):
                wait(v, "s")
                inst = v.tensor_tensor(rowB[0:1, 2:3], rowB[0:1, 0:1], rowB[0:1, 1:2], ALU.subtract)
                inc(inst, "v")
                wait(v, "v")
                inst = v.tensor_scalar(rowB[0:1, 2:3], rowB[0:1, 2:3], LAMBDA_INIT, None, ALU.add)
                inc(inst, "v")
                wait(v, "v"); wait(v, "qd")
                inst = v.tensor_scalar(SUB2[:, :], st4[:, 15:16], 1.0 - LAMBDA_INIT, None, ALU.mult)
                inc(inst, "v")
            sec("vector", vl2)
            bcast_row(rowB[0:1, 2:3].rearrange("a b -> a b"), LAM, 1)

            with (
                sb("VG", (P, TC, 512), BF16) as VG,
                sb("QK", (P, 4, T), BF16) as QK,
                sb("E", (P, TC, 512), BF16) as E,
                sb("wq4", (P, DC, 512), BF16) as wq4,
            ):
                state = {"vz": 0, "tr2": 0, "qk": 0, "first": True, "pend": None, "sexend": {}, "tzmark": {}, "pvprog": 0, "t0": {}, "tailv": 0}

                def emit_proj_v(g4):
                    load_w(wq4, vTd, 512, col_off=g4 * 512)
                    t0 = C["t"]; v0 = C["v"]
                    def tv(t):
                        wait_all(t)
                        j = 0
                        for tc in range(TC):
                            if j >= 4:
                                t.wait_ge(sems["v"], v0 + j - 3)
                            for kc in range(DC):
                                inst = t.matmul(ps[j % 4][:, :], xn_fm[:, kc, tc * P:(tc + 1) * P],
                                                wq4[:, kc, :], start=(kc == 0), stop=(kc == DC - 1))
                            inc(inst, "t")
                            j += 1
                    sec("tensor", tv)
                    def vv(v):
                        for tc in range(TC):
                            v.wait_ge(sems["t"], t0 + tc + 1)
                            inst = v.tensor_copy(VG[:, tc, :], ps[tc % 4][:, :])
                            inc(inst, "v")
                    sec("vector", vv)
                    state["qk"] = C["v"]

                def emit_proj_qk(g2):
                    for i, wT in enumerate([q1T, q2T, k1T, k2T]):
                        def ldq(gp, wT=wT, i=i, g2=g2):
                            wait_all(gp)
                            inst = gp.dma_start(out=wq4[:, :, i * P:(i + 1) * P],
                                                in_=wT.rearrange("(a b) n -> b a n", b=P)[:, :, g2 * P:(g2 + 1) * P])
                            inc(inst, "gd")
                        sec("gpsimd", ldq)
                    t0 = C["t"]; v0 = C["v"]
                    def tq(t):
                        wait_all(t)
                        j = 0
                        for i in range(4):
                            for tcol in range(4):
                                if j >= 4:
                                    t.wait_ge(sems["v"], v0 + j - 3)
                                for kc in range(DC):
                                    inst = t.matmul(ps[j % 4][:, :], wq4[:, kc, i * P:(i + 1) * P],
                                                    xn_fm[:, kc, tcol * 512:(tcol + 1) * 512],
                                                    start=(kc == 0), stop=(kc == DC - 1))
                                inc(inst, "t")
                                j += 1
                    sec("tensor", tq)
                    def vq(v):
                        j = 0
                        for i in range(4):
                            for tcol in range(4):
                                v.wait_ge(sems["t"], t0 + j + 1)
                                inst = v.tensor_copy(QK[:, i, tcol * 512:(tcol + 1) * 512], ps[j % 4][:, :])
                                inc(inst, "v")
                                j += 1
                    sec("vector", vq)
                    state["qk"] = C["v"]

                def emit_tsc(u):
                    h, qq, st_ = u
                    hb = (h % 2) * 64
                    ql = slice(qq * 512, (qq + 1) * 512)
                    s0 = C["s"]
                    first = state["first"]; state["first"] = False
                    qkm = state["qk"]
                    t0 = C["t"]
                    state["t0"][u] = t0
                    def tsc(t):
                        if first:
                            wait_all(t)
                        else:
                            t.wait_ge(sems["v"], qkm)
                        for scn in range(TC):
                            if (not first) or scn >= 4:
                                t.wait_ge(sems["s"], s0 + scn - 3)
                            inst = t.matmul(ps[scn % 4][:, :],
                                            QK[hb:hb + 64, 2 + st_, scn * P:(scn + 1) * P],
                                            QK[hb:hb + 64, st_, ql],
                                            start=True, stop=True)
                            inc(inst, "t")
                    sec("tensor", tsc)

                def emit_sex(u):
                    t0 = state["t0"][u]
                    pvprog = state["pvprog"]
                    def sex(sc):
                        for scn in range(TC):
                            sc.wait_ge(sems["t"], t0 + scn + 1)
                            if pvprog:
                                sc.wait_ge(sems["t"], pvprog + 16 + scn + 1)
                            inst = sc.activation(E[:, scn, :], ps[scn % 4][:, :], AF.Exp, scale=0.125)
                            inc(inst, "s")
                    sec("scalar", sex)
                    state["sexend"][u] = C["s"]

                def emit_zpv(u):
                    h, qq, st_ = u
                    hv = (h % 4) * P
                    smark = state["sexend"][u]
                    pv = max(state["vz"], state["tailv"])
                    t1 = C["t"]
                    state["pvprog"] = t1
                    def tz(t):
                        t.wait_ge(sems["s"], smark)
                        if pv:
                            t.wait_ge(sems["v"], pv)
                        for scn in range(TC):
                            inst = t.matmul(ps[4][0:1, :], g_onescol[:, :], E[:, scn, :],
                                            start=(scn == 0), stop=(scn == TC - 1))
                            inc(inst, "t")
                        for scn in range(TC):
                            inst = t.matmul(ps[5][:, :], VG[:, scn, hv:hv + P], E[:, scn, :],
                                            start=(scn == 0), stop=(scn == TC - 1))
                            inc(inst, "t")
                    sec("tensor", tz)
                    state["tzmark"][u] = C["t"]
                    par = qq % 2
                    rowX = rowA if par == 0 else rowB
                    def vz(v):
                        wait(v, "v")
                        v.wait_ge(sems["t"], t1 + 16)
                        inst = v.tensor_copy(rowX[0:1, st_ * 512:(st_ + 1) * 512], ps[4][0:1, :])
                        inc(inst, "v")
                        v.wait_ge(sems["t"], t1 + 32)
                        inst = v.tensor_copy(ascr[:, 2 * par + st_, :], ps[5][:, :])
                        inc(inst, "v")
                    sec("vector", vz)
                    state["vz"] = C["v"]

                def emit_tail(h, qq):
                    par = qq % 2
                    rowX = rowA if par == 0 else rowB
                    a0, a1 = 2 * par, 2 * par + 1
                    A2 = escr[:, 0, 0:512]; A3 = escr[:, 0, 512:1024]
                    def vr(v):
                        wait(v, "v")
                        inst = v.reciprocal(rowX[0:1, 0:1024], rowX[0:1, 0:1024])
                        inc(inst, "v")
                    sec("vector", vr)
                    t2 = C["t"]
                    def tb(t):
                        wait(t, "v")
                        inst = t.matmul(ps[4][:, :], g_ones1f[:, :], rowX[0:1, 0:512], start=True, stop=True)
                        inc(inst, "t")
                        inst = t.matmul(ps[5][:, :], g_ones1f[:, :], rowX[0:1, 512:1024], start=True, stop=True)
                        inc(inst, "t")
                    sec("tensor", tb)
                    def vc(v):
                        v.wait_ge(sems["t"], t2 + 2)
                        v.tensor_copy(A2, ps[4][:, :])
                        inst = v.tensor_copy(A3, ps[5][:, :])
                        inc(inst, "v")
                        wait(v, "v")
                        inst = v.tensor_tensor(ascr[:, a0, :], ascr[:, a0, :], A2, ALU.mult)
                        inc(inst, "v")
                        wait(v, "v")
                        inst = v.scalar_tensor_tensor(ascr[:, a1, :], ascr[:, a1, :], LAM[:, 0:1],
                                                      A3, ALU.mult, ALU.mult)
                        inc(inst, "v")
                        wait(v, "v")
                        inst = v.tensor_tensor(ascr[:, a0, :], ascr[:, a0, :], ascr[:, a1, :], ALU.subtract)
                        inc(inst, "v")
                    sec("vector", vc)
                    trp = state["tr2"]
                    def sq1(sc):
                        wait(sc, "v")
                        if trp:
                            sc.wait_ge(sems["t"], trp)
                        inst = sc.activation(sqb2[:, par, :], ascr[:, a0, :], AF.Square)
                        inc(inst, "s")
                    sec("scalar", sq1)
                    t3 = C["t"]
                    def tr2(t):
                        wait(t, "s")
                        inst = t.matmul(ps[4][:, :], g_ones128[:, :], sqb2[:, par, :], start=True, stop=True)
                        inc(inst, "t")
                    sec("tensor", tr2)
                    state["tr2"] = C["t"]
                    def vr2(v):
                        v.wait_ge(sems["t"], t3 + 1)
                        inst = v.tensor_scalar(A2, ps[4][:, :], 1.0 / P, 1e-5, ALU.mult, ALU.add)
                        inc(inst, "v")
                    sec("vector", vr2)
                    def sr2(sc):
                        wait(sc, "v")
                        inst = sc.activation(A2, A2, AF.Ln, scale=1.0)
                        inc(inst, "s")
                        wait(sc, "s")
                        inst = sc.activation(A2, A2, AF.Exp, scale=-0.5)
                        inc(inst, "s")
                    sec("scalar", sr2)
                    def vfin(v, h=h, qq=qq):
                        wait(v, "s")
                        inst = v.tensor_tensor(ascr[:, a0, :], ascr[:, a0, :], A2, ALU.mult)
                        inc(inst, "v")
                        wait(v, "v")
                        inst = v.tensor_scalar(YN[:, h, qq * 512:(qq + 1) * 512], ascr[:, a0, :],
                                               SUB2[:, 0:1], None, ALU.mult)
                        inc(inst, "v")
                    sec("vector", vfin)
                    state["tailv"] = C["v"]

                for h in range(TC):
                    if h % 4 == 0:
                        emit_proj_v(h // 4)
                    if h % 2 == 0:
                        emit_proj_qk(h // 2)
                    pend = None
                    tailq = []
                    for qq in range(4):
                        for st_ in range(2):
                            u = (h, qq, st_)
                            if pend is not None:
                                emit_zpv(pend)
                            emit_tsc(u)
                            emit_sex(u)
                            if pend is not None and pend[2] == 1:
                                tailq.append((pend[0], pend[1]))
                                if len(tailq) > 1:
                                    emit_tail(*tailq.pop(0))
                            pend = u
                    emit_zpv(pend)
                    tailq.append((pend[0], pend[1]))
                    for tt_ in tailq:
                        emit_tail(*tt_)

            with sb("cb", (P, TC, D), BF16) as cbuf:
                def ldc(gp):
                    wait_all(gp)
                    inst = gp.dma_start(out=cbuf[:, :, :], in_=cTd.rearrange("(a b) n -> b a n", b=P))
                    inc(inst, "gd")
                sec("gpsimd", ldc)
                proj_tm_epilogue(YN, cbuf, src, dst, extra_brep=None, kc_n=TC)
        return

    def block_l1_ffn(src, dst):
        with (
            sb("u_fm", (P, DC, T), BF16) as u_fm,
            sb("Q", (P, DC, T), BF16) as Q,
            sb("G2", (P, 4, T), BF16) as G2,
            sb("LP", (P, 2, T), F32) as LP,
            sb("LPb", (P, 2, T), BF16) as LPb,
            sb("selb", (P, 18, 256), BF16) as selb,
            sb("wb", (P, DC, D), BF16) as wb,
            sb("wbrep", (P, D), F32) as wbrep,
        ):
            with sb("u_tm", (P, TC, D), BF16) as u_tm:
                ln_u_from_dram(src, u_tm)
                transpose_tm_fm(u_tm, u_fm)
            load_w(wb, WqT2, D)
            proj_fm(u_fm, wb, Q, DC)
            with sb("wr", (P, DC, 256), BF16) as wr:
                def ldr(gp):
                    wait_all(gp)
                    inst = gp.dma_start(out=wr[:, :, :], in_=routersT.rearrange("(a b) n -> b a n", b=P))
                    inc(inst, "gd")
                sec("gpsimd", ldr)
                t0 = C["t"]; s0 = C["s"]
                def tg(t):
                    wait_all(t)
                    j = 0
                    for rc in range(2):
                        for tcol in range(4):
                            if j >= 4:
                                t.wait_ge(sems["s"], s0 + 2 * (j - 4) + 2)
                            for kc in range(DC):
                                inst = t.matmul(ps[j % 4][:, :], wr[:, kc, rc * P:(rc + 1) * P],
                                                Q[:, kc, tcol * 512:(tcol + 1) * 512],
                                                start=(kc == 0), stop=(kc == DC - 1))
                            inc(inst, "t")
                            j += 1
                sec("tensor", tg)
                def sg(sc):
                    j = 0
                    for rc in range(2):
                        for tcol in range(4):
                            sc.wait_ge(sems["t"], t0 + j + 1)
                            sl = slice(tcol * 512, (tcol + 1) * 512)
                            inst = sc.activation(G2[:, rc, sl], ps[j % 4][:, :], AF.Sigmoid)
                            inc(inst, "s")
                            inst = sc.activation(G2[:, rc + 2, sl], ps[j % 4][:, :], AF.Sigmoid, scale=-1.0)
                            inc(inst, "s")
                            j += 1
                sec("scalar", sg)
            def lds(sy):
                wait_all(sy)
                inst = sy.dma_start(out=selb[:, :, :], in_=selpd.rearrange("j b n -> b j n"))
                inc(inst, "qd")
            sec("sync", lds)
            for lv in range(8):
                pieces = SEL_BY_LEVEL[lv]
                t0 = C["t"]; v0 = C["v"]
                def tl(t, pieces=pieces, lv=lv):
                    if lv == 0:
                        wait_all(t)
                    wait(t, "v")
                    j = 0
                    for mc in range(2):
                        for tcol in range(4):
                            if j >= 4:
                                t.wait_ge(sems["v"], v0 + j - 3)
                            for kk, (jj, ch) in enumerate(pieces):
                                inst = t.matmul(ps[j % 4][:, :], selb[:, jj, mc * P:(mc + 1) * P],
                                                G2[:, ch, tcol * 512:(tcol + 1) * 512],
                                                start=(kk == 0), stop=(kk == len(pieces) - 1))
                            inc(inst, "t")
                            j += 1
                sec("tensor", tl)
                def vl_(v, lv=lv):
                    j = 0
                    for mc in range(2):
                        for tcol in range(4):
                            v.wait_ge(sems["t"], t0 + j + 1)
                            dstap = LP[:, mc, tcol * 512:(tcol + 1) * 512]
                            if lv == 0:
                                inst = v.tensor_copy(dstap, ps[j % 4][:, :])
                            else:
                                inst = v.tensor_tensor(dstap, dstap, ps[j % 4][:, :], ALU.mult)
                            inc(inst, "v")
                            j += 1
                sec("vector", vl_)
            def vcast(v):
                wait(v, "v")
                inst = v.tensor_copy(LPb[:, :, :], LP[:, :, :])
                inc(inst, "v")
            sec("vector", vcast)
            with sb("lvb", (P, 2, D), BF16) as lvb:
                def ldlv(gp):
                    wait_all(gp)
                    inst = gp.dma_start(out=lvb[:, :, :], in_=leafVd.rearrange("(a b) n -> b a n", b=P))
                    inc(inst, "gd")
                sec("gpsimd", ldlv)
                proj_fm(LPb, lvb, Q, DC, kc_n=2)
            load_w(wb, WoutT, D)
            load_row_bcast(woutbd[0:1, :], wbrep, D)
            proj_tm_epilogue(Q, wb, src, dst, extra_brep=wbrep, kc_n=DC)
        return

    srcs = [x_in.rearrange("(a b) d -> b a d", b=P), xs_d, xs_d, xs_d]
    dsts = [xs_d, xs_d, xs_d, out_d]
    dsts[n_blocks - 1] = out_d
    blocks = [block_l0_attn, block_moe, block_l1_attn, block_l1_ffn]
    for bi in range(n_blocks):
        blocks[bi](srcs[bi], dsts[bi])

    def fin(gp):
        wait_all(gp)
    sec("gpsimd", fin)

    stack.close()
    return nc


def _prep_inputs(inputs):
    c = _host_consts()
    f32 = np.float32
    def t(a):
        return np.ascontiguousarray(np.asarray(a, f32).T)
    shared = {
        "wqT": t(inputs["L0_attn_wq"]), "wkT": t(inputs["L0_attn_wk"]),
        "wvT": t(inputs["L0_attn_wv"]), "woT": t(inputs["L0_attn_wo"]),
        "bqkvo": np.stack([np.asarray(inputs["L0_attn_b" + k], f32) for k in "qkvo"]),
        "ff": np.asarray(inputs["L0_attn_freq_filter"], f32).reshape(1, D),
        "lng0": np.asarray(inputs["L0_attn_ln_gamma"], f32).reshape(1, D),
        "lnb0": np.asarray(inputs["L0_attn_ln_beta"], f32).reshape(1, D),
        "centT": t(inputs["L0_ffn_centroids"]),
        "lng8": np.asarray(inputs["L0_ffn_ln_g"], f32),
        "lnb8": np.asarray(inputs["L0_ffn_ln_b"], f32),
        "b1d": np.asarray(inputs["L0_ffn_b1"], f32),
        "b2d": np.asarray(inputs["L0_ffn_b2"], f32),
        "W1d": np.asarray(inputs["L0_ffn_W1"], f32),
        "W2d": np.asarray(inputs["L0_ffn_W2"], f32),
        "q1T": t(inputs["L1_attn_q1_w"]), "q2T": t(inputs["L1_attn_q2_w"]),
        "k1T": t(inputs["L1_attn_k1_w"]), "k2T": t(inputs["L1_attn_k2_w"]),
        "vT": t(inputs["L1_attn_v_w"]), "cT": t(inputs["L1_attn_c_w"]),
        "subln": np.asarray(inputs["L1_attn_subln_w"], f32).reshape(P, 1),
        "lpack": np.concatenate([np.asarray(inputs["L1_attn_" + k], f32).ravel()
                                 for k in ("lq1", "lk1", "lq2", "lk2")]).reshape(1, 256),
        "bng": np.ascontiguousarray(np.asarray(inputs["L1_attn_bn_gamma"], f32).reshape(DC, P).T),
        "bnb": np.ascontiguousarray(np.asarray(inputs["L1_attn_bn_beta"], f32).reshape(DC, P).T),
        "WqT2": t(inputs["L1_ffn_Wq"]),
        "routersT": np.ascontiguousarray(np.pad(t(inputs["L1_ffn_routers"]), ((0, 0), (0, 1)))),
        "leafV": np.asarray(inputs["L1_ffn_leaf_values"], f32),
        "WoutT": t(inputs["L1_ffn_Wout_w"]),
        "woutb": np.asarray(inputs["L1_ffn_Wout_b"], f32).reshape(1, D),
        "ident": c["ident"], "ones1": c["ones1"], "ones1f": c["ones1f"], "onescol": c["onescol"],
        "ones128": c["ones128"], "triu": c["triu"],
        "CF": c["CF"], "SF": c["SF"], "CI": c["CI"], "SI": c["SI"],
        "ecap": c["ecap"], "selp": c["selp"],
    }
    x = np.asarray(inputs["L0_attn_x"], f32)
    return shared, x


def kernel(**inputs):
    shared, x = _prep_inputs(inputs)
    nc = build_nc(N_BLOCKS)
    in_maps = [dict(shared, x=np.ascontiguousarray(x[i])) for i in range(8)]
    res = run_bass_kernel_spmd(nc, in_maps, core_ids=list(range(8)))
    outs = []
    for i in range(8):
        o = np.asarray(res.results[i]["out"])
        outs.append(np.ascontiguousarray(o.transpose(1, 0, 2).reshape(T, D)))
    return np.stack(outs, axis=0)


# version 09
# speedup vs baseline: 1.0425x; 1.0124x over previous
import math
import numpy as np
import ml_dtypes
from contextlib import ExitStack
import concourse.bass as bass
import concourse.mybir as mybir
from concourse.bass_utils import run_bass_kernel_spmd

F32 = mybir.dt.float32
BF16 = mybir.dt.bfloat16
I32 = mybir.dt.int32
AF = mybir.ActivationFunctionType
ALU = mybir.AluOpType
AX = mybir.AxisListType.X
BFN = ml_dtypes.bfloat16

P = 128
T = 2048
TC = 16
D = 1024
DC = 8
FP = 1152
FC = 9
NE = 8
CAP = 512
NSLOT = NE * CAP
FFD = 4096
LN_EPS = 1e-5
LAMBDA_INIT = 0.8 - 0.6 * math.exp(-0.3 * (2 - 1))

N_BLOCKS = 4


def _host_consts():
    c = {}
    c["ident"] = np.eye(P, dtype=BFN)
    c["ones1"] = np.ones((1, P), dtype=BFN)
    c["ones1f"] = np.ones((1, P), dtype=np.float32)
    c["onescol"] = np.ones((P, 1), dtype=BFN)
    c["onescolf"] = np.ones((P, 1), dtype=np.float32)
    c["ones128"] = np.ones((P, P), dtype=BFN)
    c["triu"] = np.triu(np.ones((P, P), dtype=np.float32)).astype(BFN)
    s = np.arange(T)[:, None].astype(np.float64)
    f = np.arange(FP)[None, :].astype(np.float64)
    ang = 2.0 * np.pi * (s * f) / T
    live = (f <= 1024).astype(np.float64)
    c["CF"] = (np.cos(ang) * live).astype(BFN)
    c["SF"] = (-np.sin(ang) * live).astype(BFN)
    w = np.where((f == 0) | (f == 1024), 1.0, 2.0) * live / T
    c["CI"] = np.ascontiguousarray((w * np.cos(ang)).T).astype(BFN)
    c["SI"] = np.ascontiguousarray((-w * np.sin(ang)).T).astype(BFN)
    c["ecap"] = (np.arange(NE, dtype=np.float32) * CAP).reshape(1, NE)
    sel = []
    for lv in range(8):
        S_lv = np.zeros((512, 256), dtype=np.float32)
        for m in range(256):
            node = (2 ** lv - 1) + (m >> (8 - lv))
            branch = (m >> (8 - lv - 1)) & 1
            row = node if branch else 256 + node
            S_lv[row, m] = 1.0
        for ch in ([0, 2] if lv < 7 else [0, 1, 2, 3]):
            sel.append(S_lv[ch * 128:(ch + 1) * 128, :])
    c["selp"] = np.stack(sel).astype(BFN)
    return c

SEL_MAP = [(lv, ch) for lv in range(8) for ch in ([0, 2] if lv < 7 else [0, 1, 2, 3])]
SEL_BY_LEVEL = {}
for _j, (_lv, _ch) in enumerate(SEL_MAP):
    SEL_BY_LEVEL.setdefault(_lv, []).append((_j, _ch))


def build_nc(n_blocks=4):
    nc = bass.Bass("TRN2", target_bir_lowering=False, detect_race_conditions=False)

    def din(name, shape, dt=F32):
        return nc.dram_tensor(name, list(shape), dt, kind="ExternalInput").ap()

    x_in = din("x", (T, D))
    wqT = din("wqT", (D, D)); wkT = din("wkT", (D, D)); wvT = din("wvT", (D, D)); woT = din("woT", (D, D))
    bqkvo = din("bqkvo", (4, D))
    ffin = din("ff", (1, D))
    lng0 = din("lng0", (1, D)); lnb0 = din("lnb0", (1, D))
    centT = din("centT", (D, NE))
    lng8 = din("lng8", (NE, D)); lnb8 = din("lnb8", (NE, D))
    b1d = din("b1d", (NE, FFD)); b2d = din("b2d", (NE, D))
    W1d = din("W1d", (NE, D, FFD)); W2d = din("W2d", (NE, FFD, D))
    q1T = din("q1T", (D, D)); q2T = din("q2T", (D, D)); k1T = din("k1T", (D, D)); k2T = din("k2T", (D, D))
    vTd = din("vT", (D, 2 * D)); cTd = din("cT", (2 * D, D))
    sublnd = din("subln", (P, 1))
    lpackd = din("lpack", (1, 256))
    bngd = din("bng", (P, DC)); bnbd = din("bnb", (P, DC))
    WqT2 = din("WqT2", (D, D)); routersT = din("routersT", (D, 256))
    leafVd = din("leafV", (256, D)); WoutT = din("WoutT", (D, D)); woutbd = din("woutb", (1, D))
    ident = din("ident", (P, P), BF16)
    ones1 = din("ones1", (1, P), BF16)
    ones1f = din("ones1f", (1, P), F32)
    onescol = din("onescol", (P, 1), BF16)
    onescolf = din("onescolf", (P, 1), F32)
    ones128 = din("ones128", (P, P), BF16)
    triu = din("triu", (P, P), BF16)
    CFd = din("CF", (T, FP), BF16); SFd = din("SF", (T, FP), BF16)
    CId = din("CI", (FP, T), BF16); SId = din("SI", (FP, T), BF16)
    ecapd = din("ecap", (1, NE))
    selpd = din("selp", (18, P, 256), BF16)

    out_d = nc.dram_tensor("out", [P, TC, D], F32, kind="ExternalOutput").ap()
    xs_d = nc.dram_tensor("xs", [P, TC, D], F32).ap()
    zslot_d = nc.dram_tensor("zslot", [NSLOT, D], BF16).ap()
    dslot_d = nc.dram_tensor("dslot", [NSLOT, D], BF16).ap()
    bnin_d = nc.dram_tensor("bnin", [P, 16], F32).ap()
    bnout_d = nc.dram_tensor("bnout", [P, 16], F32).ap()

    CFr = CFd.rearrange("(a b) f -> b a f", b=P)
    SFr = SFd.rearrange("(a b) f -> b a f", b=P)
    CIr = CId.rearrange("(a b) s -> b a s", b=P)
    SIr = SId.rearrange("(a b) s -> b a s", b=P)
    zslot_r = zslot_d.rearrange("(e a b) d -> e b a d", e=NE, b=P)
    dslot_r = dslot_d.rearrange("(e a b) d -> e b a d", e=NE, b=P)

    stack = ExitStack()
    sem_names = ["qd", "gd", "v", "s", "t", "g", "cc", "wa", "wb", "dfa", "q0", "q1", "q2", "q3"]
    sems = {n: stack.enter_context(nc.semaphore(n)) for n in sem_names}
    C = {n: 0 for n in sem_names}

    DMA_SEMS = {"qd", "gd", "wa", "wb", "dfa", "q0", "q1", "q2", "q3"}

    def inc(inst, n, k=1):
        if n in DMA_SEMS:
            k = 16
        inst.then_inc(sems[n], k)
        C[n] += k
        return C[n]

    def wait(eng, n, val=None):
        eng.wait_ge(sems[n], C[n] if val is None else val)

    def wait_all(eng):
        for n in ["qd", "gd", "v", "s", "t", "g", "wa", "wb", "dfa", "cc",
                  "q0", "q1", "q2", "q3"]:
            if C[n]:
                eng.wait_ge(sems[n], C[n])

    ps = [stack.enter_context(nc.psum_tensor(f"ps{i}", [P, 512], F32)) for i in range(6)]
    pt = [stack.enter_context(nc.psum_tensor(f"pt{i}", [P, P], BF16)) for i in range(2)]

    block = stack.enter_context(nc.Block())

    def sec(eng, fn):
        getattr(block, eng)(fn)

    _name_ctr = [0]

    def sb(name, shape, dt):
        _name_ctr[0] += 1
        return nc.sbuf_tensor(f"{name}_{_name_ctr[0]}", list(shape), dt)

    g_ident = stack.enter_context(sb("g_ident", (P, P), BF16))
    g_ones1 = stack.enter_context(sb("g_ones1", (1, P), BF16))
    g_ones1f = stack.enter_context(sb("g_ones1f", (1, P), F32))
    g_onescol = stack.enter_context(sb("g_onescol", (P, 1), BF16))
    g_onescolf = stack.enter_context(sb("g_onescolf", (P, 1), F32))
    g_ones128 = stack.enter_context(sb("g_ones128", (P, P), BF16))
    g_triu = stack.enter_context(sb("g_triu", (P, P), BF16))
    xstr = stack.enter_context(sb("xstr", (P, 2, D), F32))
    xou = stack.enter_context(sb("xou", (P, 2, D), F32))
    st1 = stack.enter_context(sb("st1", (P, TC), F32))
    st2 = stack.enter_context(sb("st2", (P, TC), F32))
    st3 = stack.enter_context(sb("st3", (P, TC), F32))
    st4 = stack.enter_context(sb("st4", (P, TC), F32))
    rowA = stack.enter_context(sb("rowA", (1, D), F32))
    rowB = stack.enter_context(sb("rowB", (1, D), F32))
    escr = stack.enter_context(sb("escr", (P, 4, D), F32))

    def ld_consts(sy):
        for dst, src in [(g_ident, ident), (g_ones1, ones1), (g_ones1f, ones1f), (g_onescol, onescol),
                         (g_onescolf, onescolf), (g_ones128, ones128), (g_triu, triu)]:
            inst = sy.dma_start(out=dst[:, :], in_=src[:, :])
            inc(inst, "qd")
    sec("sync", ld_consts)


    def bcast_row(src_row_ap, dst_tile_ap, ncols):
        nmm = (ncols + 511) // 512
        t0 = C["t"]; v0 = C["v"]
        def tf(t):
            wait_all(t)
            for j in range(nmm):
                if j >= 2:
                    t.wait_ge(sems["v"], v0 + j - 1)
                cw = min(512, ncols - j * 512)
                inst = t.matmul(ps[4 + (j % 2)][:, 0:cw], g_ones1f[:, :],
                                src_row_ap[0:1, j * 512:j * 512 + cw], start=True, stop=True)
                inc(inst, "t")
        sec("tensor", tf)
        def vf(v):
            for j in range(nmm):
                cw = min(512, ncols - j * 512)
                v.wait_ge(sems["t"], t0 + j + 1)
                inst = v.tensor_copy(dst_tile_ap[:, j * 512:j * 512 + cw], ps[4 + (j % 2)][:, 0:cw])
                inc(inst, "v")
        sec("vector", vf)

    def load_row_bcast(src_dram_row, dst_tile_ap, ncols):
        def ld(sy):
            wait_all(sy)
            inst = sy.dma_start(out=rowB[0:1, 0:ncols], in_=src_dram_row)
            inc(inst, "qd")
        sec("sync", ld)
        bcast_row(rowB, dst_tile_ap, ncols)

    def transpose_tm_fm(src_tm, dst_fm, nt=TC, nd=DC):
        t0 = C["t"]; v0 = C["v"]
        def tf(t):
            wait_all(t)
            j = 0
            for tc in range(nt):
                for dc in range(nd):
                    if j >= 2:
                        t.wait_ge(sems["v"], v0 + j - 1)
                    inst = t.transpose(pt[j % 2][:, :], src_tm[:, tc, dc * P:(dc + 1) * P], g_ident[:, :])
                    inc(inst, "t")
                    j += 1
        sec("tensor", tf)
        def vf(v):
            j = 0
            for tc in range(nt):
                for dc in range(nd):
                    v.wait_ge(sems["t"], t0 + j + 1)
                    inst = v.tensor_copy(dst_fm[:, dc, tc * P:(tc + 1) * P], pt[j % 2][:, :])
                    inc(inst, "v")
                    j += 1
        sec("vector", vf)

    def ln_u_from_dram(xsrc, u_tm):
        qp0 = [C["q0"], C["q1"]]; v0 = C["v"]
        def dsec(sy):
            wait_all(sy)
            for tc in range(TC):
                if tc >= 2:
                    sy.wait_ge(sems["v"], v0 + (tc - 1))
                inst = sy.dma_start(out=xstr[:, tc % 2, :], in_=xsrc[:, tc, :])
                inc(inst, f"q{tc % 2}")
        sec("sync", dsec)
        def vsec(v):
            v.wait_ge(sems["q2"], C["q2"])
            v.wait_ge(sems["q3"], C["q3"])
            for tc in range(TC):
                v.wait_ge(sems[f"q{tc % 2}"], qp0[tc % 2] + 16 * (tc // 2 + 1))
                xc = xstr[:, tc % 2, :]
                v.reduce_sum(st1[:, tc:tc + 1], xc, axis=AX)
                inst = v.scalar_tensor_tensor(xou[:, tc % 2, :], xc, 1.0, xc, ALU.mult, ALU.mult,
                                              accum_out=st2[:, tc:tc + 1])
                inc(inst, "v")
        sec("vector", vsec)
        def vf(v):
            wait(v, "v")
            v.tensor_scalar(st1[:, :], st1[:, :], 1.0 / D, None, ALU.mult)
            inst = v.tensor_scalar(st2[:, :], st2[:, :], 1.0 / D, None, ALU.mult)
            inc(inst, "v")
            wait(v, "v")
            inst = v.scalar_tensor_tensor(st3[:, :], st1[:, :], 1.0, st1[:, :], ALU.mult, ALU.mult)
            inc(inst, "v")
            wait(v, "v")
            inst = v.tensor_tensor(st2[:, :], st2[:, :], st3[:, :], ALU.subtract)
            inc(inst, "v")
            wait(v, "v")
            inst = v.tensor_scalar(st2[:, :], st2[:, :], LN_EPS, None, ALU.add)
            inc(inst, "v")
        sec("vector", vf)
        def sf(sc):
            wait(sc, "v")
            inst = sc.activation(st2[:, :], st2[:, :], AF.Ln, scale=1.0)
            inc(inst, "s")
            wait(sc, "s")
            inst = sc.activation(st2[:, :], st2[:, :], AF.Exp, scale=-0.5)
            inc(inst, "s")
        sec("scalar", sf)
        qp1 = [C["q0"], C["q1"]]; v1 = C["v"]
        def dsec2(sy):
            wait(sy, "s")
            for tc in range(TC):
                if tc >= 2:
                    sy.wait_ge(sems["v"], v1 + 2 * (tc - 1))
                inst = sy.dma_start(out=xstr[:, tc % 2, :], in_=xsrc[:, tc, :])
                inc(inst, f"q{tc % 2}")
        sec("sync", dsec2)
        def vsec2(v):
            for tc in range(TC):
                v.wait_ge(sems[f"q{tc % 2}"], qp1[tc % 2] + 16 * (tc // 2 + 1))
                inst = v.tensor_scalar(st3[:, tc:tc + 1], st1[:, tc:tc + 1], -1.0,
                                       st2[:, tc:tc + 1], ALU.mult, ALU.mult)
                inc(inst, "v")
                v.wait_ge(sems["v"], C["v"])
                inst = v.tensor_scalar(u_tm[:, tc, :], xstr[:, tc % 2, :], st2[:, tc:tc + 1],
                                       st3[:, tc:tc + 1], ALU.mult, ALU.add)
                inc(inst, "v")
        sec("vector", vsec2)

    def load_w(wbuf, wT_dram, ncols, col_off=0, total_cols=None):
        def g(gp):
            wait_all(gp)
            src = wT_dram.rearrange("(a b) n -> b a n", b=P)
            src = src[:, :, col_off:col_off + ncols]
            inst = gp.dma_start(out=wbuf[:, :DC, 0:ncols], in_=src)
            inc(inst, "gd")
        sec("gpsimd", g)

    def proj_tm(u_fm, wbuf, dst_tm, brep=None, ncols=D, kc_n=DC, out_off=0):
        nh = (ncols + 511) // 512
        t0 = C["t"]; v0 = C["v"]
        def tf(t):
            wait_all(t)
            j = 0
            for tc in range(TC):
                for h in range(nh):
                    if j >= 4:
                        t.wait_ge(sems["v"], v0 + j - 3)
                    cw = min(512, ncols - h * 512)
                    for kc in range(kc_n):
                        inst = t.matmul(ps[j % 4][:, 0:cw], u_fm[:, kc, tc * P:(tc + 1) * P],
                                        wbuf[:, kc, h * 512:h * 512 + cw],
                                        start=(kc == 0), stop=(kc == kc_n - 1))
                    inc(inst, "t")
                    j += 1
        sec("tensor", tf)
        def vf(v):
            j = 0
            for tc in range(TC):
                for h in range(nh):
                    v.wait_ge(sems["t"], t0 + j + 1)
                    cw = min(512, ncols - h * 512)
                    o0 = out_off + h * 512
                    dstap = dst_tm[:, tc, o0:o0 + cw]
                    if brep is not None:
                        inst = v.tensor_tensor(dstap, ps[j % 4][:, 0:cw],
                                               brep[:, o0:o0 + cw], ALU.add)
                    else:
                        inst = v.tensor_copy(dstap, ps[j % 4][:, 0:cw])
                    inc(inst, "v")
                    j += 1
        sec("vector", vf)

    def proj_fm(x_fm, wbuf, dst_fm, mc_n, kc_n=DC, wcol_of_mc=lambda mc: None, act=None, bias_col=None):
        t0 = C["t"]
        ekey = "v" if act is None else "s"
        e0 = C[ekey]
        def tf(t):
            wait_all(t)
            j = 0
            for mc in range(mc_n):
                for tcol in range(4):
                    if j >= 4:
                        t.wait_ge(sems[ekey], e0 + j - 3)
                    for kc in range(kc_n):
                        inst = t.matmul(ps[j % 4][:, :], wbuf[:, kc, mc * P:(mc + 1) * P],
                                        x_fm[:, kc, tcol * 512:(tcol + 1) * 512],
                                        start=(kc == 0), stop=(kc == kc_n - 1))
                    inc(inst, "t")
                    j += 1
        sec("tensor", tf)
        def ef(e):
            j = 0
            for mc in range(mc_n):
                for tcol in range(4):
                    e.wait_ge(sems["t"], t0 + j + 1)
                    dstap = dst_fm[:, mc, tcol * 512:(tcol + 1) * 512]
                    if act is None:
                        inst = e.tensor_copy(dstap, ps[j % 4][:, :])
                    else:
                        if bias_col is not None:
                            inst = e.activation(dstap, ps[j % 4][:, :], act, bias=bias_col[:, mc:mc + 1])
                        else:
                            inst = e.activation(dstap, ps[j % 4][:, :], act)
                    inc(inst, ekey)
                    j += 1
        sec("vector" if act is None else "scalar", ef)

    def plain_ln_epilogue(psrc_of_tc, src, dst, extra_brep=None, entry_snap=None):
        v0 = C["v"]; s0 = C["s"]
        qp0 = [C["q0"], C["q1"]]; sp0 = [C["q2"], C["q3"]]
        vmark = []
        KV = 9
        def dsec(sy):
            snapw = entry_snap or dict(C)
            for n, val in snapw.items():
                if n in sems and val:
                    sy.wait_ge(sems[n], val)
            for tc in range(TC):
                if tc >= 2:
                    sy.wait_ge(sems["v"], v0 + KV * (tc - 1))
                inst = sy.dma_start(out=xstr[:, tc % 2, :], in_=src[:, tc, :])
                inc(inst, f"q{tc % 2}")
        sec("sync", dsec)
        def vf(v):
            for tc in range(TC):
                plo, phi = psrc_of_tc(tc)
                v.wait_ge(sems[f"q{tc % 2}"], qp0[tc % 2] + 16 * (tc // 2 + 1))
                v.wait_ge(sems["t"], tmark_ref[0][tc])
                if tc >= 2:
                    v.wait_ge(sems[f"q{2 + tc % 2}"], sp0[tc % 2] + 16 * ((tc - 2) // 2 + 1))
                else:
                    v.wait_ge(sems[f"q{2 + tc % 2}"], sp0[tc % 2])
                if extra_brep is not None:
                    v.tensor_tensor(escr[:, 1, 0:512], plo, extra_brep[:, 0:512], ALU.add)
                    inst = v.tensor_tensor(escr[:, 1, 512:1024], phi, extra_brep[:, 512:1024], ALU.add)
                else:
                    v.tensor_copy(escr[:, 1, 0:512], plo)
                    inst = v.tensor_copy(escr[:, 1, 512:1024], phi)
                inc(inst, "v")
                v.wait_ge(sems["v"], C["v"])
                v.reduce_sum(st3[:, tc:tc + 1], escr[:, 1, :], axis=AX)
                inst = v.scalar_tensor_tensor(escr[:, 2, :], escr[:, 1, :], 1.0, escr[:, 1, :],
                                              ALU.mult, ALU.mult, accum_out=st4[:, tc:tc + 1])
                inc(inst, "v")
                v.wait_ge(sems["v"], C["v"])
                v.tensor_scalar(st3[:, tc:tc + 1], st3[:, tc:tc + 1], 1.0 / D, None, ALU.mult)
                inst = v.tensor_scalar(st4[:, tc:tc + 1], st4[:, tc:tc + 1], 1.0 / D, None, ALU.mult)
                inc(inst, "v")
                v.wait_ge(sems["v"], C["v"])
                inst = v.scalar_tensor_tensor(escr[:, 3, 0:1], st3[:, tc:tc + 1], 1.0,
                                              st3[:, tc:tc + 1], ALU.mult, ALU.mult)
                inc(inst, "v")
                v.wait_ge(sems["v"], C["v"])
                inst = v.tensor_scalar(st4[:, tc:tc + 1], escr[:, 3, 0:1], -1.0, st4[:, tc:tc + 1],
                                       ALU.mult, ALU.add)
                inc(inst, "v")
                v.wait_ge(sems["v"], C["v"])
                inst = v.tensor_scalar(st4[:, tc:tc + 1], st4[:, tc:tc + 1], LN_EPS, None, ALU.add)
                inc(inst, "v")
                v.wait_ge(sems["s"], s0 + 2 * (tc + 1))
                inst = v.tensor_scalar(escr[:, 3, 0:1], st3[:, tc:tc + 1], -1.0, st4[:, tc:tc + 1],
                                       ALU.mult, ALU.mult)
                inc(inst, "v")
                v.wait_ge(sems["v"], C["v"])
                inst = v.tensor_scalar(escr[:, 0, :], escr[:, 1, :], st4[:, tc:tc + 1],
                                       escr[:, 3, 0:1], ALU.mult, ALU.add)
                inc(inst, "v")
                v.wait_ge(sems["v"], C["v"])
                inst = v.tensor_tensor(xou[:, tc % 2, :], xstr[:, tc % 2, :], escr[:, 0, :], ALU.add)
                inc(inst, "v")
                vmark.append(v0 + KV * tc + 1)
        sec("vector", vf)
        def sf(sc):
            for tc in range(TC):
                sc.wait_ge(sems["v"], v0 + KV * tc + 6)
                inst = sc.activation(st4[:, tc:tc + 1], st4[:, tc:tc + 1], AF.Ln, scale=1.0)
                inc(inst, "s")
                sc.wait_ge(sems["s"], C["s"])
                inst = sc.activation(st4[:, tc:tc + 1], st4[:, tc:tc + 1], AF.Exp, scale=-0.5)
                inc(inst, "s")
        sec("scalar", sf)
        def osec(gp):
            for tc in range(TC):
                gp.wait_ge(sems["v"], v0 + KV * (tc + 1))
                inst = gp.dma_start(out=dst[:, tc, :], in_=xou[:, tc % 2, :])
                inc(inst, f"q{2 + tc % 2}")
        sec("gpsimd", osec)
        return v0, KV

    tmark_ref = [{}]
    qd_out_ref = [0]

    def proj_tm_epilogue(in_fm, wbuf, src, dst, extra_brep, kc_n=DC):
        entry_snap = dict(C)
        t0 = C["t"]
        tmark = {}
        v0_expected = C["v"]
        KV = 9
        def tf(t):
            wait_all(t)
            j = 0
            for tc in range(TC):
                for h in range(2):
                    if j >= 4:
                        t.wait_ge(sems["v"], v0_expected + KV * (tc - 2) + 1)
                    for kc in range(kc_n):
                        inst = t.matmul(ps[j % 4][:, :], in_fm[:, kc, tc * P:(tc + 1) * P],
                                        wbuf[:, kc, h * 512:(h + 1) * 512],
                                        start=(kc == 0), stop=(kc == kc_n - 1))
                    inc(inst, "t")
                    j += 1
                tmark[tc] = C["t"]
        sec("tensor", tf)
        tmark_ref[0] = tmark
        def psrc(tc):
            return ps[(2 * tc) % 4][:, :], ps[(2 * tc + 1) % 4][:, :]
        plain_ln_epilogue(psrc, src, dst, extra_brep=extra_brep, entry_snap=entry_snap)

    def block_l0_attn(src, dst):
        with (
            sb("QM", (P, FC, D), BF16) as QM,
            sb("Vc", (P, FC, D), BF16) as Vc,
            sb("Vs", (P, FC, D), BF16) as Vs,
            sb("scrA", (P, 2 * D), F32) as scrA,
            sb("scrB", (P, 2 * D), F32) as scrB,
            sb("repF", (P, D), F32) as repF,
        ):
            scrS = escr[:, 0:2, :]
            def fwd_dft(src_tm, kind, out_c, out_s=None):
                with sb("dfc", (P, TC, P), BF16) as dfc, sb("dfs", (P, TC, P), BF16) as dfs:
                    df0 = C["dfa"]; t0 = C["t"]; s0 = C["s"]; v0 = C["v"]
                    def dload(sy):
                        wait_all(sy)
                        for fc in range(FC):
                            if fc >= 1:
                                sy.wait_ge(sems["t"], t0 + fc)
                            i1 = sy.dma_start(out=dfc[:, :, :], in_=CFr[:, :, fc * P:(fc + 1) * P])
                            inc(i1, "dfa")
                            i2 = sy.dma_start(out=dfs[:, :, :], in_=SFr[:, :, fc * P:(fc + 1) * P])
                            inc(i2, "dfa")
                    sec("sync", dload)
                    def tf(t):
                        wait(t, "v"); wait(t, "s")
                        for fc in range(FC):
                            t.wait_ge(sems["dfa"], df0 + 32 * (fc + 1))
                            if fc >= 1:
                                if kind == "mag":
                                    t.wait_ge(sems["s"], s0 + 2 * fc - 1)
                                else:
                                    t.wait_ge(sems["v"], v0 + fc)
                            for h in range(2):
                                for i in range(TC):
                                    t.matmul(ps[2 * h][:, :], dfc[:, i, :],
                                             src_tm[:, i, h * 512:(h + 1) * 512],
                                             start=(i == 0), stop=(i == TC - 1))
                                for i in range(TC):
                                    inst = t.matmul(ps[2 * h + 1][:, :], dfs[:, i, :],
                                                    src_tm[:, i, h * 512:(h + 1) * 512],
                                                    start=(i == 0), stop=(i == TC - 1))
                            inc(inst, "t")
                    sec("tensor", tf)
                    if kind == "copy":
                        def vf(v):
                            for fc in range(FC):
                                v.wait_ge(sems["t"], t0 + fc + 1)
                                v.tensor_copy(out_c[:, fc, 0:512], ps[0][:, :])
                                v.tensor_copy(out_s[:, fc, 0:512], ps[1][:, :])
                                v.tensor_copy(out_c[:, fc, 512:1024], ps[2][:, :])
                                inst = v.tensor_copy(out_s[:, fc, 512:1024], ps[3][:, :])
                                inc(inst, "v")
                        sec("vector", vf)
                    else:
                        def sf(sc):
                            for fc in range(FC):
                                p_ = fc % 2
                                sc.wait_ge(sems["t"], t0 + fc + 1)
                                if fc >= 2:
                                    sc.wait_ge(sems["v"], v0 + fc - 1)
                                sc.activation(scrA[:, p_ * D:p_ * D + 512], ps[0][:, :], AF.Square)
                                sc.activation(scrB[:, p_ * D:p_ * D + 512], ps[1][:, :], AF.Square)
                                sc.activation(scrA[:, p_ * D + 512:(p_ + 1) * D], ps[2][:, :], AF.Square)
                                inst = sc.activation(scrB[:, p_ * D + 512:(p_ + 1) * D], ps[3][:, :], AF.Square)
                                inc(inst, "s")
                                sc.wait_ge(sems["v"], v0 + fc + 1)
                                inst = sc.activation(out_c[:, fc, :], escr[:, p_, :], AF.Ln, scale=1.0)
                                inc(inst, "s")
                        sec("scalar", sf)
                        def vf(v):
                            for fc in range(FC):
                                p_ = fc % 2
                                v.wait_ge(sems["s"], s0 + 2 * fc + 1)
                                inst = v.scalar_tensor_tensor(escr[:, p_, :], scrA[:, p_ * D:(p_ + 1) * D],
                                                              1e-30, scrB[:, p_ * D:(p_ + 1) * D],
                                                              ALU.add, ALU.add)
                                inc(inst, "v")
                        sec("vector", vf)

            with sb("u_fm", (P, DC, T), BF16) as u_fm:
                with sb("u_tm", (P, TC, D), BF16) as u_tm:
                    ln_u_from_dram(src, u_tm)
                    transpose_tm_fm(u_tm, u_fm)
                def ldff(sy):
                    wait_all(sy)
                    inst = sy.dma_start(out=rowA[0:1, 0:D], in_=ffin[0:1, :])
                    inc(inst, "qd")
                sec("sync", ldff)
                def sqff(v):
                    wait(v, "qd")
                    inst = v.tensor_tensor(rowA[0:1, 0:D], rowA[0:1, 0:D], rowA[0:1, 0:D], ALU.mult)
                    inc(inst, "v")
                sec("vector", sqff)
                bcast_row(rowA, repF, D)

                with (
                    sb("KM", (P, FC, D), BF16) as KM,
                    sb("tmb", (P, TC, D), BF16) as tmb,
                ):
                    brep = xou[:, 0, :]
                    for wT, brow, oc in [(wqT, 0, QM), (wkT, 1, KM)]:
                        load_row_bcast(bqkvo[brow:brow + 1, :], brep, D)
                        with sb(f"wbuf{brow}", (P, DC, 512), BF16) as wbuf:
                            for hh in range(2):
                                load_w(wbuf, wT, 512, col_off=hh * 512)
                                proj_tm(u_fm, wbuf, tmb, brep=brep, ncols=512, out_off=hh * 512)
                        fwd_dft(tmb, "mag", oc)
                    for fc in range(FC):
                        p_ = fc % 2
                        def v1(v, fc=fc, p_=p_):
                            if fc == 0:
                                wait_all(v)
                            wait(v, "s")
                            inst = v.tensor_tensor(escr[:, p_, :], QM[:, fc, :], KM[:, fc, :], ALU.add)
                            inc(inst, "v")
                        sec("vector", v1)
                        def s1(sc, fc=fc, p_=p_):
                            wait(sc, "v")
                            inst = sc.activation(scrA[:, p_ * D:(p_ + 1) * D], escr[:, p_, :], AF.Exp, scale=0.5)
                            inc(inst, "s")
                        sec("scalar", s1)
                        def v2(v, fc=fc, p_=p_):
                            wait(v, "s")
                            inst = v.tensor_tensor(QM[:, fc, :], scrA[:, p_ * D:(p_ + 1) * D], repF[:, :], ALU.mult)
                            inc(inst, "v")
                        sec("vector", v2)
                with sb("tmbv", (P, TC, D), BF16) as tmbv:
                    load_row_bcast(bqkvo[2:3, :], repF, D)
                    with sb("wbufv", (P, DC, 512), BF16) as wbufv:
                        for hh in range(2):
                            load_w(wbufv, wvT, 512, col_off=hh * 512)
                            proj_tm(u_fm, wbufv, tmbv, brep=repF, ncols=512, out_off=hh * 512)
                    fwd_dft(tmbv, "copy", Vc, Vs)

            with (
                sb("attn", (P, TC, D), BF16) as attn,
            ):
                Oc, Os = Vc, Vs
                with sb("ci1", (P, FC, P), BF16) as ci1:
                    df0 = C["dfa"]
                    t0 = C["t"]; v0 = C["v"]; s0 = C["s"]
                    tmk = {scn: (t0 + 1 if scn == 0 else t0 + 3 + scn) for scn in range(TC)}
                    vfree = {scn: v0 + 2 + scn + 1 for scn in range(TC)}
                    def dload(sy):
                        wait_all(sy)
                        for scn in range(TC):
                            if scn >= 1:
                                sy.wait_ge(sems["t"], tmk[scn - 1])
                            inst = sy.dma_start(out=ci1[:, :, :],
                                                in_=CIr[:, :, scn * P:(scn + 1) * P])
                            inc(inst, "dfa")
                    sec("sync", dload)
                    def tf(t):
                        wait(t, "v"); wait(t, "s")
                        for scn in range(TC):
                            t.wait_ge(sems["dfa"], df0 + 16 * (scn + 1))
                            if scn == 1:
                                t.wait_ge(sems["v"], v0 + 1)
                            if scn >= 2:
                                t.wait_ge(sems["v"], vfree[scn - 2])
                            pb = 2 * (scn % 2)
                            for h in range(2):
                                for fc in range(FC):
                                    inst = t.matmul(ps[pb + h][:, :], ci1[:, fc, :],
                                                    QM[:, fc, h * 512:(h + 1) * 512],
                                                    start=(fc == 0), stop=(fc == FC - 1))
                            inc(inst, "t")
                            assert C["t"] == tmk[scn], (scn, C["t"], tmk[scn])
                            if scn == 0:
                                t.wait_ge(sems["v"], v0 + 1)
                                for j in range(2):
                                    inst = t.matmul(ps[4 + j][:, :], g_ones1f[:, :],
                                                    rowA[0:1, j * 512:(j + 1) * 512],
                                                    start=True, stop=True)
                                    inc(inst, "t")
                    sec("tensor", tf)
                    def vf(v):
                        v.wait_ge(sems["t"], t0 + 1)
                        v.tensor_copy(rowA[0:1, 0:512], ps[0][0:1, :])
                        inst = v.tensor_copy(rowA[0:1, 512:1024], ps[1][0:1, :])
                        inc(inst, "v")
                        v.wait_ge(sems["t"], t0 + 3)
                        v.tensor_copy(repF[:, 0:512], ps[4][:, :])
                        inst = v.tensor_copy(repF[:, 512:1024], ps[5][:, :])
                        inc(inst, "v")
                        v.wait_ge(sems["v"], v0 + 2)
                        for scn in range(TC):
                            v.wait_ge(sems["t"], tmk[scn])
                            if scn >= 2:
                                v.wait_ge(sems["s"], s0 + (scn - 1))
                            pb = 2 * (scn % 2)
                            p_ = scn % 2
                            v.tensor_tensor(scrA[:, p_ * D:p_ * D + 512], ps[pb][:, :],
                                            repF[:, 0:512], ALU.subtract)
                            inst = v.tensor_tensor(scrA[:, p_ * D + 512:(p_ + 1) * D], ps[pb + 1][:, :],
                                                   repF[:, 512:1024], ALU.subtract)
                            inc(inst, "v")
                    sec("vector", vf)
                    def sf(sc):
                        for scn in range(TC):
                            sc.wait_ge(sems["v"], v0 + 2 + scn + 1)
                            p_ = scn % 2
                            sc.activation(attn[:, scn, 0:512], scrA[:, p_ * D:p_ * D + 512], AF.Exp)
                            inst = sc.activation(attn[:, scn, 512:1024],
                                                 scrA[:, p_ * D + 512:(p_ + 1) * D], AF.Exp)
                            inc(inst, "s")
                    sec("scalar", sf)

                t0 = C["t"]
                def ztf(t):
                    wait_all(t)
                    for h in range(2):
                        for i in range(TC):
                            inst = t.matmul(ps[4 + h][0:1, :], g_onescol[:, :],
                                            attn[:, i, h * 512:(h + 1) * 512],
                                            start=(i == 0), stop=(i == TC - 1))
                        inc(inst, "t")
                sec("tensor", ztf)
                def zvf(v):
                    v.wait_ge(sems["t"], t0 + 2)
                    v.tensor_copy(rowA[0:1, 0:512], ps[4][0:1, :])
                    inst = v.tensor_copy(rowA[0:1, 512:1024], ps[5][0:1, :])
                    inc(inst, "v")
                    wait(v, "v")
                    inst = v.reciprocal(rowA[0:1, 0:D], rowA[0:1, 0:D])
                    inc(inst, "v")
                sec("vector", zvf)
                bcast_row(rowA, repF, D)
                def vapply(v):
                    wait(v, "v")
                    for fc in range(FC):
                        v.tensor_tensor(Vc[:, fc, :], Vc[:, fc, :], repF[:, :], ALU.mult)
                        inst = v.tensor_tensor(Vs[:, fc, :], Vs[:, fc, :], repF[:, :], ALU.mult)
                        inc(inst, "v")
                        v.wait_ge(sems["v"], C["v"])
                sec("vector", vapply)

                with sb("dfc", (P, TC, P), BF16) as dfc, sb("dfs", (P, TC, P), BF16) as dfs:
                    df0 = C["dfa"]; t0 = C["t"]; v0 = C["v"]
                    def dload(sy):
                        wait_all(sy)
                        for fc in range(FC):
                            if fc >= 1:
                                sy.wait_ge(sems["t"], t0 + fc)
                            i1 = sy.dma_start(out=dfc[:, :, :], in_=CFr[:, :, fc * P:(fc + 1) * P])
                            inc(i1, "dfa")
                            i2 = sy.dma_start(out=dfs[:, :, :], in_=SFr[:, :, fc * P:(fc + 1) * P])
                            inc(i2, "dfa")
                    sec("sync", dload)
                    def tf(t):
                        wait(t, "v"); wait(t, "s")
                        for fc in range(FC):
                            t.wait_ge(sems["dfa"], df0 + 32 * (fc + 1))
                            if fc >= 1:
                                t.wait_ge(sems["v"], v0 + 8 * fc)
                            for h in range(2):
                                for i in range(TC):
                                    t.matmul(ps[2 * h][:, :], dfc[:, i, :],
                                             attn[:, i, h * 512:(h + 1) * 512],
                                             start=(i == 0), stop=(i == TC - 1))
                                for i in range(TC):
                                    inst = t.matmul(ps[2 * h + 1][:, :], dfs[:, i, :],
                                                    attn[:, i, h * 512:(h + 1) * 512],
                                                    start=(i == 0), stop=(i == TC - 1))
                            inc(inst, "t")
                    sec("tensor", tf)
                    def vf(v):
                        for fc in range(FC):
                            v.wait_ge(sems["t"], t0 + fc + 1)
                            for h in range(2):
                                sl = slice(h * 512, (h + 1) * 512)
                                pc, psn = ps[2 * h], ps[2 * h + 1]
                                v.scalar_tensor_tensor(scrA[:, 0:512], pc[:, :], 1.0, Vc[:, fc, sl], ALU.mult, ALU.mult)
                                v.scalar_tensor_tensor(scrA[:, 512:1024], psn[:, :], 1.0, Vs[:, fc, sl], ALU.mult, ALU.mult)
                                v.scalar_tensor_tensor(scrB[:, 0:512], pc[:, :], 1.0, Vs[:, fc, sl], ALU.mult, ALU.mult)
                                inst = v.scalar_tensor_tensor(scrB[:, 512:1024], psn[:, :], 1.0, Vc[:, fc, sl], ALU.mult, ALU.mult)
                                inc(inst, "v")
                                v.wait_ge(sems["v"], C["v"])
                                v.tensor_tensor(Oc[:, fc, sl], scrA[:, 0:512], scrA[:, 512:1024], ALU.subtract)
                                inst = v.tensor_tensor(Os[:, fc, sl], scrB[:, 0:512], scrB[:, 512:1024], ALU.add)
                                inc(inst, "v", 3)
                                v.wait_ge(sems["v"], C["v"])
                    sec("vector", vf)

                with (
                    sb("out_fm", (P, DC, T), BF16) as out_fm,
                    sb("ci2", (P, FC, 512), BF16) as ci2,
                    sb("si2", (P, FC, 512), BF16) as si2,
                ):
                    df0 = C["dfa"]; t0 = C["t"]; v0 = C["v"]
                    def dload(sy):
                        wait_all(sy)
                        for scol in range(4):
                            if scol >= 1:
                                sy.wait_ge(sems["t"], t0 + 8 * scol)
                            i1 = sy.dma_start(out=ci2[:, :, :],
                                              in_=CIr[:, :, scol * 512:(scol + 1) * 512])
                            inc(i1, "dfa")
                            i2 = sy.dma_start(out=si2[:, :, :],
                                              in_=SIr[:, :, scol * 512:(scol + 1) * 512])
                            inc(i2, "dfa")
                    sec("sync", dload)
                    def tf(t):
                        wait(t, "v"); wait(t, "s")
                        j = 0
                        for scol in range(4):
                            t.wait_ge(sems["dfa"], df0 + 32 * (scol + 1))
                            for cc in range(DC):
                                if j >= 4:
                                    t.wait_ge(sems["v"], v0 + j - 3)
                                for fc in range(FC):
                                    t.matmul(ps[j % 4][:, :], Oc[:, fc, cc * P:(cc + 1) * P],
                                             ci2[:, fc, :], start=(fc == 0), stop=False)
                                for fc in range(FC):
                                    inst = t.matmul(ps[j % 4][:, :], Os[:, fc, cc * P:(cc + 1) * P],
                                                    si2[:, fc, :], start=False, stop=(fc == FC - 1))
                                inc(inst, "t")
                                j += 1
                    sec("tensor", tf)
                    def vf(v):
                        j = 0
                        for scol in range(4):
                            for cc in range(DC):
                                v.wait_ge(sems["t"], t0 + j + 1)
                                inst = v.tensor_copy(out_fm[:, cc, scol * 512:(scol + 1) * 512],
                                                     ps[j % 4][:, :])
                                inc(inst, "v")
                                j += 1
                    sec("vector", vf)

                    if True:
                        wbuf2 = attn[:, 0:DC, :]
                        borep = scrB[:, 0:D]
                        grep = scrA[:, 0:D]
                        brep2 = scrA[:, D:2 * D]
                        load_row_bcast(bqkvo[3:4, :], borep, D)
                        load_row_bcast(lng0[0:1, :], grep, D)
                        load_row_bcast(lnb0[0:1, :], brep2, D)
                        load_w(wbuf2, woT, D)
                        affine_ln_epilogue(out_fm, wbuf2, src, dst, borep, grep, brep2)
        return

    def affine_ln_epilogue(in_fm, wbuf, src, dst, borep, grep, brep2):
        entry_snap = dict(C)
        t0 = C["t"]; tmark = {}
        v0 = C["v"]; s0 = C["s"]
        qp0 = [C["q0"], C["q1"]]; sp0 = [C["q2"], C["q3"]]
        KV = 21
        def tf(t):
            wait_all(t)
            j = 0
            for tc in range(TC):
                for h in range(2):
                    if tc >= 2 and h == 0:
                        t.wait_ge(sems["v"], v0 + KV * (tc - 2) + 4)
                    for kc in range(DC):
                        inst = t.matmul(ps[j % 4][:, :], in_fm[:, kc, tc * P:(tc + 1) * P],
                                        wbuf[:, kc, h * 512:(h + 1) * 512],
                                        start=(kc == 0), stop=(kc == DC - 1))
                    inc(inst, "t")
                    j += 1
                tmark[tc] = C["t"]
        sec("tensor", tf)
        def dsec(sy):
            for n, val in entry_snap.items():
                if n in sems and val:
                    sy.wait_ge(sems[n], val)
            for tc in range(TC):
                if tc >= 2:
                    sy.wait_ge(sems["v"], v0 + KV * (tc - 1))
                inst = sy.dma_start(out=xstr[:, tc % 2, :], in_=src[:, tc, :])
                inc(inst, f"q{tc % 2}")
        sec("sync", dsec)
        def vf(v):
            for tc in range(TC):
                v.wait_ge(sems[f"q{tc % 2}"], qp0[tc % 2] + 16 * (tc // 2 + 1))
                v.wait_ge(sems["t"], tmark[tc])
                if tc >= 2:
                    v.wait_ge(sems[f"q{2 + tc % 2}"], sp0[tc % 2] + 16 * ((tc - 2) // 2 + 1))
                else:
                    v.wait_ge(sems[f"q{2 + tc % 2}"], sp0[tc % 2])
                xc = xstr[:, tc % 2, :]
                inst = v.tensor_scalar(st3[:, tc:tc + 1], st1[:, tc:tc + 1], -1.0,
                                       st2[:, tc:tc + 1], ALU.mult, ALU.mult)
                inc(inst, "v")
                v.wait_ge(sems["v"], C["v"])
                inst = v.tensor_scalar(escr[:, 0, :], xc, st2[:, tc:tc + 1],
                                       st3[:, tc:tc + 1], ALU.mult, ALU.add)
                inc(inst, "v")
                v.wait_ge(sems["v"], C["v"])
                v.tensor_tensor(escr[:, 1, 0:512], ps[(2 * tc) % 4][:, :], borep[:, 0:512], ALU.add)
                inst = v.tensor_tensor(escr[:, 1, 512:1024], ps[(2 * tc + 1) % 4][:, :],
                                       borep[:, 512:1024], ALU.add)
                inc(inst, "v")
                v.wait_ge(sems["v"], C["v"])
                inst = v.tensor_tensor(escr[:, 1, :], escr[:, 1, :], escr[:, 0, :], ALU.add)
                inc(inst, "v")
                v.wait_ge(sems["v"], C["v"])
                v.reduce_sum(st3[:, tc:tc + 1], escr[:, 1, :], axis=AX)
                inst = v.scalar_tensor_tensor(escr[:, 2, :], escr[:, 1, :], 1.0, escr[:, 1, :],
                                              ALU.mult, ALU.mult, accum_out=st4[:, tc:tc + 1])
                inc(inst, "v")
                v.wait_ge(sems["v"], C["v"])
                v.tensor_scalar(st3[:, tc:tc + 1], st3[:, tc:tc + 1], 1.0 / D, None, ALU.mult)
                inst = v.tensor_scalar(st4[:, tc:tc + 1], st4[:, tc:tc + 1], 1.0 / D, None, ALU.mult)
                inc(inst, "v")
                v.wait_ge(sems["v"], C["v"])
                inst = v.scalar_tensor_tensor(escr[:, 3, 0:1], st3[:, tc:tc + 1], 1.0,
                                              st3[:, tc:tc + 1], ALU.mult, ALU.mult)
                inc(inst, "v")
                v.wait_ge(sems["v"], C["v"])
                inst = v.tensor_scalar(st4[:, tc:tc + 1], escr[:, 3, 0:1], -1.0, st4[:, tc:tc + 1],
                                       ALU.mult, ALU.add)
                inc(inst, "v")
                v.wait_ge(sems["v"], C["v"])
                inst = v.tensor_scalar(st4[:, tc:tc + 1], st4[:, tc:tc + 1], LN_EPS, None, ALU.add)
                inc(inst, "v")
                v.wait_ge(sems["s"], s0 + 4 * tc + 2)
                inst = v.tensor_scalar(escr[:, 3, 0:1], st3[:, tc:tc + 1], -1.0, st4[:, tc:tc + 1],
                                       ALU.mult, ALU.mult)
                inc(inst, "v")
                v.wait_ge(sems["v"], C["v"])
                inst = v.tensor_scalar(escr[:, 2, :], escr[:, 1, :], st4[:, tc:tc + 1],
                                       escr[:, 3, 0:1], ALU.mult, ALU.add)
                inc(inst, "v")
                v.wait_ge(sems["v"], C["v"])
                inst = v.tensor_tensor(escr[:, 2, :], escr[:, 2, :], grep[:, :], ALU.mult)
                inc(inst, "v")
                v.wait_ge(sems["v"], C["v"])
                inst = v.tensor_tensor(escr[:, 2, :], escr[:, 2, :], brep2[:, :], ALU.add)
                inc(inst, "v")
                v.wait_ge(sems["v"], C["v"])
                v.reduce_sum(st3[:, tc:tc + 1], escr[:, 2, :], axis=AX)
                inst = v.scalar_tensor_tensor(escr[:, 1, :], escr[:, 2, :], 1.0, escr[:, 2, :],
                                              ALU.mult, ALU.mult, accum_out=st4[:, tc:tc + 1])
                inc(inst, "v")
                v.wait_ge(sems["v"], C["v"])
                v.tensor_scalar(st3[:, tc:tc + 1], st3[:, tc:tc + 1], 1.0 / D, None, ALU.mult)
                inst = v.tensor_scalar(st4[:, tc:tc + 1], st4[:, tc:tc + 1], 1.0 / D, None, ALU.mult)
                inc(inst, "v")
                v.wait_ge(sems["v"], C["v"])
                inst = v.scalar_tensor_tensor(escr[:, 3, 0:1], st3[:, tc:tc + 1], 1.0,
                                              st3[:, tc:tc + 1], ALU.mult, ALU.mult)
                inc(inst, "v")
                v.wait_ge(sems["v"], C["v"])
                inst = v.tensor_scalar(st4[:, tc:tc + 1], escr[:, 3, 0:1], -1.0, st4[:, tc:tc + 1],
                                       ALU.mult, ALU.add)
                inc(inst, "v")
                v.wait_ge(sems["v"], C["v"])
                inst = v.tensor_scalar(st4[:, tc:tc + 1], st4[:, tc:tc + 1], LN_EPS, None, ALU.add)
                inc(inst, "v")
                v.wait_ge(sems["s"], s0 + 4 * tc + 4)
                inst = v.tensor_scalar(escr[:, 3, 0:1], st3[:, tc:tc + 1], -1.0, st4[:, tc:tc + 1],
                                       ALU.mult, ALU.mult)
                inc(inst, "v")
                v.wait_ge(sems["v"], C["v"])
                inst = v.tensor_scalar(escr[:, 0, :], escr[:, 2, :], st4[:, tc:tc + 1],
                                       escr[:, 3, 0:1], ALU.mult, ALU.add)
                inc(inst, "v")
                v.wait_ge(sems["v"], C["v"])
                inst = v.tensor_tensor(xou[:, tc % 2, :], xstr[:, tc % 2, :], escr[:, 0, :], ALU.add)
                inc(inst, "v")
        sec("vector", vf)
        def sf(sc):
            for tc in range(TC):
                sc.wait_ge(sems["v"], v0 + KV * tc + 9)
                inst = sc.activation(st4[:, tc:tc + 1], st4[:, tc:tc + 1], AF.Ln, scale=1.0)
                inc(inst, "s")
                sc.wait_ge(sems["s"], C["s"])
                inst = sc.activation(st4[:, tc:tc + 1], st4[:, tc:tc + 1], AF.Exp, scale=-0.5)
                inc(inst, "s")
                sc.wait_ge(sems["v"], v0 + KV * tc + 18)
                inst = sc.activation(st4[:, tc:tc + 1], st4[:, tc:tc + 1], AF.Ln, scale=1.0)
                inc(inst, "s")
                sc.wait_ge(sems["s"], C["s"])
                inst = sc.activation(st4[:, tc:tc + 1], st4[:, tc:tc + 1], AF.Exp, scale=-0.5)
                inc(inst, "s")
        sec("scalar", sf)
        def osec(gp):
            for tc in range(TC):
                gp.wait_ge(sems["v"], v0 + KV * (tc + 1))
                inst = gp.dma_start(out=dst[:, tc, :], in_=xou[:, tc % 2, :])
                inc(inst, f"q{2 + tc % 2}")
        sec("gpsimd", osec)

    def block_moe(src, dst):
        with (
            sb("z_tm", (P, TC, D), BF16) as z_tm,
            sb("SL32", (P, TC), I32) as SL32,
            sb("CUM", (P, TC, NE), F32) as CUM,
            sb("MKb", (P, TC, NE), BF16) as MKb,
            sb("LG", (P, TC, NE), F32) as LG,
            sb("BREP", (P, TC * NE), F32) as BREP,
            sb("ECREP", (P, NE), F32) as ECREP,
            sb("mscr", (P, 2 * D), F32) as mscr,
        ):
            ln_u_from_dram(src, z_tm)

            with sb("u_fm", (P, DC, T), BF16) as u_fm, sb("cb", (P, DC, NE), BF16) as cb:
                transpose_tm_fm(z_tm, u_fm)
                def ldc(gp):
                    wait_all(gp)
                    inst = gp.dma_start(out=cb[:, :, :], in_=centT.rearrange("(a b) n -> b a n", b=P))
                    inc(inst, "gd")
                sec("gpsimd", ldc)
                t0 = C["t"]; v0 = C["v"]
                def tf(t):
                    wait_all(t)
                    for tc in range(TC):
                        if tc >= 2:
                            t.wait_ge(sems["v"], v0 + tc - 1)
                        for kc in range(DC):
                            inst = t.matmul(ps[tc % 2][:, 0:NE], u_fm[:, kc, tc * P:(tc + 1) * P],
                                            cb[:, kc, :], start=(kc == 0), stop=(kc == DC - 1))
                        inc(inst, "t")
                sec("tensor", tf)
                def vf(v):
                    for tc in range(TC):
                        v.wait_ge(sems["t"], t0 + tc + 1)
                        inst = v.tensor_copy(LG[:, tc, :], ps[tc % 2][:, 0:NE])
                        inc(inst, "v")
                sec("vector", vf)

            def vm(v):
                wait(v, "v")
                for tc in range(TC):
                    inst = v.reduce_max(st3[:, tc:tc + 1], LG[:, tc, :], axis=AX)
                    inc(inst, "v")
                    v.wait_ge(sems["v"], C["v"])
                    inst = v.tensor_scalar(MKb[:, tc, :], LG[:, tc, :], st3[:, tc:tc + 1], None, ALU.is_ge)
                    inc(inst, "v")
                    v.wait_ge(sems["v"], C["v"])
            sec("vector", vm)

            t0 = C["t"]; v0 = C["v"]
            def tcum(t):
                wait_all(t)
                for tc in range(TC):
                    if tc >= 2:
                        t.wait_ge(sems["v"], v0 + 2 * (tc - 1))
                    t.matmul(ps[2 + tc % 2][:, 0:NE], g_triu[:, :], MKb[:, tc, :],
                             start=True, stop=True)
                    inst = t.matmul(ps[4 + tc % 2][0:1, 0:NE], g_onescol[:, :], MKb[:, tc, :],
                                    start=True, stop=True)
                    inc(inst, "t")
            sec("tensor", tcum)
            def vcum(v):
                for tc in range(TC):
                    v.wait_ge(sems["t"], t0 + tc + 1)
                    inst = v.tensor_copy(CUM[:, tc, :], ps[2 + tc % 2][:, 0:NE])
                    inc(inst, "v")
                    inst = v.tensor_copy(rowA[0:1, tc * NE:(tc + 1) * NE], ps[4 + tc % 2][0:1, 0:NE])
                    inc(inst, "v")
            sec("vector", vcum)
            def ldec(sy):
                wait_all(sy)
                inst = sy.dma_start(out=rowB[0:1, 128 + 0:128 + NE], in_=ecapd[0:1, :])
                inc(inst, "qd")
            sec("sync", ldec)
            def vpre(v):
                wait(v, "qd"); wait(v, "v")
                inst = v.memset(rowB[0:1, 0:NE], 0.0)
                inc(inst, "v")
                for tc in range(1, TC):
                    v.wait_ge(sems["v"], C["v"])
                    inst = v.tensor_tensor(rowB[0:1, tc * NE:(tc + 1) * NE],
                                           rowB[0:1, (tc - 1) * NE:tc * NE],
                                           rowA[0:1, (tc - 1) * NE:tc * NE], ALU.add)
                    inc(inst, "v")
                v.wait_ge(sems["v"], C["v"])
                inst = v.tensor_scalar(rowB[0:1, 0:TC * NE], rowB[0:1, 0:TC * NE], -1.0, None, ALU.add)
                inc(inst, "v")
            sec("vector", vpre)
            bcast_row(rowB, BREP, TC * NE)
            def vpre2(v):
                wait(v, "v")
                inst = v.tensor_copy(rowA[0:1, 0:NE], rowB[0:1, 128:128 + NE])
                inc(inst, "v")
            sec("vector", vpre2)
            bcast_row(rowA, ECREP, NE)

            def vslot(v):
                wait(v, "v")
                for tc in range(TC):
                    inst = v.tensor_tensor(mscr[:, 0:NE], CUM[:, tc, :],
                                           BREP[:, tc * NE:(tc + 1) * NE], ALU.add)
                    inc(inst, "v")
                    v.wait_ge(sems["v"], C["v"])
                    inst = v.tensor_scalar(mscr[:, 0:NE], mscr[:, 0:NE], float(CAP - 1), None, ALU.min)
                    inc(inst, "v")
                    v.wait_ge(sems["v"], C["v"])
                    inst = v.tensor_tensor(mscr[:, 0:NE], mscr[:, 0:NE], ECREP[:, :], ALU.add)
                    inc(inst, "v")
                    v.wait_ge(sems["v"], C["v"])
                    inst = v.tensor_tensor(mscr[:, NE:2 * NE], mscr[:, 0:NE], MKb[:, tc, :], ALU.mult)
                    inc(inst, "v")
                    v.wait_ge(sems["v"], C["v"])
                    inst = v.reduce_sum(mscr[:, 2 * NE:2 * NE + 1], mscr[:, NE:2 * NE], axis=AX)
                    inc(inst, "v")
                    v.wait_ge(sems["v"], C["v"])
                    inst = v.tensor_scalar(mscr[:, 2 * NE:2 * NE + 1], mscr[:, 2 * NE:2 * NE + 1],
                                           float(NSLOT - 1), None, ALU.min)
                    inc(inst, "v")
                    v.wait_ge(sems["v"], C["v"])
                    inst = v.tensor_copy(SL32[:, tc:tc + 1], mscr[:, 2 * NE:2 * NE + 1])
                    inc(inst, "v")
                    v.wait_ge(sems["v"], C["v"])
            sec("vector", vslot)

            with sb("zzero", (P, 4, D), BF16) as zzero:
                def vz(v):
                    wait(v, "v")
                    inst = v.memset(zzero[:, :, :], 0.0)
                    inc(inst, "v")
                sec("vector", vz)
                def zfill(sy):
                    wait(sy, "v")
                    for e in range(NE):
                        inst = sy.dma_start(out=zslot_r[e], in_=zzero[:, :, :])
                        inc(inst, "qd")
                sec("sync", zfill)
                def gsc(gp):
                    wait_all(gp)
                    for tc in range(TC):
                        inst = gp.indirect_dma_start(
                            out=zslot_d,
                            out_offset=bass.IndirectOffsetOnAxis(ap=SL32[:, tc:tc + 1], axis=0),
                            in_=z_tm[:, tc, :], in_offset=None)
                        inc(inst, "gd")
                    wait(gp, "gd")
                sec("gpsimd", gsc)

            with (
                sb("W1b", (P, 2, DC, D), BF16) as W1b,
                sb("W2b", (P, 2, DC, D), BF16) as W2b,
                sb("xn", (P, DC, CAP), BF16) as xn,
                sb("hq", (P, DC, CAP), BF16) as hq,
                sb("Y", (P, 4, D), F32) as Y,
                sb("zr", (P, 4, D), BF16) as zr,
                sb("db", (P, 4, D), BF16) as db,
                sb("lgc", (P, DC), F32) as lgc,
                sb("lbc", (P, DC), F32) as lbc,
                sb("b1c", (P, 32), F32) as b1c,
                sb("B2R", (P, D), F32) as B2R,
            ):
                wsem = ["wa", "wb"]
                def pre(gp):
                    wait_all(gp)
                    i1 = gp.dma_start(out=W1b[:, 0, :, :],
                                      in_=W1d[0].rearrange("(a b) f -> b a f", b=P)[:, :, 0:D])
                    inc(i1, "wa")
                    i2 = gp.dma_start(out=W2b[:, 0, :, :],
                                      in_=W2d[0][0:D, :].rearrange("(a b) f -> b a f", b=P))
                    inc(i2, "wa")
                sec("gpsimd", pre)

                for e in range(NE):
                    def lde(sy, e=e):
                        wait_all(sy)
                        i1 = sy.dma_start(out=lgc[:, :], in_=lng8[e].rearrange("(a b) -> b a", b=P),
                                          allow_slow_non_contiguous=True)
                        inc(i1, "qd")
                        i2 = sy.dma_start(out=lbc[:, :], in_=lnb8[e].rearrange("(a b) -> b a", b=P),
                                          allow_slow_non_contiguous=True)
                        inc(i2, "qd")
                        i3 = sy.dma_start(out=b1c[:, :], in_=b1d[e].rearrange("(a b) -> b a", b=P),
                                          allow_slow_non_contiguous=True)
                        inc(i3, "qd")
                        i4 = sy.dma_start(out=rowB[0:1, 0:D], in_=b2d[e:e + 1, :])
                        inc(i4, "qd")
                    sec("sync", lde)
                    bcast_row(rowB, B2R, D)
                    def gxn(sy, e=e):
                        wait_all(sy)
                        for dc in range(DC):
                            inst = sy.dma_start(out=xn[:, dc, :],
                                                in_=zslot_d[e * CAP:(e + 1) * CAP, dc * P:(dc + 1) * P],
                                                transpose=True)
                            inc(inst, "qd")
                    sec("sync", gxn)
                    def vxn(v, e=e):
                        wait(v, "qd")
                        for dc in range(DC):
                            inst = v.tensor_scalar(xn[:, dc, :], xn[:, dc, :], lgc[:, dc:dc + 1],
                                                   lbc[:, dc:dc + 1], ALU.mult, ALU.add)
                            inc(inst, "v")
                            v.wait_ge(sems["v"], C["v"])
                    sec("vector", vxn)

                    for qt in range(4):
                        buf = (e * 4 + qt) % 2
                        wcur = wsem[buf]
                        wn = (e * 4 + qt)
                        if wn + 1 < 32:
                            ne_, nq = divmod(wn + 1, 4)
                            def prefetch(gp, ne_=ne_, nq=nq, buf=buf):
                                wait(gp, "t")
                                i1 = gp.dma_start(out=W1b[:, 1 - buf, :, :],
                                                  in_=W1d[ne_].rearrange("(a b) f -> b a f", b=P)[:, :, nq * D:(nq + 1) * D])
                                inc(i1, wsem[1 - buf])
                                i2 = gp.dma_start(out=W2b[:, 1 - buf, :, :],
                                                  in_=W2d[ne_][nq * D:(nq + 1) * D, :].rearrange("(a b) f -> b a f", b=P))
                                inc(i2, wsem[1 - buf])
                            sec("gpsimd", prefetch)
                        t0 = C["t"]; s0 = C["s"]
                        def th(t, buf=buf, wcur=wcur):
                            wait(t, wcur)
                            wait(t, "v"); wait(t, "s")
                            for fc in range(DC):
                                if fc >= 2:
                                    t.wait_ge(sems["s"], s0 + fc - 1)
                                for kc in range(DC):
                                    inst = t.matmul(ps[fc % 2][:, :], W1b[:, buf, kc, fc * P:(fc + 1) * P],
                                                    xn[:, kc, :], start=(kc == 0), stop=(kc == DC - 1))
                                inc(inst, "t")
                        sec("tensor", th)
                        def sh(sc, qt=qt):
                            for fc in range(DC):
                                sc.wait_ge(sems["t"], t0 + fc + 1)
                                inst = sc.activation(hq[:, fc, :], ps[fc % 2][:, :], AF.Relu,
                                                     bias=b1c[:, qt * 8 + fc:qt * 8 + fc + 1])
                                inc(inst, "s")
                        sec("scalar", sh)
                        t1 = C["t"]; v1 = C["v"]
                        def ty(t, buf=buf):
                            wait(t, "s")
                            j = 0
                            for sc_ in range(4):
                                for h in range(2):
                                    if j >= 2:
                                        t.wait_ge(sems["v"], v1 + j - 1)
                                    for fc in range(DC):
                                        inst = t.matmul(ps[2 + j % 2][:, :], hq[:, fc, sc_ * P:(sc_ + 1) * P],
                                                        W2b[:, buf, fc, h * 512:(h + 1) * 512],
                                                        start=(fc == 0), stop=(fc == DC - 1))
                                    inc(inst, "t")
                                    j += 1
                        sec("tensor", ty)
                        def vy(v, qt=qt):
                            j = 0
                            for sc_ in range(4):
                                for h in range(2):
                                    v.wait_ge(sems["t"], t1 + j + 1)
                                    dstap = Y[:, sc_, h * 512:(h + 1) * 512]
                                    if qt == 0:
                                        inst = v.tensor_tensor(dstap, ps[2 + j % 2][:, :],
                                                               B2R[:, h * 512:(h + 1) * 512], ALU.add)
                                    else:
                                        inst = v.tensor_tensor(dstap, dstap, ps[2 + j % 2][:, :], ALU.add)
                                    inc(inst, "v")
                                    j += 1
                        sec("vector", vy)

                    def ldz(sy, e=e):
                        wait(sy, "v")
                        inst = sy.dma_start(out=zr[:, :, :], in_=zslot_r[e])
                        inc(inst, "qd")
                    sec("sync", ldz)
                    v0 = C["v"]; s0 = C["s"]
                    def vw(v, e=e):
                        wait(v, "qd")
                        for sc_ in range(4):
                            inst = v.tensor_tensor(escr[:, 0, :], Y[:, sc_, :], zr[:, sc_, :], ALU.add)
                            inc(inst, "v")
                            v.wait_ge(sems["v"], C["v"])
                            v.reduce_sum(st3[:, sc_:sc_ + 1], escr[:, 0, :], axis=AX)
                            inst = v.scalar_tensor_tensor(escr[:, 1, :], escr[:, 0, :], 1.0, escr[:, 0, :],
                                                          ALU.mult, ALU.mult, accum_out=st4[:, sc_:sc_ + 1])
                            inc(inst, "v")
                            v.wait_ge(sems["v"], C["v"])
                            v.tensor_scalar(st3[:, sc_:sc_ + 1], st3[:, sc_:sc_ + 1], 1.0 / D, None, ALU.mult)
                            inst = v.tensor_scalar(st4[:, sc_:sc_ + 1], st4[:, sc_:sc_ + 1], 1.0 / D, None, ALU.mult)
                            inc(inst, "v")
                            v.wait_ge(sems["v"], C["v"])
                            inst = v.scalar_tensor_tensor(escr[:, 3, 0:1], st3[:, sc_:sc_ + 1], 1.0,
                                                          st3[:, sc_:sc_ + 1], ALU.mult, ALU.mult)
                            inc(inst, "v")
                            v.wait_ge(sems["v"], C["v"])
                            inst = v.tensor_scalar(st4[:, sc_:sc_ + 1], escr[:, 3, 0:1], -1.0,
                                                   st4[:, sc_:sc_ + 1], ALU.mult, ALU.add)
                            inc(inst, "v")
                            v.wait_ge(sems["v"], C["v"])
                            inst = v.tensor_scalar(st4[:, sc_:sc_ + 1], st4[:, sc_:sc_ + 1], LN_EPS, None, ALU.add)
                            inc(inst, "v")
                            v.wait_ge(sems["s"], s0 + 2 * (sc_ + 1))
                            inst = v.tensor_scalar(escr[:, 3, 0:1], st3[:, sc_:sc_ + 1], -1.0,
                                                   st4[:, sc_:sc_ + 1], ALU.mult, ALU.mult)
                            inc(inst, "v")
                            v.wait_ge(sems["v"], C["v"])
                            inst = v.tensor_scalar(db[:, sc_, :], escr[:, 0, :], st4[:, sc_:sc_ + 1],
                                                   escr[:, 3, 0:1], ALU.mult, ALU.add)
                            inc(inst, "v")
                            v.wait_ge(sems["v"], C["v"])
                    sec("vector", vw)
                    def sw(sc, e=e):
                        for sc_ in range(4):
                            sc.wait_ge(sems["v"], v0 + 8 * sc_ + 6)
                            inst = sc.activation(st4[:, sc_:sc_ + 1], st4[:, sc_:sc_ + 1], AF.Ln, scale=1.0)
                            inc(inst, "s")
                            sc.wait_ge(sems["s"], C["s"])
                            inst = sc.activation(st4[:, sc_:sc_ + 1], st4[:, sc_:sc_ + 1], AF.Exp, scale=-0.5)
                            inc(inst, "s")
                    sec("scalar", sw)
                    def std(sy, e=e):
                        wait(sy, "v")
                        inst = sy.dma_start(out=dslot_r[e], in_=db[:, :, :])
                        inc(inst, "qd")
                    sec("sync", std)

            with sb("DT", (P, TC, D), BF16) as DT:
                def gug(gp):
                    wait_all(gp)
                    for tc in range(TC):
                        inst = gp.indirect_dma_start(
                            out=DT[:, tc, :], out_offset=None, in_=dslot_d,
                            in_offset=bass.IndirectOffsetOnAxis(ap=SL32[:, tc:tc + 1], axis=0))
                        inc(inst, "gd")
                    wait(gp, "gd")
                sec("gpsimd", gug)
                v0 = C["v"]
                qp0 = [C["q0"], C["q1"]]; sp0 = [C["q2"], C["q3"]]
                def dsec(sy):
                    wait_all(sy)
                    for tc in range(TC):
                        if tc >= 2:
                            sy.wait_ge(sems["v"], v0 + (tc - 1))
                        inst = sy.dma_start(out=xstr[:, tc % 2, :], in_=src[:, tc, :])
                        inc(inst, f"q{tc % 2}")
                sec("sync", dsec)
                def vf(v):
                    for tc in range(TC):
                        v.wait_ge(sems[f"q{tc % 2}"], qp0[tc % 2] + 16 * (tc // 2 + 1))
                        if tc >= 2:
                            v.wait_ge(sems[f"q{2 + tc % 2}"], sp0[tc % 2] + 16 * ((tc - 2) // 2 + 1))
                        else:
                            v.wait_ge(sems[f"q{2 + tc % 2}"], sp0[tc % 2])
                        inst = v.tensor_tensor(xou[:, tc % 2, :], xstr[:, tc % 2, :], DT[:, tc, :], ALU.add)
                        inc(inst, "v")
                sec("vector", vf)
                def osec(gp):
                    for tc in range(TC):
                        gp.wait_ge(sems["v"], v0 + tc + 1)
                        inst = gp.dma_start(out=dst[:, tc, :], in_=xou[:, tc % 2, :])
                        inc(inst, f"q{2 + tc % 2}")
                sec("gpsimd", osec)
        return

    def block_l1_attn(src, dst):
        with (
            sb("xn_fm", (P, DC, T), BF16) as xn_fm,
            sb("BN", (P, 16), F32) as BN,
            sb("A8", (P, DC), F32) as A8,
            sb("B8", (P, DC), F32) as B8,
            sb("LAM", (P, 1), F32) as LAM,
            sb("SUB2", (P, 1), F32) as SUB2,
            sb("YN", (P, TC, T), BF16) as YN,
            sb("sqb2", (P, 2, 512), BF16) as sqb2,

        ):
            def asl(i):
                return escr[:, 2 + i // 2, (i % 2) * 512:(i % 2) * 512 + 512]
            with sb("u_tm", (P, TC, D), BF16) as u_tm:
                ln_u_from_dram(src, u_tm)
                transpose_tm_fm(u_tm, xn_fm)

            def vbn(v):
                wait(v, "v")
                for dc in range(DC):
                    v.reduce_sum(BN[:, dc:dc + 1], xn_fm[:, dc, :], axis=AX)
                    inst = v.scalar_tensor_tensor(escr[:, 0, :512], xn_fm[:, dc, 0:512], 1.0,
                                                  xn_fm[:, dc, 0:512], ALU.mult, ALU.mult)
                    inc(inst, "v")
                    v.wait_ge(sems["v"], C["v"])
                for dc in range(DC):
                    inst = v.scalar_tensor_tensor(xou[:, 0, :], xn_fm[:, dc, 0:D], 1.0,
                                                  xn_fm[:, dc, 0:D], ALU.mult, ALU.mult,
                                                  accum_out=st3[:, 0:1])
                    inc(inst, "v")
                    v.wait_ge(sems["v"], C["v"])
                    inst = v.scalar_tensor_tensor(xou[:, 0, :], xn_fm[:, dc, D:2 * D], 1.0,
                                                  xn_fm[:, dc, D:2 * D], ALU.mult, ALU.mult,
                                                  accum_out=st3[:, 1:2])
                    inc(inst, "v")
                    v.wait_ge(sems["v"], C["v"])
                    inst = v.tensor_tensor(BN[:, 8 + dc:9 + dc], st3[:, 0:1], st3[:, 1:2], ALU.add)
                    inc(inst, "v")
                    v.wait_ge(sems["v"], C["v"])
            sec("vector", vbn)
            def bnout(sy):
                wait(sy, "v")
                inst = sy.dma_start(out=bnin_d[:, :], in_=BN[:, :])
                inc(inst, "qd")
            sec("sync", bnout)
            def gcc(gp):
                wait(gp, "qd")
                inst = gp.collective_compute("AllReduce", ALU.add,
                                             replica_groups=[list(range(8))],
                                             ins=[bnin_d.opt() if hasattr(bnin_d, "opt") else bnin_d],
                                             outs=[bnout_d.opt() if hasattr(bnout_d, "opt") else bnout_d])
                inc(inst, "cc")
                wait(gp, "cc")
            sec("gpsimd", gcc)
            def bnin(sy):
                wait(sy, "cc")
                inst = sy.dma_start(out=BN[:, :], in_=bnout_d[:, :])
                inc(inst, "qd")
            sec("sync", bnin)
            with sb("gb", (P, 2 * DC), F32) as gb:
                def ldgb(sy):
                    wait(sy, "qd")
                    i1 = sy.dma_start(out=gb[:, 0:DC], in_=bngd[:, :])
                    inc(i1, "qd")
                    i2 = sy.dma_start(out=gb[:, DC:2 * DC], in_=bnbd[:, :])
                    inc(i2, "qd")
                sec("sync", ldgb)
                def vbn2(v):
                    wait(v, "qd")
                    inst = v.tensor_scalar(BN[:, :], BN[:, :], 1.0 / (8 * T), None, ALU.mult)
                    inc(inst, "v")
                    wait(v, "v")
                    inst = v.scalar_tensor_tensor(st3[:, 0:DC], BN[:, 0:DC], 1.0, BN[:, 0:DC],
                                                  ALU.mult, ALU.mult)
                    inc(inst, "v")
                    wait(v, "v")
                    inst = v.tensor_tensor(BN[:, 8:16], BN[:, 8:16], st3[:, 0:DC], ALU.subtract)
                    inc(inst, "v")
                    wait(v, "v")
                    inst = v.tensor_scalar(BN[:, 8:16], BN[:, 8:16], LN_EPS, None, ALU.add)
                    inc(inst, "v")
                sec("vector", vbn2)
                def sbn(sc):
                    wait(sc, "v")
                    inst = sc.activation(BN[:, 8:16], BN[:, 8:16], AF.Ln, scale=1.0)
                    inc(inst, "s")
                    wait(sc, "s")
                    inst = sc.activation(BN[:, 8:16], BN[:, 8:16], AF.Exp, scale=-0.5)
                    inc(inst, "s")
                sec("scalar", sbn)
                def vbn3(v):
                    wait(v, "s")
                    inst = v.tensor_tensor(A8[:, :], gb[:, 0:DC], BN[:, 8:16], ALU.mult)
                    inc(inst, "v")
                    wait(v, "v")
                    inst = v.tensor_tensor(st3[:, 0:DC], BN[:, 0:DC], A8[:, :], ALU.mult)
                    inc(inst, "v")
                    wait(v, "v")
                    inst = v.tensor_tensor(B8[:, :], gb[:, DC:2 * DC], st3[:, 0:DC], ALU.subtract)
                    inc(inst, "v")
                    wait(v, "v")
                    for dc in range(DC):
                        for h in range(2):
                            inst = v.tensor_scalar(xn_fm[:, dc, h * D:(h + 1) * D],
                                                   xn_fm[:, dc, h * D:(h + 1) * D],
                                                   A8[:, dc:dc + 1], B8[:, dc:dc + 1],
                                                   ALU.mult, ALU.add)
                            inc(inst, "v")
                            v.wait_ge(sems["v"], C["v"])
                sec("vector", vbn3)

            def ldl(sy):
                wait_all(sy)
                i1 = sy.dma_start(out=rowA[0:1, 0:256], in_=lpackd[0:1, :])
                inc(i1, "qd")
                i2 = sy.dma_start(out=st4[:, 15:16], in_=sublnd[:, :])
                inc(i2, "qd")
            sec("sync", ldl)
            def vl(v):
                wait(v, "qd")
                inst = v.tensor_tensor(rowA[0:1, 0:64], rowA[0:1, 0:64], rowA[0:1, 64:128], ALU.mult)
                inc(inst, "v")
                wait(v, "v")
                inst = v.reduce_sum(rowB[0:1, 0:1], rowA[0:1, 0:64], axis=AX)
                inc(inst, "v")
                wait(v, "v")
                inst = v.tensor_tensor(rowA[0:1, 128:192], rowA[0:1, 128:192], rowA[0:1, 192:256], ALU.mult)
                inc(inst, "v")
                wait(v, "v")
                inst = v.reduce_sum(rowB[0:1, 1:2], rowA[0:1, 128:192], axis=AX)
                inc(inst, "v")
            sec("vector", vl)
            def sl(sc):
                wait(sc, "v")
                inst = sc.activation(rowB[0:1, 0:2], rowB[0:1, 0:2], AF.Exp)
                inc(inst, "s")
            sec("scalar", sl)
            def vl2(v):
                wait(v, "s")
                inst = v.tensor_tensor(rowB[0:1, 2:3], rowB[0:1, 0:1], rowB[0:1, 1:2], ALU.subtract)
                inc(inst, "v")
                wait(v, "v")
                inst = v.tensor_scalar(rowB[0:1, 2:3], rowB[0:1, 2:3], LAMBDA_INIT, None, ALU.add)
                inc(inst, "v")
                wait(v, "v"); wait(v, "qd")
                inst = v.tensor_scalar(SUB2[:, :], st4[:, 15:16], 1.0 - LAMBDA_INIT, None, ALU.mult)
                inc(inst, "v")
            sec("vector", vl2)
            bcast_row(rowB[0:1, 2:3].rearrange("a b -> a b"), LAM, 1)

            with (
                sb("VG", (P, TC, 256), BF16) as VG,
                sb("QK", (P, 4, T), BF16) as QK,
                sb("E", (P, 2, TC, 512), BF16) as E,
                sb("wq4", (P, DC, 512), BF16) as wq4,
            ):
                state = {"vz": 0, "tr2": 0, "qk": 0, "first": True, "t0": {}, "sex0": {},
                         "sexend": {}, "tailv": 0,
                         "pvt": [0, 0], "zmm": [0, 0], "zadd": [0, 0]}

                def emit_proj_v(g2):
                    load_w(wq4, vTd, 256, col_off=g2 * 256)
                    t0 = C["t"]; v0 = C["v"]
                    def tv(t):
                        wait_all(t)
                        j = 0
                        for tc in range(TC):
                            if j >= 4:
                                t.wait_ge(sems["v"], v0 + j - 3)
                            for kc in range(DC):
                                inst = t.matmul(ps[j % 4][:, 0:256], xn_fm[:, kc, tc * P:(tc + 1) * P],
                                                wq4[:, kc, 0:256], start=(kc == 0), stop=(kc == DC - 1))
                            inc(inst, "t")
                            j += 1
                    sec("tensor", tv)
                    def vv(v):
                        for tc in range(TC):
                            v.wait_ge(sems["t"], t0 + tc + 1)
                            inst = v.tensor_copy(VG[:, tc, :], ps[tc % 4][:, 0:256])
                            inc(inst, "v")
                    sec("vector", vv)
                    state["qk"] = C["v"]

                def emit_proj_qk(g2):
                    for i, wT in enumerate([q1T, q2T, k1T, k2T]):
                        def ldq(gp, wT=wT, i=i, g2=g2):
                            wait_all(gp)
                            inst = gp.dma_start(out=wq4[:, :, i * P:(i + 1) * P],
                                                in_=wT.rearrange("(a b) n -> b a n", b=P)[:, :, g2 * P:(g2 + 1) * P])
                            inc(inst, "gd")
                        sec("gpsimd", ldq)
                    t0 = C["t"]; v0 = C["v"]
                    def tq(t):
                        wait_all(t)
                        j = 0
                        for i in range(4):
                            for tcol in range(4):
                                if j >= 4:
                                    t.wait_ge(sems["v"], v0 + j - 3)
                                for kc in range(DC):
                                    inst = t.matmul(ps[j % 4][:, :], wq4[:, kc, i * P:(i + 1) * P],
                                                    xn_fm[:, kc, tcol * 512:(tcol + 1) * 512],
                                                    start=(kc == 0), stop=(kc == DC - 1))
                                inc(inst, "t")
                                j += 1
                    sec("tensor", tq)
                    def vq(v):
                        j = 0
                        for i in range(4):
                            for tcol in range(4):
                                v.wait_ge(sems["t"], t0 + j + 1)
                                inst = v.tensor_copy(QK[:, i, tcol * 512:(tcol + 1) * 512], ps[j % 4][:, :])
                                inc(inst, "v")
                                j += 1
                    sec("vector", vq)
                    state["qk"] = C["v"]

                def emit_tsc(u):
                    h, qq, st_ = u
                    hb = (h % 2) * 64
                    ql = slice(qq * 512, (qq + 1) * 512)
                    s0 = C["s"]
                    first = state["first"]; state["first"] = False
                    qkm = state["qk"]
                    t0 = C["t"]
                    state["t0"][u] = t0
                    def tsc(t):
                        if first:
                            wait_all(t)
                        else:
                            t.wait_ge(sems["v"], qkm)
                        for scn in range(TC):
                            if (not first) or scn >= 4:
                                t.wait_ge(sems["s"], s0 + scn - 3)
                            inst = t.matmul(ps[scn % 4][:, :],
                                            QK[hb:hb + 64, 2 + st_, scn * P:(scn + 1) * P],
                                            QK[hb:hb + 64, st_, ql],
                                            start=True, stop=True)
                            inc(inst, "t")
                    sec("tensor", tsc)

                def emit_sex(u):
                    h, qq, st_ = u
                    par = st_
                    t0 = state["t0"][u]
                    pvp = state["pvt"][par]
                    zap = state["zadd"][par]
                    state["sex0"][u] = C["s"]
                    def sex(sc):
                        if zap:
                            sc.wait_ge(sems["v"], zap)
                        for scn in range(TC):
                            sc.wait_ge(sems["t"], t0 + scn + 1)
                            if pvp:
                                sc.wait_ge(sems["t"], pvp + scn + 1)
                            inst = sc.activation(E[:, par, scn, :], ps[scn % 4][:, :], AF.Exp, scale=0.125)
                            inc(inst, "s")
                    sec("scalar", sex)
                    state["sexend"][u] = C["s"]

                def emit_zadd(u):
                    h, qq, st_ = u
                    par = st_
                    s0 = state["sex0"][u]
                    zmm = state["zmm"][par]
                    zsl = escr[:, 1, par * 512:(par + 1) * 512]
                    def zadd(v):
                        if zmm:
                            v.wait_ge(sems["t"], zmm)
                        v.wait_ge(sems["s"], s0 + 2)
                        inst = v.tensor_tensor(zsl, E[:, par, 0, :], E[:, par, 1, :], ALU.add)
                        inc(inst, "v")
                        for k in range(2, TC):
                            v.wait_ge(sems["s"], s0 + k + 1)
                            v.wait_ge(sems["v"], C["v"])
                            inst = v.tensor_tensor(zsl, zsl, E[:, par, k, :], ALU.add)
                            inc(inst, "v")
                    sec("vector", zadd)
                    state["zadd"][par] = C["v"]

                def emit_zpv(u):
                    h, qq, st_ = u
                    par = st_
                    hv = (h % 2) * P
                    sx0 = state["sex0"][u]
                    zam = state["zadd"][par]
                    pv = max(state["vz"], state["tailv"])
                    t1 = C["t"]
                    state["pvt"][par] = t1
                    zsl = escr[:, 1, par * 512:(par + 1) * 512]
                    def tz(t):
                        if pv:
                            t.wait_ge(sems["v"], pv)
                        for scn in range(TC):
                            t.wait_ge(sems["s"], sx0 + scn + 1)
                            inst = t.matmul(ps[5][:, :], VG[:, scn, hv:hv + P], E[:, par, scn, :],
                                            start=(scn == 0), stop=(scn == TC - 1))
                            inc(inst, "t")
                        t.wait_ge(sems["v"], zam)
                        inst = t.matmul(ps[4][0:1, :], g_onescolf[:, :], zsl, start=True, stop=True)
                        inc(inst, "t")
                    sec("tensor", tz)
                    state["zmm"][par] = C["t"]
                    parq = qq % 2
                    rowX = rowA if parq == 0 else rowB
                    def vz(v):
                        wait(v, "v")
                        v.wait_ge(sems["t"], t1 + TC)
                        inst = v.tensor_copy(asl(2 * parq + st_), ps[5][:, :])
                        inc(inst, "v")
                        v.wait_ge(sems["t"], t1 + TC + 1)
                        inst = v.tensor_copy(rowX[0:1, st_ * 512:(st_ + 1) * 512], ps[4][0:1, :])
                        inc(inst, "v")
                    sec("vector", vz)
                    state["vz"] = C["v"]

                def emit_tail(h, qq):
                    par = qq % 2
                    rowX = rowA if par == 0 else rowB
                    a0, a1 = 2 * par, 2 * par + 1
                    A2 = escr[:, 0, 0:512]; A3 = escr[:, 0, 512:1024]
                    def vr(v):
                        wait(v, "v")
                        inst = v.reciprocal(rowX[0:1, 0:1024], rowX[0:1, 0:1024])
                        inc(inst, "v")
                    sec("vector", vr)
                    t2 = C["t"]
                    def tb(t):
                        wait(t, "v")
                        inst = t.matmul(ps[4][:, :], g_ones1f[:, :], rowX[0:1, 0:512], start=True, stop=True)
                        inc(inst, "t")
                        inst = t.matmul(ps[5][:, :], g_ones1f[:, :], rowX[0:1, 512:1024], start=True, stop=True)
                        inc(inst, "t")
                    sec("tensor", tb)
                    def vc(v):
                        v.wait_ge(sems["t"], t2 + 2)
                        v.tensor_copy(A2, ps[4][:, :])
                        inst = v.tensor_copy(A3, ps[5][:, :])
                        inc(inst, "v")
                        wait(v, "v")
                        inst = v.tensor_tensor(asl(a0), asl(a0), A2, ALU.mult)
                        inc(inst, "v")
                        wait(v, "v")
                        inst = v.scalar_tensor_tensor(asl(a1), asl(a1), LAM[:, 0:1],
                                                      A3, ALU.mult, ALU.mult)
                        inc(inst, "v")
                        wait(v, "v")
                        inst = v.tensor_tensor(asl(a0), asl(a0), asl(a1), ALU.subtract)
                        inc(inst, "v")
                    sec("vector", vc)
                    trp = state["tr2"]
                    def sq1(sc):
                        wait(sc, "v")
                        if trp:
                            sc.wait_ge(sems["t"], trp)
                        inst = sc.activation(sqb2[:, par, :], asl(a0), AF.Square)
                        inc(inst, "s")
                    sec("scalar", sq1)
                    t3 = C["t"]
                    def tr2(t):
                        wait(t, "s")
                        inst = t.matmul(ps[4][:, :], g_ones128[:, :], sqb2[:, par, :], start=True, stop=True)
                        inc(inst, "t")
                    sec("tensor", tr2)
                    state["tr2"] = C["t"]
                    def vr2(v):
                        v.wait_ge(sems["t"], t3 + 1)
                        inst = v.tensor_scalar(A2, ps[4][:, :], 1.0 / P, 1e-5, ALU.mult, ALU.add)
                        inc(inst, "v")
                    sec("vector", vr2)
                    def sr2(sc):
                        wait(sc, "v")
                        inst = sc.activation(A2, A2, AF.Ln, scale=1.0)
                        inc(inst, "s")
                        wait(sc, "s")
                        inst = sc.activation(A2, A2, AF.Exp, scale=-0.5)
                        inc(inst, "s")
                    sec("scalar", sr2)
                    def vfin(v, h=h, qq=qq):
                        wait(v, "s")
                        inst = v.tensor_tensor(asl(a0), asl(a0), A2, ALU.mult)
                        inc(inst, "v")
                        wait(v, "v")
                        inst = v.tensor_scalar(YN[:, h, qq * 512:(qq + 1) * 512], asl(a0),
                                               SUB2[:, 0:1], None, ALU.mult)
                        inc(inst, "v")
                    sec("vector", vfin)
                    state["tailv"] = C["v"]

                for h in range(TC):
                    if h % 2 == 0:
                        emit_proj_v(h // 2)
                        emit_proj_qk(h // 2)
                    pend = None
                    tailq = []
                    for qq in range(4):
                        for st_ in range(2):
                            u = (h, qq, st_)
                            emit_tsc(u)
                            emit_sex(u)
                            if pend is not None:
                                emit_zpv(pend)
                                if pend[2] == 1:
                                    tailq.append((pend[0], pend[1]))
                                    if len(tailq) > 1:
                                        emit_tail(*tailq.pop(0))
                            emit_zadd(u)
                            pend = u
                    emit_zpv(pend)
                    tailq.append((pend[0], pend[1]))
                    for tt_ in tailq:
                        emit_tail(*tt_)

            with sb("cb", (P, TC, D), BF16) as cbuf:
                def ldc(gp):
                    wait_all(gp)
                    inst = gp.dma_start(out=cbuf[:, :, :], in_=cTd.rearrange("(a b) n -> b a n", b=P))
                    inc(inst, "gd")
                sec("gpsimd", ldc)
                proj_tm_epilogue(YN, cbuf, src, dst, extra_brep=None, kc_n=TC)
        return

    def block_l1_ffn(src, dst):
        with (
            sb("u_fm", (P, DC, T), BF16) as u_fm,
            sb("Q", (P, DC, T), BF16) as Q,
            sb("G2", (P, 4, T), BF16) as G2,
            sb("LP", (P, 2, T), F32) as LP,
            sb("LPb", (P, 2, T), BF16) as LPb,
            sb("selb", (P, 18, 256), BF16) as selb,
            sb("wb", (P, DC, D), BF16) as wb,
            sb("wbrep", (P, D), F32) as wbrep,
        ):
            with sb("u_tm", (P, TC, D), BF16) as u_tm:
                ln_u_from_dram(src, u_tm)
                transpose_tm_fm(u_tm, u_fm)
            load_w(wb, WqT2, D)
            proj_fm(u_fm, wb, Q, DC)
            with sb("wr", (P, DC, 256), BF16) as wr:
                def ldr(gp):
                    wait_all(gp)
                    inst = gp.dma_start(out=wr[:, :, :], in_=routersT.rearrange("(a b) n -> b a n", b=P))
                    inc(inst, "gd")
                sec("gpsimd", ldr)
                t0 = C["t"]; s0 = C["s"]
                def tg(t):
                    wait_all(t)
                    j = 0
                    for rc in range(2):
                        for tcol in range(4):
                            if j >= 4:
                                t.wait_ge(sems["s"], s0 + 2 * (j - 4) + 2)
                            for kc in range(DC):
                                inst = t.matmul(ps[j % 4][:, :], wr[:, kc, rc * P:(rc + 1) * P],
                                                Q[:, kc, tcol * 512:(tcol + 1) * 512],
                                                start=(kc == 0), stop=(kc == DC - 1))
                            inc(inst, "t")
                            j += 1
                sec("tensor", tg)
                def sg(sc):
                    j = 0
                    for rc in range(2):
                        for tcol in range(4):
                            sc.wait_ge(sems["t"], t0 + j + 1)
                            sl = slice(tcol * 512, (tcol + 1) * 512)
                            inst = sc.activation(G2[:, rc, sl], ps[j % 4][:, :], AF.Sigmoid)
                            inc(inst, "s")
                            inst = sc.activation(G2[:, rc + 2, sl], ps[j % 4][:, :], AF.Sigmoid, scale=-1.0)
                            inc(inst, "s")
                            j += 1
                sec("scalar", sg)
            def lds(sy):
                wait_all(sy)
                inst = sy.dma_start(out=selb[:, :, :], in_=selpd.rearrange("j b n -> b j n"))
                inc(inst, "qd")
            sec("sync", lds)
            for lv in range(8):
                pieces = SEL_BY_LEVEL[lv]
                t0 = C["t"]; v0 = C["v"]
                def tl(t, pieces=pieces, lv=lv):
                    if lv == 0:
                        wait_all(t)
                    wait(t, "v")
                    j = 0
                    for mc in range(2):
                        for tcol in range(4):
                            if j >= 4:
                                t.wait_ge(sems["v"], v0 + j - 3)
                            for kk, (jj, ch) in enumerate(pieces):
                                inst = t.matmul(ps[j % 4][:, :], selb[:, jj, mc * P:(mc + 1) * P],
                                                G2[:, ch, tcol * 512:(tcol + 1) * 512],
                                                start=(kk == 0), stop=(kk == len(pieces) - 1))
                            inc(inst, "t")
                            j += 1
                sec("tensor", tl)
                def vl_(v, lv=lv):
                    j = 0
                    for mc in range(2):
                        for tcol in range(4):
                            v.wait_ge(sems["t"], t0 + j + 1)
                            dstap = LP[:, mc, tcol * 512:(tcol + 1) * 512]
                            if lv == 0:
                                inst = v.tensor_copy(dstap, ps[j % 4][:, :])
                            else:
                                inst = v.tensor_tensor(dstap, dstap, ps[j % 4][:, :], ALU.mult)
                            inc(inst, "v")
                            j += 1
                sec("vector", vl_)
            def vcast(v):
                wait(v, "v")
                inst = v.tensor_copy(LPb[:, :, :], LP[:, :, :])
                inc(inst, "v")
            sec("vector", vcast)
            with sb("lvb", (P, 2, D), BF16) as lvb:
                def ldlv(gp):
                    wait_all(gp)
                    inst = gp.dma_start(out=lvb[:, :, :], in_=leafVd.rearrange("(a b) n -> b a n", b=P))
                    inc(inst, "gd")
                sec("gpsimd", ldlv)
                proj_fm(LPb, lvb, Q, DC, kc_n=2)
            load_w(wb, WoutT, D)
            load_row_bcast(woutbd[0:1, :], wbrep, D)
            proj_tm_epilogue(Q, wb, src, dst, extra_brep=wbrep, kc_n=DC)
        return

    srcs = [x_in.rearrange("(a b) d -> b a d", b=P), xs_d, xs_d, xs_d]
    dsts = [xs_d, xs_d, xs_d, out_d]
    dsts[n_blocks - 1] = out_d
    blocks = [block_l0_attn, block_moe, block_l1_attn, block_l1_ffn]
    for bi in range(n_blocks):
        blocks[bi](srcs[bi], dsts[bi])

    def fin(gp):
        wait_all(gp)
    sec("gpsimd", fin)

    stack.close()
    return nc


def _prep_inputs(inputs):
    c = _host_consts()
    f32 = np.float32
    def t(a):
        return np.ascontiguousarray(np.asarray(a, f32).T)
    shared = {
        "wqT": t(inputs["L0_attn_wq"]), "wkT": t(inputs["L0_attn_wk"]),
        "wvT": t(inputs["L0_attn_wv"]), "woT": t(inputs["L0_attn_wo"]),
        "bqkvo": np.stack([np.asarray(inputs["L0_attn_b" + k], f32) for k in "qkvo"]),
        "ff": np.asarray(inputs["L0_attn_freq_filter"], f32).reshape(1, D),
        "lng0": np.asarray(inputs["L0_attn_ln_gamma"], f32).reshape(1, D),
        "lnb0": np.asarray(inputs["L0_attn_ln_beta"], f32).reshape(1, D),
        "centT": t(inputs["L0_ffn_centroids"]),
        "lng8": np.asarray(inputs["L0_ffn_ln_g"], f32),
        "lnb8": np.asarray(inputs["L0_ffn_ln_b"], f32),
        "b1d": np.asarray(inputs["L0_ffn_b1"], f32),
        "b2d": np.asarray(inputs["L0_ffn_b2"], f32),
        "W1d": np.asarray(inputs["L0_ffn_W1"], f32),
        "W2d": np.asarray(inputs["L0_ffn_W2"], f32),
        "q1T": t(inputs["L1_attn_q1_w"]), "q2T": t(inputs["L1_attn_q2_w"]),
        "k1T": t(inputs["L1_attn_k1_w"]), "k2T": t(inputs["L1_attn_k2_w"]),
        "vT": t(inputs["L1_attn_v_w"]), "cT": t(inputs["L1_attn_c_w"]),
        "subln": np.asarray(inputs["L1_attn_subln_w"], f32).reshape(P, 1),
        "lpack": np.concatenate([np.asarray(inputs["L1_attn_" + k], f32).ravel()
                                 for k in ("lq1", "lk1", "lq2", "lk2")]).reshape(1, 256),
        "bng": np.ascontiguousarray(np.asarray(inputs["L1_attn_bn_gamma"], f32).reshape(DC, P).T),
        "bnb": np.ascontiguousarray(np.asarray(inputs["L1_attn_bn_beta"], f32).reshape(DC, P).T),
        "WqT2": t(inputs["L1_ffn_Wq"]),
        "routersT": np.ascontiguousarray(np.pad(t(inputs["L1_ffn_routers"]), ((0, 0), (0, 1)))),
        "leafV": np.asarray(inputs["L1_ffn_leaf_values"], f32),
        "WoutT": t(inputs["L1_ffn_Wout_w"]),
        "woutb": np.asarray(inputs["L1_ffn_Wout_b"], f32).reshape(1, D),
        "ident": c["ident"], "ones1": c["ones1"], "ones1f": c["ones1f"], "onescol": c["onescol"],
        "onescolf": c["onescolf"],
        "ones128": c["ones128"], "triu": c["triu"],
        "CF": c["CF"], "SF": c["SF"], "CI": c["CI"], "SI": c["SI"],
        "ecap": c["ecap"], "selp": c["selp"],
    }
    x = np.asarray(inputs["L0_attn_x"], f32)
    return shared, x


def kernel(**inputs):
    shared, x = _prep_inputs(inputs)
    nc = build_nc(N_BLOCKS)
    in_maps = [dict(shared, x=np.ascontiguousarray(x[i])) for i in range(8)]
    res = run_bass_kernel_spmd(nc, in_maps, core_ids=list(range(8)))
    outs = []
    for i in range(8):
        o = np.asarray(res.results[i]["out"])
        outs.append(np.ascontiguousarray(o.transpose(1, 0, 2).reshape(T, D)))
    return np.stack(outs, axis=0)
